# version 12 (final)
# baseline (speedup 1.0000x reference)
import numpy as np
import ml_dtypes
from contextlib import ExitStack
import concourse.bass as bass
import concourse.mybir as mybir
from concourse.bass_utils import run_bass_kernel_spmd

F32 = mybir.dt.float32
BF16 = mybir.dt.bfloat16
AF = mybir.ActivationFunctionType
ALU = mybir.AluOpType
AX = mybir.AxisListType
BF = ml_dtypes.bfloat16

B, T, D = 4, 4096, 1024
ROWS = B * T
NCORE = 8
RPC = ROWS // NCORE
NH0, NG0, DH = 16, 4, 64
EPS = 1e-5

SWAPMASK = [i ^ 1 for i in range(32)]



class Ser:

    def __init__(self):
        self.steps = []

    def mark(self):
        return len(self.steps)

    def s(self, engine, fn, inc=1, wait=None):
        self.steps.append((engine, fn, inc, wait))
        return len(self.steps)

    def dma(self, fn, wait=None):
        self.steps.append(("sync", fn, 16, wait))
        return len(self.steps)

    def dmag(self, fn, wait=None):
        self.steps.append(("gpsimd", fn, 16, wait))
        return len(self.steps)

    def tensor(self, fn, wait=None):
        return self.s("tensor", fn, wait=wait)

    def vector(self, fn, wait=None):
        return self.s("vector", fn, wait=wait)

    def scalar(self, fn, wait=None):
        return self.s("scalar", fn, wait=wait)

    def gpsimd(self, fn, wait=None):
        return self.s("gpsimd", fn, wait=wait)

    KEYS = ("t", "v", "s", "gc", "gd", "hd")

    @staticmethod
    def _sem_of(e, inc):
        if e == "tensor":
            return "t"
        if e == "vector":
            return "v"
        if e == "scalar":
            return "s"
        if e == "gpsimd":
            return "gd" if inc == 16 else "gc"
        return "hd"

    def emit_block(self, nc, sems, acc0, final=False):
        n = len(self.steps)
        cum = {k: [0] * (n + 1) for k in self.KEYS}
        for i, (e, fn, inc, wait) in enumerate(self.steps):
            for k in self.KEYS:
                cum[k][i + 1] = cum[k][i]
            cum[self._sem_of(e, inc)][i + 1] += inc
        with nc.Block() as block:
            def mk(engine_name):
                def f(eng):
                    waited = {k: -1 for k in self.KEYS}
                    for i, (e, fn, inc, wait) in enumerate(self.steps):
                        if e != engine_name:
                            continue
                        W = i if wait is None else wait
                        own = self._sem_of(e, inc)
                        for k in self.KEYS:
                            if k == "t" and own == "t":
                                continue
                            tgt = acc0[k] + cum[k][W]
                            if tgt > 0 and tgt > waited[k]:
                                eng.wait_ge(sems[k], tgt)
                                waited[k] = tgt
                        fn(eng).then_inc(sems[own], inc)
                    if final and engine_name == "sync":
                        for k in self.KEYS:
                            tgt = acc0[k] + cum[k][n]
                            if tgt > 0 and tgt > waited[k]:
                                eng.wait_ge(sems[k], tgt)
                return f
            block.sync(mk("sync"))
            block.gpsimd(mk("gpsimd"))
            block.vector(mk("vector"))
            block.scalar(mk("scalar"))
            block.tensor(mk("tensor"))
        for k in self.KEYS:
            acc0[k] += cum[k][n]
        self.steps = []

    def emit(self, nc):
        with ExitStack() as sctx:
            sems = {k: sctx.enter_context(nc.semaphore("ser_" + k)) for k in self.KEYS}
            acc = {k: 0 for k in self.KEYS}
            self.emit_block(nc, sems, acc, final=True)


def _run(nc, in_maps):
    res = run_bass_kernel_spmd(nc, in_maps, core_ids=list(range(NCORE)))
    return res.results



def _rope0_tables():
    p = np.arange(128)
    j = (p % 64) // 2
    freq = np.exp(-(2.0 * j / 64.0) * np.log(10000.0))
    t = np.tile(np.arange(T, dtype=np.float64), B)
    ang = freq[:, None] * t[None, :]
    cc = np.cos(ang)
    ss = np.sin(ang)
    ss[p % 2 == 0] *= -1.0
    return cc.astype(np.float32), ss.astype(np.float32)


def _rope1_table():
    import math
    dim_exp = 16
    freqs = 2.0 ** np.linspace(0.0, math.log2(5.0), dim_exp)
    coords = 1.0 + np.arange(T, dtype=np.float64)
    radians = coords[:, None] * freqs[None, :] * math.pi
    c = np.cos(radians)
    s = np.sin(radians)
    tbl = np.ones((128, T), np.float64)
    for pp in range(128):
        dd = pp % 64
        if dd < 32:
            sign = 1.0 if dd % 2 == 1 else -1.0
            tbl[pp] = s[:, dd % 16] + sign * c[:, dd % 16]
    return np.ascontiguousarray(np.tile(tbl, (1, B))).astype(np.float32)


def _diag_masks():
    d = np.arange(4)[:, None, None]
    pk = np.arange(128)[None, :, None]
    fq = np.arange(512)[None, None, :]
    return ((128 * d + pk) <= fq).astype(np.float32)


IDENT = np.eye(128, dtype=np.float32)
B2 = np.zeros((128, 2), np.float32)
B2[:64, 0] = 1.0
B2[64:, 1] = 1.0



def ln_tile(ser, sc, x_sb, out_sb, tiles, gain_col=None, bias_col=None):
    st = tiles["st"]
    mean, var, sd, rstd, m2 = (st[:, i:i + 1] for i in range(5))
    sq = tiles["sq"][:, :]
    x_sb = x_sb[:, :]
    out_sb = out_sb[:, :]
    ser.vector(lambda e, o=mean, i_=x_sb: e.reduce_sum(o, i_, AX.X))
    ser.vector(lambda e, o=mean: e.tensor_scalar_mul(o, o, 1.0 / D))
    ser.scalar(lambda e, o=sq, i_=x_sb, a=var: e.activation(o, i_, AF.Square, accum_out=a))
    ser.vector(lambda e, o=var: e.tensor_scalar_mul(o, o, 1.0 / D))
    ser.vector(lambda e, o=m2, i_=mean: e.tensor_tensor(o, i_, i_, ALU.mult))
    ser.vector(lambda e, o=var, i_=m2: e.tensor_tensor(o, o, i_, ALU.subtract))
    ser.vector(lambda e, o=var: e.tensor_scalar_add(o, o, EPS))
    ser.scalar(lambda e, o=sd, i_=var: e.activation(o, i_, AF.Sqrt))
    ser.vector(lambda e, o=rstd, i_=sd: e.reciprocal(o, i_))
    ser.vector(lambda e, o=out_sb, i_=x_sb, s1=mean, s2=rstd: e.tensor_scalar(
        out=o, in0=i_, scalar1=s1, scalar2=s2, op0=ALU.subtract, op1=ALU.mult))



def build_p1():
    nc = bass.Bass("TRN2", target_bir_lowering=False)
    xd = nc.dram_tensor("x_rows", [RPC, D], F32, kind="ExternalInput").ap()
    wq_d = nc.dram_tensor("Wq", [D, D], F32, kind="ExternalInput").ap()
    bq_d = nc.dram_tensor("bq", [1, D], F32, kind="ExternalInput").ap()
    wk_d = nc.dram_tensor("Wk", [D, 256], F32, kind="ExternalInput").ap()
    bk_d = nc.dram_tensor("bk", [1, 256], F32, kind="ExternalInput").ap()
    wv_d = nc.dram_tensor("Wv", [D, 256], F32, kind="ExternalInput").ap()
    bv_d = nc.dram_tensor("bv", [1, 256], F32, kind="ExternalInput").ap()
    cc_d = nc.dram_tensor("cc0", [128, RPC], F32, kind="ExternalInput").ap()
    ss_d = nc.dram_tensor("ss0", [128, RPC], F32, kind="ExternalInput").ap()
    id_d = nc.dram_tensor("ident", [128, 128], F32, kind="ExternalInput").ap()

    xn_o = nc.dram_tensor("xn", [RPC, D], F32, kind="ExternalOutput").ap()
    qf_o = nc.dram_tensor("q_f", [D, RPC], F32, kind="ExternalOutput").ap()
    kf_o = nc.dram_tensor("k_f", [256, RPC], F32, kind="ExternalOutput").ap()
    vt_o = nc.dram_tensor("v_tok", [RPC, 256], F32, kind="ExternalOutput").ap()

    ser = Ser()
    with ExitStack() as ctx:
        sb = lambda name, shape, dt=BF16: ctx.enter_context(nc.sbuf_tensor(name, shape, dt))
        ps = lambda name, shape, dt=F32: ctx.enter_context(nc.psum_tensor(name, shape, dt))

        wq = sb("wq_s", [128, 8, D])
        wk = sb("wk_s", [128, 8, 256])
        wv = sb("wv_s", [128, 8, 256])
        bq = sb("bq_s", [1, D])
        bk = sb("bk_s", [1, 256])
        bv = sb("bv_s", [1, 256])
        ones = sb("ones", [1, 512])
        cc = sb("cc_s", [128, RPC])
        ss = sb("ss_s", [128, RPC])
        ident = sb("ident_s", [128, 128])
        x_sb = sb("x_sb", [128, D], F32)
        sq = sb("sq", [128, D], F32)
        st = sb("st", [128, 8], F32)
        xn_sb = sb("xn_sb", [128, D])
        xn32 = sb("xn32", [128, D], F32)
        xnf = sb("xnf", [128, 8, RPC])
        qraw = sb("qraw", [128, 512])
        qshuf = sb("qshuf", [128, 512])
        t1 = sb("t1", [128, 512])
        t2 = sb("t2", [128, 512])
        stage = sb("stage", [128, RPC], F32)
        v_sb = sb("v_sb", [128, 256], F32)
        psA = ps("psA", [128, 512])
        psT = ps("psT", [128, 128], BF16)
        psV = ps("psV", [128, 256])

        ser.dmag(lambda e: e.dma_start(out=wq[:], in_=wq_d.rearrange("(c p) n -> p c n", p=128)))
        ser.dmag(lambda e: e.dma_start(out=wk[:], in_=wk_d.rearrange("(c p) n -> p c n", p=128)))
        ser.dmag(lambda e: e.dma_start(out=wv[:], in_=wv_d.rearrange("(c p) n -> p c n", p=128)))
        ser.dmag(lambda e: e.dma_start(out=bq[:], in_=bq_d[:]))
        ser.dmag(lambda e: e.dma_start(out=bk[:], in_=bk_d[:]))
        ser.dmag(lambda e: e.dma_start(out=bv[:], in_=bv_d[:]))
        ser.dmag(lambda e: e.dma_start(out=cc[:], in_=cc_d[:]))
        ser.dmag(lambda e: e.dma_start(out=ss[:], in_=ss_d[:]))
        ser.dmag(lambda e: e.dma_start(out=ident[:], in_=id_d[:]))
        ser.vector(lambda e: e.memset(ones[:], 1.0))

        tiles = {"sq": sq, "st": st}
        NT = RPC // 128
        for i in range(NT):
            r = slice(128 * i, 128 * i + 128)
            ser.dma(lambda e, r=r: e.dma_start(out=x_sb[:], in_=xd[r, :]))
            ln_tile(ser, None, x_sb, xn32, tiles)
            ser.dma(lambda e, r=r: e.dma_start(out=xn_o[r, :], in_=xn32[:]))
            ser.vector(lambda e: e.tensor_copy(xn_sb[:, :], xn32[:, :]))
            for c in range(8):
                ser.tensor(lambda e, c=c: e.transpose(psT[:], xn_sb[:, 128 * c:128 * c + 128], ident[:]))
                ser.vector(lambda e, c=c, i=i: e.tensor_copy(xnf[:, c, 128 * i:128 * i + 128], psT[:]))

        NB = RPC // 512
        for (wsb, bsb, nm, out_d) in ((wq, bq, 8, qf_o), (wk, bk, 2, kf_o)):
            for m in range(nm):
                msl = slice(128 * m, 128 * m + 128)
                for n in range(NB):
                    nsl = slice(512 * n, 512 * n + 512)
                    for c in range(8):
                        ser.tensor(lambda e, c=c, msl=msl, nsl=nsl, wsb=wsb: e.matmul(
                            psA[:], wsb[:, c, msl], xnf[:, c, nsl],
                            start=(c == 0), stop=False))
                    ser.tensor(lambda e, msl=msl, bsb=bsb: e.matmul(
                        psA[:], bsb[:, msl], ones[:], start=False, stop=True))
                    ser.vector(lambda e: e.tensor_copy(qraw[:], psA[:]))
                    ser.vector(lambda e: e.stream_shuffle(qshuf[:], qraw[:], SWAPMASK))
                    ser.vector(lambda e, nsl=nsl: e.tensor_tensor(t1[:], qraw[:], cc[:, nsl], ALU.mult))
                    ser.vector(lambda e, nsl=nsl: e.tensor_tensor(t2[:], qshuf[:], ss[:, nsl], ALU.mult))
                    ser.vector(lambda e, nsl=nsl: e.tensor_tensor(stage[:, nsl], t1[:], t2[:], ALU.add))
                ser.dma(lambda e, msl=msl, out_d=out_d: e.dma_start(out=out_d[msl, :], in_=stage[:]))

        for i in range(NT):
            r = slice(128 * i, 128 * i + 128)
            for c in range(8):
                ser.tensor(lambda e, c=c, i=i: e.matmul(
                    psV[:], xnf[:, c, 128 * i:128 * i + 128], wv[:, c, :],
                    start=(c == 0), stop=False))
            ser.tensor(lambda e: e.matmul(psV[:], ones[:, 0:128], bv[:], start=False, stop=True))
            ser.vector(lambda e: e.tensor_copy(v_sb[:], psV[:]))
            ser.dma(lambda e, r=r: e.dma_start(out=vt_o[r, :], in_=v_sb[:]))

        ser.emit(nc)
    return nc


def run_p1(x_flat, inp):
    cc0, ss0 = _rope0_tables()
    nc = build_p1()
    in_maps = []
    for c in range(NCORE):
        r = slice(RPC * c, RPC * c + RPC)
        in_maps.append({
            "x_rows": np.ascontiguousarray(x_flat[r]),
            "Wq": np.asarray(inp["L0_attn_Wq"], np.float32),
            "bq": np.asarray(inp["L0_attn_bq"], np.float32).reshape(1, D),
            "Wk": np.asarray(inp["L0_attn_Wk"], np.float32),
            "bk": np.asarray(inp["L0_attn_bk"], np.float32).reshape(1, 256),
            "Wv": np.asarray(inp["L0_attn_Wv"], np.float32),
            "bv": np.asarray(inp["L0_attn_bv"], np.float32).reshape(1, 256),
            "cc0": np.ascontiguousarray(cc0[:, r]),
            "ss0": np.ascontiguousarray(ss0[:, r]),
            "ident": IDENT,
        })
    res = _run(nc, in_maps)
    xn = np.concatenate([np.asarray(r_["xn"]) for r_ in res], axis=0)
    q_f = np.concatenate([np.asarray(r_["q_f"]) for r_ in res], axis=1)
    k_f = np.concatenate([np.asarray(r_["k_f"]) for r_ in res], axis=1)
    v_tok = np.concatenate([np.asarray(r_["v_tok"]) for r_ in res], axis=0)
    return xn, q_f, k_f, v_tok



def kernel_multi(**inputs):
    x = np.asarray(inputs["L0_attn_x"], np.float32).reshape(ROWS, D)
    xn, q_f, k_f, v_tok = run_p1(x, inputs)
    att0 = run_p2(q_f, k_f, v_tok)
    c_f = run_p3(xn, inputs)
    x1, xn1 = run_p4(x, xn, att0, c_f, inputs)
    nc_pre = build_subpre()
    nc_att = build_subatt()
    nc_post = build_subpost()
    q0, k0, v0 = run_subpre(nc_pre, xn1, inputs, 0)
    att_s0 = run_subatt(nc_att, q0, k0, v0)
    z2 = run_subpost(nc_post, xn1, att_s0, inputs, 0)
    q1, k1, v1 = run_subpre(nc_pre, z2, inputs, 1)
    att_s1 = run_subatt(nc_att, q1, k1, v1)
    z4 = run_subpost(nc_post, z2, att_s1, inputs, 1)
    out = run_p9(xn1, x1, z4, inputs)
    return out.reshape(B, T, D)



def build_p2():
    nc = bass.Bass("TRN2", target_bir_lowering=False)
    q_d = nc.dram_tensor("q_sl", [512, T], F32, kind="ExternalInput").ap()
    k_d = nc.dram_tensor("k_sl", [128, T], F32, kind="ExternalInput").ap()
    v_d = nc.dram_tensor("v_sl", [T, 128], F32, kind="ExternalInput").ap()
    m_d = nc.dram_tensor("masks", [128, 4, 512], F32, kind="ExternalInput").ap()
    att_o = nc.dram_tensor("att_sl", [512, T], F32, kind="ExternalOutput").ap()

    NKT = T // 128
    NQT = T // 512
    ser = Ser()
    with ExitStack() as ctx:
        sb = lambda name, shape, dt=BF16: ctx.enter_context(nc.sbuf_tensor(name, shape, dt))
        ps = lambda name, shape, dt=F32: ctx.enter_context(nc.psum_tensor(name, shape, dt))

        q_sb = sb("q_sb", [128, 4, T])
        k2 = sb("k2", [128, 2, T])
        v_sb = sb("v_sb", [128, NKT, 128])
        vaug = sb("vaug", [128, NKT, 130])
        mask = sb("mask", [128, 4, 512])
        sq = sb("sq_p2", [128, T])
        pa = sb("pa", [128, 512])
        pb = sb("pb", [128, 512])
        araw = sb("araw", [64, 512], F32)
        asb = sb("asb", [64, 512], F32)
        rec = sb("rec", [1, 512], F32)
        onesb = sb("onesb", [128, 1])
        onesf = sb("onesf", [1, 128], F32)
        hsc = sb("hsc", [1, 16], F32)
        ksc = sb("ksc", [1, 4], F32)
        bias_sb = sb("bias_sb", [128, 8], F32)
        psS_A = ps("psS_A", [128, 512])
        psS_B = ps("psS_B", [128, 512])
        psO_A = ps("psO_A", [128, 512])
        psO_B = ps("psO_B", [128, 512])
        psR = ps("psR", [128, 512])
        psQ = ps("psQ", [128, 512])

        ser.dmag(lambda e: e.dma_start(out=q_sb[:], in_=q_d.rearrange("(c p) t -> p c t", p=128)))
        for g in range(2):
            for half in range(2):
                ser.dmag(lambda e, g=g, half=half: e.dma_start(
                    out=k2[64 * half:64 * half + 64, g, :], in_=k_d[64 * g:64 * g + 64, :]))
        ser.dmag(lambda e: e.dma_start(out=v_sb[:], in_=v_d.rearrange("(kt p) d -> p kt d", p=128)))
        ser.dmag(lambda e: e.dma_start(out=mask[:], in_=m_d[:]))
        ser.vector(lambda e: e.memset(vaug[:], 1.0))
        for kt in range(NKT):
            ser.vector(lambda e, kt=kt: e.tensor_copy(vaug[:, kt, 0:64], v_sb[:, kt, 0:64]))
            ser.vector(lambda e, kt=kt: e.tensor_copy(vaug[:, kt, 65:129], v_sb[:, kt, 64:128]))
        ser.vector(lambda e: e.memset(onesb[:], 1.0))
        ser.vector(lambda e: e.memset(onesf[:], 1.0))
        ser.vector(lambda e: e.memset(hsc[:], 0.0))
        ser.vector(lambda e: e.memset(ksc[:], 0.0))

        for c in range(4):
            ser.scalar(lambda e, c=c: e.activation(sq[:], q_sb[:, c, :], AF.Square))
            for half in range(2):
                h = 2 * c + half
                pr = slice(64 * half, 64 * half + 64)
                for n in range(NQT):
                    nsl = slice(512 * n, 512 * n + 512)
                    ser.tensor(lambda e, pr=pr, nsl=nsl: e.matmul(
                        psQ[0:1, :], onesb[pr, :], sq[pr, nsl], start=True, stop=True,
                        tile_position=(pr.start, 0)))
                    ser.vector(lambda e: e.reduce_max(rec[0:1, 0:1], psQ[0:1, :], AX.X))
                    ser.vector(lambda e, h=h: e.tensor_tensor(
                        hsc[0:1, h:h + 1], hsc[0:1, h:h + 1], rec[0:1, 0:1], ALU.max))
        ser.scalar(lambda e: e.activation(sq[:], k2[:, 0, :], AF.Square))
        for g in range(2):
            pr = slice(64 * g, 64 * g + 64)
            for n in range(NQT):
                nsl = slice(512 * n, 512 * n + 512)
                ser.tensor(lambda e, pr=pr, nsl=nsl: e.matmul(
                    psQ[0:1, :], onesb[pr, :], sq[pr, nsl], start=True, stop=True,
                    tile_position=(pr.start, 0)))
                ser.vector(lambda e: e.reduce_max(rec[0:1, 0:1], psQ[0:1, :], AX.X))
                ser.vector(lambda e, g=g: e.tensor_tensor(
                    ksc[0:1, g:g + 1], ksc[0:1, g:g + 1], rec[0:1, 0:1], ALU.max))
        ser.scalar(lambda e: e.activation(hsc[0:1, 8:16], hsc[0:1, 0:8], AF.Sqrt))
        ser.scalar(lambda e: e.activation(ksc[0:1, 2:4], ksc[0:1, 0:2], AF.Sqrt))
        for h in range(8):
            g = h // 4
            ser.vector(lambda e, h=h, g=g: e.tensor_tensor(
                hsc[0:1, h:h + 1], hsc[0:1, 8 + h:9 + h], ksc[0:1, 2 + g:3 + g], ALU.mult))
            ser.vector(lambda e, h=h: e.tensor_scalar_mul(hsc[0:1, h:h + 1], hsc[0:1, h:h + 1], -0.125))
            ser.tensor(lambda e, h=h: e.matmul(
                psR[:, 0:1], onesf[:, :], hsc[0:1, h:h + 1], start=True, stop=True))
            ser.vector(lambda e, h=h: e.tensor_copy(bias_sb[:, h:h + 1], psR[:, 0:1]))

        for p in range(4):
            g = p // 2
            for j in range(NQT):
                jsl = slice(512 * j, 512 * j + 512)
                nkt = 4 * j + 4
                for i in range(nkt):
                    isl = slice(128 * i, 128 * i + 128)
                    last = i == nkt - 1
                    ser.tensor(lambda e, g=g, isl=isl, p=p, jsl=jsl: e.matmul(
                        psS_A[:], k2[0:64, g, isl], q_sb[0:64, p, jsl],
                        start=True, stop=True, tile_position=(0, 0)))
                    ser.tensor(lambda e, g=g, isl=isl, p=p, jsl=jsl: e.matmul(
                        psS_B[:], k2[64:128, g, isl], q_sb[64:128, p, jsl],
                        start=True, stop=True, tile_position=(64, 0)))
                    ser.scalar(lambda e, p=p: e.activation(
                        pa[:], psS_A[:], AF.Exp, bias=bias_sb[:, 2 * p:2 * p + 1], scale=0.125))
                    ser.scalar(lambda e, p=p: e.activation(
                        pb[:], psS_B[:], AF.Exp, bias=bias_sb[:, 2 * p + 1:2 * p + 2], scale=0.125))
                    d = i - 4 * j
                    if d >= 0:
                        ser.vector(lambda e, d=d: e.tensor_tensor(pa[:], pa[:], mask[:, d, :], ALU.mult))
                        ser.vector(lambda e, d=d: e.tensor_tensor(pb[:], pb[:], mask[:, d, :], ALU.mult))
                    ser.tensor(lambda e, i=i, last=last, g=g: e.matmul(
                        psO_A[0:65, :], vaug[:, i, 65 * g:65 * g + 65], pa[:], start=(i == 0), stop=last))
                    ser.tensor(lambda e, i=i, last=last, g=g: e.matmul(
                        psO_B[0:65, :], vaug[:, i, 65 * g:65 * g + 65], pb[:], start=(i == 0), stop=last))
                for half, psO in ((0, psO_A), (1, psO_B)):
                    h = 2 * p + half
                    ser.vector(lambda e, psO=psO: e.reciprocal(rec[:], psO[64:65, :]))
                    ser.tensor(lambda e: e.matmul(psR[0:64, :], onesf[:, 0:64], rec[:], start=True, stop=True))
                    ser.vector(lambda e, psO=psO: e.tensor_copy(araw[:], psO[0:64, :]))
                    ser.vector(lambda e: e.tensor_tensor(asb[:], araw[:], psR[0:64, :], ALU.mult))
                    ser.dma(lambda e, h=h, jsl=jsl: e.dma_start(
                        out=att_o[64 * h:64 * h + 64, jsl], in_=asb[:]))
        ser.emit(nc)
    return nc


def run_p2(q_f, k_f, v_tok):
    masks = np.ascontiguousarray(_diag_masks().transpose(1, 0, 2))
    nc = build_p2()
    in_maps = []
    for c in range(NCORE):
        b, hg = c // 2, c % 2
        in_maps.append({
            "q_sl": np.ascontiguousarray(q_f[512 * hg:512 * hg + 512, T * b:T * b + T]),
            "k_sl": np.ascontiguousarray(k_f[128 * hg:128 * hg + 128, T * b:T * b + T]),
            "v_sl": np.ascontiguousarray(v_tok[T * b:T * b + T, 128 * hg:128 * hg + 128]),
            "masks": masks,
        })
    res = _run(nc, in_maps)
    att = np.zeros((D, ROWS), np.float32)
    for c in range(NCORE):
        b, hg = c // 2, c % 2
        att[512 * hg:512 * hg + 512, T * b:T * b + T] = np.asarray(res[c]["att_sl"])
    return att



def emit_tr(ser, items, psTs, ident, w0=None):
    lag = 2 if len(psTs) >= 2 else 1
    done = {}
    for idx, (s_ap, d_ap) in enumerate(items):
        pt = psTs[idx % len(psTs)]
        w = done.get(idx - lag)
        if w0 is not None:
            w = w0 if w is None else max(w, w0)
        ser.tensor(lambda e, s_ap=s_ap, pt=pt: e.transpose(pt[:], s_ap, ident[:]), wait=w)
        mk = ser.mark()
        ser.vector(lambda e, d_ap=d_ap, pt=pt: e.tensor_copy(d_ap, pt[:]), wait=mk)
        done[idx] = ser.mark()


def emit_load_wT(ser, nc, w_d, stage, wT, ident, psT, nchunk_out=8, nchunk_in=8):
    ser.dmag(lambda e: e.dma_start(out=stage[:, 0:nchunk_out, :], in_=w_d.rearrange("(c p) n -> p c n", p=128)))
    psTs = psT if isinstance(psT, list) else [psT]
    w0 = ser.mark()
    items = [(stage[:, co, 128 * ci:128 * ci + 128], wT[:, ci, 128 * co:128 * co + 128])
             for co in range(nchunk_out) for ci in range(nchunk_in)]
    emit_tr(ser, items, psTs, ident, w0=w0)


def build_p3():
    nc = bass.Bass("TRN2", target_bir_lowering=False)
    xn_d = nc.dram_tensor("xn_sl", [4 * 512, D], F32, kind="ExternalInput").ap()
    w_ds = {}
    for nm in ("wf", "uf", "wi", "ui", "wc", "uc"):
        w_ds[nm] = nc.dram_tensor(nm, [D, D], F32, kind="ExternalInput").ap()
    b_ds = {}
    for nm in ("bf", "bi", "bc"):
        b_ds[nm] = nc.dram_tensor(nm, [1, D], F32, kind="ExternalInput").ap()
    inc_d = nc.dram_tensor("in_c", [128, 8], F32, kind="ExternalInput").ap()
    id_d = nc.dram_tensor("ident", [128, 128], F32, kind="ExternalInput").ap()
    c_o = nc.dram_tensor("c_final", [D, 512], F32, kind="ExternalOutput").ap()

    ser = Ser()
    with ExitStack() as ctx:
        sb = lambda name, shape, dt=BF16: ctx.enter_context(nc.sbuf_tensor(name, shape, dt))
        ps = lambda name, shape, dt=F32: ctx.enter_context(nc.psum_tensor(name, shape, dt))

        ident = sb("ident_s", [128, 128])
        stage = sb("stage_w", [128, 8, D])
        wT = {nm: sb(nm + "_T", [128, 8, D]) for nm in w_ds}
        brow = {nm: sb(nm + "_r", [1, D]) for nm in b_ds}
        inc_sb = sb("inc_sb", [128, 8], F32)
        ones = sb("ones_p3", [1, 512])
        xf = sb("xf", [128, 8, 512])
        xld = sb("xld", [128, D])
        cf32 = sb("cf32", [128, 8, 512], F32)
        cb16 = sb("cb16", [128, 8, 512])
        cnew = sb("cnew", [128, 8, 512], F32)
        gsb = sb("gsb", [128, 8, 512], F32)
        gch = sb("gch", [128, 512], F32)
        tmp = sb("tmp_p3", [128, 512], F32)
        psA = ps("psA_p3", [128, 512])
        psT = ps("psT_p3", [128, 128], BF16)

        ser.dmag(lambda e: e.dma_start(out=ident[:], in_=id_d[:]))
        for nm in w_ds:
            emit_load_wT(ser, nc, w_ds[nm], stage, wT[nm], ident, psT)
        for nm in b_ds:
            ser.dmag(lambda e, nm=nm: e.dma_start(out=brow[nm][:], in_=b_ds[nm][:]))
        ser.dma(lambda e: e.dma_start(out=inc_sb[:], in_=inc_d[:]))
        ser.vector(lambda e: e.memset(ones[:], 1.0))

        for m in range(8):
            ser.vector(lambda e, m=m: e.memset(cf32[:, m, :], 0.0))
            ser.vector(lambda e, m=m: e.tensor_scalar_add(cf32[:, m, :], cf32[:, m, :], inc_sb[:, m:m + 1]))
            ser.vector(lambda e, m=m: e.tensor_copy(cb16[:, m, :], cf32[:, m, :]))

        def gate_psum(wname, uname, bname, m):
            for ci in range(8):
                ser.tensor(lambda e, ci=ci, m=m, wname=wname: e.matmul(
                    psA[:], wT[wname][:, ci, 128 * m:128 * m + 128], xf[:, ci, :],
                    start=(ci == 0), stop=False))
            for ci in range(8):
                ser.tensor(lambda e, ci=ci, m=m, uname=uname: e.matmul(
                    psA[:], wT[uname][:, ci, 128 * m:128 * m + 128], cb16[:, ci, :],
                    start=False, stop=False))
            ser.tensor(lambda e, m=m, bname=bname: e.matmul(
                psA[:], brow[bname][:, 128 * m:128 * m + 128], ones[:], start=False, stop=True))

        for t in range(4):
            for rt in range(4):
                r = slice(512 * t + 128 * rt, 512 * t + 128 * rt + 128)
                ser.dmag(lambda e, r=r: e.dma_start(out=xld[:], in_=xn_d[r, :]))
                for c in range(8):
                    ser.tensor(lambda e, c=c: e.transpose(
                        psT[:], xld[:, 128 * c:128 * c + 128], ident[:]))
                    ser.vector(lambda e, c=c, rt=rt: e.tensor_copy(
                        xf[:, c, 128 * rt:128 * rt + 128], psT[:]))
            for m in range(8):
                gate_psum("wc", "uc", "bc", m)
                ser.scalar(lambda e, m=m: e.activation(gsb[:, m, :], psA[:], AF.Tanh))
            for m in range(8):
                gate_psum("wf", "uf", "bf", m)
                ser.scalar(lambda e, m=m: e.activation(gch[:], psA[:], AF.Sigmoid))
                ser.vector(lambda e, m=m: e.tensor_tensor(
                    cnew[:, m, :], gch[:], cf32[:, m, :], ALU.mult))
            for m in range(8):
                gate_psum("wi", "ui", "bi", m)
                ser.scalar(lambda e, m=m: e.activation(gch[:], psA[:], AF.Sigmoid))
                ser.vector(lambda e, m=m: e.tensor_tensor(tmp[:], gch[:], gsb[:, m, :], ALU.mult))
                ser.vector(lambda e, m=m: e.tensor_tensor(
                    cf32[:, m, :], cnew[:, m, :], tmp[:], ALU.add))
            for m in range(8):
                ser.vector(lambda e, m=m: e.tensor_copy(cb16[:, m, :], cf32[:, m, :]))

        for m in range(8):
            ser.dma(lambda e, m=m: e.dma_start(out=c_o[128 * m:128 * m + 128, :], in_=cf32[:, m, :]))
        ser.emit(nc)
    return nc


def run_p3(xn, inp):
    nc = build_p3()
    in_maps = []
    for c in range(NCORE):
        s0 = 512 * c
        xn_sl = np.concatenate([xn[t * T + s0:t * T + s0 + 512] for t in range(B)], axis=0)
        m = {"xn_sl": np.ascontiguousarray(xn_sl), "ident": IDENT,
             "in_c": np.ascontiguousarray(np.asarray(inp["L0_ffn_in_c"], np.float32).reshape(8, 128).T)}
        for nm in ("wf", "uf", "wi", "ui", "wc", "uc"):
            m[nm] = np.asarray(inp["L0_ffn_" + nm], np.float32)
        for nm in ("bf", "bi", "bc"):
            m[nm] = np.asarray(inp["L0_ffn_" + nm], np.float32).reshape(1, D)
        in_maps.append(m)
    res = _run(nc, in_maps)
    return np.concatenate([np.asarray(r_["c_final"]) for r_ in res], axis=1)



def build_p4():
    nc = bass.Bass("TRN2", target_bir_lowering=False)
    x_d = nc.dram_tensor("x_rows", [RPC, D], F32, kind="ExternalInput").ap()
    xn_d = nc.dram_tensor("xn_rows", [RPC, D], F32, kind="ExternalInput").ap()
    att_d = nc.dram_tensor("att_sl", [D, RPC], F32, kind="ExternalInput").ap()
    cf_d = nc.dram_tensor("cf_sl", [D, RPC], F32, kind="ExternalInput").ap()
    Wo_d = nc.dram_tensor("Wo", [D, D], F32, kind="ExternalInput").ap()
    bo_d = nc.dram_tensor("bo", [1, D], F32, kind="ExternalInput").ap()
    wof_d = nc.dram_tensor("wof", [D, D], F32, kind="ExternalInput").ap()
    uof_d = nc.dram_tensor("uof", [D, D], F32, kind="ExternalInput").ap()
    bof_d = nc.dram_tensor("bof", [1, D], F32, kind="ExternalInput").ap()
    id_d = nc.dram_tensor("ident", [128, 128], F32, kind="ExternalInput").ap()
    x1_o = nc.dram_tensor("x1", [RPC, D], F32, kind="ExternalOutput").ap()
    xn1_o = nc.dram_tensor("xn1", [RPC, D], F32, kind="ExternalOutput").ap()

    ser = Ser()
    with ExitStack() as ctx:
        sb = lambda name, shape, dt=BF16: ctx.enter_context(nc.sbuf_tensor(name, shape, dt))
        ps = lambda name, shape, dt=F32: ctx.enter_context(nc.psum_tensor(name, shape, dt))
        ident = sb("ident_s", [128, 128])
        stage = sb("stage_w", [128, 8, D])
        attf = sb("attf", [128, 8, RPC])
        cfb = sb("cfb", [128, 8, RPC])
        wo_sb = sb("wo_sb", [128, 8, D])
        woT = sb("woT", [128, 8, D])
        uoT = sb("uoT", [128, 8, D])
        bo_r = sb("bo_r", [1, D])
        bof_r = sb("bof_r", [1, D])
        ones = sb("ones_p4", [1, 512])
        x_sb = sb("x_sb", [128, D], F32)
        xn_sb = sb("xn_sb", [128, D])
        xnf = sb("xnf_t", [128, 8, 128])
        h_sb = sb("h_sb", [128, D], F32)
        x1_sb = sb("x1_sb", [128, D], F32)
        xn1_sb = sb("xn1_sb", [128, D], F32)
        sq = sb("sq", [128, D], F32)
        st = sb("st", [128, 8], F32)
        psO = ps("psO_p4", [128, 512])
        psH = ps("psH_p4", [128, 512])
        psT = ps("psT_p4", [128, 128], BF16)

        ser.dmag(lambda e: e.dma_start(out=ident[:], in_=id_d[:]))
        ser.dmag(lambda e: e.dma_start(out=attf[:], in_=att_d.rearrange("(c p) t -> p c t", p=128)))
        ser.dmag(lambda e: e.dma_start(out=cfb[:], in_=cf_d.rearrange("(c p) t -> p c t", p=128)))
        ser.dmag(lambda e: e.dma_start(out=wo_sb[:], in_=Wo_d.rearrange("(c p) n -> p c n", p=128)))
        ser.dmag(lambda e: e.dma_start(out=bo_r[:], in_=bo_d[:]))
        ser.dmag(lambda e: e.dma_start(out=bof_r[:], in_=bof_d[:]))
        emit_load_wT(ser, nc, wof_d, stage, woT, ident, psT)
        emit_load_wT(ser, nc, uof_d, stage, uoT, ident, psT)
        ser.vector(lambda e: e.memset(ones[:], 1.0))

        tiles = {"sq": sq, "st": st}
        NT = RPC // 128
        for i in range(NT):
            r = slice(128 * i, 128 * i + 128)
            tsl = slice(128 * i, 128 * i + 128)
            ser.dma(lambda e, r=r: e.dma_start(out=x_sb[:], in_=x_d[r, :]))
            ser.dmag(lambda e, r=r: e.dma_start(out=xn_sb[:], in_=xn_d[r, :]))
            for c in range(8):
                ser.tensor(lambda e, c=c: e.transpose(psT[:], xn_sb[:, 128 * c:128 * c + 128], ident[:]))
                ser.vector(lambda e, c=c: e.tensor_copy(xnf[:, c, :], psT[:]))
            for nb in range(2):
                nsl = slice(512 * nb, 512 * nb + 512)
                for c in range(8):
                    ser.tensor(lambda e, c=c, tsl=tsl, nsl=nsl: e.matmul(
                        psO[:], attf[:, c, tsl], wo_sb[:, c, nsl], start=(c == 0), stop=False))
                ser.tensor(lambda e, nsl=nsl: e.matmul(
                    psO[:], ones[:, 0:128], bo_r[:, nsl], start=False, stop=True))
                for c in range(8):
                    ser.tensor(lambda e, c=c, nsl=nsl: e.matmul(
                        psH[:], xnf[:, c, :], woT[:, c, nsl], start=(c == 0), stop=False))
                for c in range(8):
                    ser.tensor(lambda e, c=c, tsl=tsl, nsl=nsl: e.matmul(
                        psH[:], cfb[:, c, tsl], uoT[:, c, nsl], start=False, stop=False))
                ser.tensor(lambda e, nsl=nsl: e.matmul(
                    psH[:], ones[:, 0:128], bof_r[:, nsl], start=False, stop=True))
                ser.scalar(lambda e, nsl=nsl: e.activation(h_sb[:, nsl], psH[:], AF.Tanh))
                ser.vector(lambda e, nsl=nsl: e.tensor_tensor(
                    x1_sb[:, nsl], x_sb[:, nsl], h_sb[:, nsl], ALU.add))
                ser.vector(lambda e, nsl=nsl: e.tensor_tensor(
                    x1_sb[:, nsl], x1_sb[:, nsl], psO[:], ALU.add))
            ser.dma(lambda e, r=r: e.dma_start(out=x1_o[r, :], in_=x1_sb[:]))
            ln_tile(ser, None, x1_sb, xn1_sb, tiles)
            ser.dma(lambda e, r=r: e.dma_start(out=xn1_o[r, :], in_=xn1_sb[:]))
        ser.emit(nc)
    return nc


def run_p4(x_flat, xn, att_f, c_f, inp):
    nc = build_p4()
    in_maps = []
    for c in range(NCORE):
        r = slice(RPC * c, RPC * c + RPC)
        b, half = c // 2, c % 2
        tcols = slice(2048 * half, 2048 * half + 2048)
        in_maps.append({
            "x_rows": np.ascontiguousarray(x_flat[r]),
            "xn_rows": np.ascontiguousarray(xn[r]),
            "att_sl": np.ascontiguousarray(att_f[:, r]),
            "cf_sl": np.ascontiguousarray(c_f[:, tcols]),
            "Wo": np.asarray(inp["L0_attn_Wo"], np.float32),
            "bo": np.asarray(inp["L0_attn_bo"], np.float32).reshape(1, D),
            "wof": np.asarray(inp["L0_ffn_wo"], np.float32),
            "uof": np.asarray(inp["L0_ffn_uo"], np.float32),
            "bof": np.asarray(inp["L0_ffn_bo"], np.float32).reshape(1, D),
            "ident": IDENT,
        })
    res = _run(nc, in_maps)
    x1 = np.concatenate([np.asarray(r_["x1"]) for r_ in res], axis=0)
    xn1 = np.concatenate([np.asarray(r_["xn1"]) for r_ in res], axis=0)
    return x1, xn1



def build_subpre():
    nc = bass.Bass("TRN2", target_bir_lowering=False)
    z_d = nc.dram_tensor("z_rows", [RPC, D], F32, kind="ExternalInput").ap()
    g_d = nc.dram_tensor("g_row", [1, D], F32, kind="ExternalInput").ap()
    b_d = nc.dram_tensor("b_row", [1, D], F32, kind="ExternalInput").ap()
    wq_d = nc.dram_tensor("wq", [D, D], F32, kind="ExternalInput").ap()
    wk_d = nc.dram_tensor("wk", [D, D], F32, kind="ExternalInput").ap()
    wv_d = nc.dram_tensor("wv", [D, D], F32, kind="ExternalInput").ap()
    bq_d = nc.dram_tensor("bq", [1, D], F32, kind="ExternalInput").ap()
    bk_d = nc.dram_tensor("bk", [1, D], F32, kind="ExternalInput").ap()
    bv_d = nc.dram_tensor("bv", [1, D], F32, kind="ExternalInput").ap()
    tb_d = nc.dram_tensor("tbl", [128, RPC], F32, kind="ExternalInput").ap()
    id_d = nc.dram_tensor("ident", [128, 128], F32, kind="ExternalInput").ap()
    qf_o = nc.dram_tensor("q_f", [D, RPC], F32, kind="ExternalOutput").ap()
    kf_o = nc.dram_tensor("k_f", [D, RPC], F32, kind="ExternalOutput").ap()
    vt_o = nc.dram_tensor("v_tok", [RPC, D], F32, kind="ExternalOutput").ap()

    ser = Ser()
    with ExitStack() as ctx:
        sb = lambda name, shape, dt=BF16: ctx.enter_context(nc.sbuf_tensor(name, shape, dt))
        ps = lambda name, shape, dt=F32: ctx.enter_context(nc.psum_tensor(name, shape, dt))
        ident = sb("ident_s", [128, 128])
        wq = sb("wq_s", [128, 8, D])
        wk = sb("wk_s", [128, 8, D])
        wv = sb("wv_s", [128, 8, D])
        bq = sb("bq_s", [1, D])
        bk = sb("bk_s", [1, D])
        bv = sb("bv_s", [1, D])
        tbl = sb("tbl_s", [128, RPC])
        gb = sb("gb", [128, D], F32)
        bb = sb("bb", [128, D], F32)
        onesf = sb("onesf", [1, 128], F32)
        ones = sb("ones_sp", [1, 512])
        z_sb = sb("z_sb", [128, D], F32)
        zl1 = sb("zl1", [128, D], F32)
        zl_sb = sb("zl_sb", [128, D])
        zlf = sb("zlf", [128, 8, RPC])
        sq = sb("sq", [128, D], F32)
        st = sb("st", [128, 8], F32)
        qraw = sb("qraw", [128, 512])
        stage = sb("stage_q", [128, RPC], F32)
        v_sb = sb("v_sb", [128, D], F32)
        psA = ps("psA_sp", [128, 512])
        psT = ps("psT_sp", [128, 128], BF16)

        ser.dmag(lambda e: e.dma_start(out=ident[:], in_=id_d[:]))
        for w, wd in ((wq, wq_d), (wk, wk_d), (wv, wv_d)):
            ser.dmag(lambda e, w=w, wd=wd: e.dma_start(out=w[:], in_=wd.rearrange("(c p) n -> p c n", p=128)))
        for b_, bd in ((bq, bq_d), (bk, bk_d), (bv, bv_d)):
            ser.dmag(lambda e, b_=b_, bd=bd: e.dma_start(out=b_[:], in_=bd[:]))
        ser.dmag(lambda e: e.dma_start(out=tbl[:], in_=tb_d[:]))
        ser.vector(lambda e: e.memset(onesf[:], 1.0))
        ser.vector(lambda e: e.memset(ones[:], 1.0))
        gbounce = sb("gbounce", [1, D], F32)
        for row_d, dst in ((g_d, gb), (b_d, bb)):
            ser.dma(lambda e, row_d=row_d: e.dma_start(out=gbounce[:], in_=row_d[:]))
            for nb in range(2):
                nsl = slice(512 * nb, 512 * nb + 512)
                ser.tensor(lambda e, nsl=nsl: e.matmul(
                    psA[:], onesf[:, :], gbounce[:, nsl], start=True, stop=True))
                ser.vector(lambda e, dst=dst, nsl=nsl: e.tensor_copy(dst[:, nsl], psA[:]))

        tiles = {"sq": sq, "st": st}
        NT = RPC // 128
        for i in range(NT):
            r = slice(128 * i, 128 * i + 128)
            ser.dma(lambda e, r=r: e.dma_start(out=z_sb[:], in_=z_d[r, :]))
            ln_tile(ser, None, z_sb, zl1, tiles)
            ser.vector(lambda e: e.tensor_tensor(zl1[:], zl1[:], gb[:], ALU.mult))
            ser.vector(lambda e: e.tensor_tensor(zl_sb[:], zl1[:], bb[:], ALU.add))
            for c in range(8):
                ser.tensor(lambda e, c=c: e.transpose(psT[:], zl_sb[:, 128 * c:128 * c + 128], ident[:]))
                ser.vector(lambda e, c=c, i=i: e.tensor_copy(zlf[:, c, 128 * i:128 * i + 128], psT[:]))

        NB = RPC // 512
        for (wsb, bsb, out_d, rope) in ((wq, bq, qf_o, True), (wk, bk, kf_o, True)):
            for m in range(8):
                msl = slice(128 * m, 128 * m + 128)
                for n in range(NB):
                    nsl = slice(512 * n, 512 * n + 512)
                    for c in range(8):
                        ser.tensor(lambda e, c=c, msl=msl, nsl=nsl, wsb=wsb: e.matmul(
                            psA[:], wsb[:, c, msl], zlf[:, c, nsl], start=(c == 0), stop=False))
                    ser.tensor(lambda e, msl=msl, bsb=bsb: e.matmul(
                        psA[:], bsb[:, msl], ones[:], start=False, stop=True))
                    ser.vector(lambda e: e.tensor_copy(qraw[:], psA[:]))
                    ser.vector(lambda e, nsl=nsl: e.tensor_tensor(
                        stage[:, nsl], qraw[:], tbl[:, nsl], ALU.mult))
                ser.dma(lambda e, msl=msl, out_d=out_d: e.dma_start(out=out_d[msl, :], in_=stage[:]))
        for i in range(NT):
            r = slice(128 * i, 128 * i + 128)
            for nb in range(2):
                nsl = slice(512 * nb, 512 * nb + 512)
                for c in range(8):
                    ser.tensor(lambda e, c=c, i=i, nsl=nsl: e.matmul(
                        psA[:], zlf[:, c, 128 * i:128 * i + 128], wv[:, c, nsl],
                        start=(c == 0), stop=False))
                ser.tensor(lambda e, nsl=nsl: e.matmul(
                    psA[:], ones[:, 0:128], bv[:, nsl], start=False, stop=True))
                ser.vector(lambda e, nsl=nsl: e.tensor_copy(v_sb[:, nsl], psA[:]))
            ser.dma(lambda e, r=r: e.dma_start(out=vt_o[r, :], in_=v_sb[:]))
        ser.emit(nc)
    return nc


def run_subpre(nc, z, inp, li):
    tbl = _rope1_table()
    in_maps = []
    for c in range(NCORE):
        r = slice(RPC * c, RPC * c + RPC)
        in_maps.append({
            "z_rows": np.ascontiguousarray(z[r]),
            "g_row": np.asarray(inp["L1_attn_ln1_g"], np.float32)[li].reshape(1, D),
            "b_row": np.asarray(inp["L1_attn_ln1_b"], np.float32)[li].reshape(1, D),
            "wq": np.asarray(inp["L1_attn_wq"], np.float32)[li],
            "wk": np.asarray(inp["L1_attn_wk"], np.float32)[li],
            "wv": np.asarray(inp["L1_attn_wv"], np.float32)[li],
            "bq": np.asarray(inp["L1_attn_bq"], np.float32)[li].reshape(1, D),
            "bk": np.asarray(inp["L1_attn_bk"], np.float32)[li].reshape(1, D),
            "bv": np.asarray(inp["L1_attn_bv"], np.float32)[li].reshape(1, D),
            "tbl": np.ascontiguousarray(tbl[:, r]),
            "ident": IDENT,
        })
    res = _run(nc, in_maps)
    q_f = np.concatenate([np.asarray(r_["q_f"]) for r_ in res], axis=1)
    k_f = np.concatenate([np.asarray(r_["k_f"]) for r_ in res], axis=1)
    v_tok = np.concatenate([np.asarray(r_["v_tok"]) for r_ in res], axis=0)
    return q_f, k_f, v_tok



def build_subatt():
    nc = bass.Bass("TRN2", target_bir_lowering=False)
    q_d = nc.dram_tensor("q_sl", [512, T], F32, kind="ExternalInput").ap()
    k_d = nc.dram_tensor("k_sl", [512, T], F32, kind="ExternalInput").ap()
    v_d = nc.dram_tensor("v_sl", [T, 512], F32, kind="ExternalInput").ap()
    att_o = nc.dram_tensor("att_sl", [512, T], F32, kind="ExternalOutput").ap()

    NKT = T // 128
    NQT = T // 512
    ser = Ser()
    with ExitStack() as ctx:
        sb = lambda name, shape, dt=BF16: ctx.enter_context(nc.sbuf_tensor(name, shape, dt))
        ps = lambda name, shape, dt=F32: ctx.enter_context(nc.psum_tensor(name, shape, dt))
        q_sb = sb("q_sb", [128, 4, T])
        k_sb = sb("k_sb", [128, 4, T])
        v_sb = sb("v_sb2", [128, NKT, 512])
        vaug = sb("vaug2", [128, NKT, 8, 65])
        sq = sb("sq_sa", [128, T])
        pa = sb("pa", [128, 512])
        pb = sb("pb", [128, 512])
        araw = sb("araw", [64, 512], F32)
        asb = sb("asb", [64, 512], F32)
        rec = sb("rec", [1, 512], F32)
        onesb = sb("onesb", [128, 1])
        onesf = sb("onesf", [1, 128], F32)
        hsc = sb("hsc", [1, 16], F32)
        ksc = sb("ksc", [1, 16], F32)
        bias_sb = sb("bias_sb", [128, 8], F32)
        psS_A = ps("psS_A", [128, 512])
        psS_B = ps("psS_B", [128, 512])
        psO_A = ps("psO_A", [128, 512])
        psO_B = ps("psO_B", [128, 512])
        psR = ps("psR", [128, 512])
        psQ = ps("psQ", [128, 512])

        ser.dmag(lambda e: e.dma_start(out=q_sb[:], in_=q_d.rearrange("(c p) t -> p c t", p=128)))
        ser.dmag(lambda e: e.dma_start(out=k_sb[:], in_=k_d.rearrange("(c p) t -> p c t", p=128)))
        ser.dmag(lambda e: e.dma_start(out=v_sb[:], in_=v_d.rearrange("(kt p) d -> p kt d", p=128)))
        ser.vector(lambda e: e.memset(vaug[:], 1.0))
        for kt in range(NKT):
            for h in range(8):
                ser.vector(lambda e, kt=kt, h=h: e.tensor_copy(
                    vaug[:, kt, h, 0:64], v_sb[:, kt, 64 * h:64 * h + 64]))
        ser.vector(lambda e: e.memset(onesb[:], 1.0))
        ser.vector(lambda e: e.memset(onesf[:], 1.0))
        ser.vector(lambda e: e.memset(hsc[:], 0.0))
        ser.vector(lambda e: e.memset(ksc[:], 0.0))

        for src, acc in ((q_sb, hsc), (k_sb, ksc)):
            for c in range(4):
                ser.scalar(lambda e, c=c, src=src: e.activation(sq[:], src[:, c, :], AF.Square))
                for half in range(2):
                    h = 2 * c + half
                    pr = slice(64 * half, 64 * half + 64)
                    for n in range(NQT):
                        nsl = slice(512 * n, 512 * n + 512)
                        ser.tensor(lambda e, pr=pr, nsl=nsl: e.matmul(
                            psQ[0:1, :], onesb[pr, :], sq[pr, nsl], start=True, stop=True,
                            tile_position=(pr.start, 0)))
                        ser.vector(lambda e: e.reduce_max(rec[0:1, 0:1], psQ[0:1, :], AX.X))
                        ser.vector(lambda e, h=h, acc=acc: e.tensor_tensor(
                            acc[0:1, h:h + 1], acc[0:1, h:h + 1], rec[0:1, 0:1], ALU.max))
        ser.scalar(lambda e: e.activation(hsc[0:1, 8:16], hsc[0:1, 0:8], AF.Sqrt))
        ser.scalar(lambda e: e.activation(ksc[0:1, 8:16], ksc[0:1, 0:8], AF.Sqrt))
        for h in range(8):
            ser.vector(lambda e, h=h: e.tensor_tensor(
                hsc[0:1, h:h + 1], hsc[0:1, 8 + h:9 + h], ksc[0:1, 8 + h:9 + h], ALU.mult))
            ser.vector(lambda e, h=h: e.tensor_scalar_mul(hsc[0:1, h:h + 1], hsc[0:1, h:h + 1], -0.125))
            ser.tensor(lambda e, h=h: e.matmul(
                psR[:, 0:1], onesf[:, :], hsc[0:1, h:h + 1], start=True, stop=True))
            ser.vector(lambda e, h=h: e.tensor_copy(bias_sb[:, h:h + 1], psR[:, 0:1]))

        for p in range(4):
            for j in range(NQT):
                jsl = slice(512 * j, 512 * j + 512)
                for i in range(NKT):
                    isl = slice(128 * i, 128 * i + 128)
                    last = i == NKT - 1
                    ser.tensor(lambda e, p=p, isl=isl, jsl=jsl: e.matmul(
                        psS_A[:], k_sb[0:64, p, isl], q_sb[0:64, p, jsl],
                        start=True, stop=True, tile_position=(0, 0)))
                    ser.tensor(lambda e, p=p, isl=isl, jsl=jsl: e.matmul(
                        psS_B[:], k_sb[64:128, p, isl], q_sb[64:128, p, jsl],
                        start=True, stop=True, tile_position=(64, 0)))
                    ser.scalar(lambda e, p=p: e.activation(
                        pa[:], psS_A[:], AF.Exp, bias=bias_sb[:, 2 * p:2 * p + 1], scale=0.125))
                    ser.scalar(lambda e, p=p: e.activation(
                        pb[:], psS_B[:], AF.Exp, bias=bias_sb[:, 2 * p + 1:2 * p + 2], scale=0.125))
                    ser.tensor(lambda e, i=i, last=last, p=p: e.matmul(
                        psO_A[0:65, :], vaug[:, i, 2 * p, :], pa[:], start=(i == 0), stop=last))
                    ser.tensor(lambda e, i=i, last=last, p=p: e.matmul(
                        psO_B[0:65, :], vaug[:, i, 2 * p + 1, :], pb[:], start=(i == 0), stop=last))
                for half, psO in ((0, psO_A), (1, psO_B)):
                    h = 2 * p + half
                    ser.vector(lambda e, psO=psO: e.reciprocal(rec[:], psO[64:65, :]))
                    ser.tensor(lambda e: e.matmul(psR[0:64, :], onesf[:, 0:64], rec[:], start=True, stop=True))
                    ser.vector(lambda e, psO=psO: e.tensor_copy(araw[:], psO[0:64, :]))
                    ser.vector(lambda e: e.tensor_tensor(asb[:], araw[:], psR[0:64, :], ALU.mult))
                    ser.dma(lambda e, h=h, jsl=jsl: e.dma_start(
                        out=att_o[64 * h:64 * h + 64, jsl], in_=asb[:]))
        ser.emit(nc)
    return nc


def run_subatt(nc, q_f, k_f, v_tok):
    in_maps = []
    for c in range(NCORE):
        b, hg = c // 2, c % 2
        in_maps.append({
            "q_sl": np.ascontiguousarray(q_f[512 * hg:512 * hg + 512, T * b:T * b + T]),
            "k_sl": np.ascontiguousarray(k_f[512 * hg:512 * hg + 512, T * b:T * b + T]),
            "v_sl": np.ascontiguousarray(v_tok[T * b:T * b + T, 512 * hg:512 * hg + 512]),
        })
    res = _run(nc, in_maps)
    att = np.zeros((D, ROWS), np.float32)
    for c in range(NCORE):
        b, hg = c // 2, c % 2
        att[512 * hg:512 * hg + 512, T * b:T * b + T] = np.asarray(res[c]["att_sl"])
    return att



def build_subpost():
    nc = bass.Bass("TRN2", target_bir_lowering=False)
    z_d = nc.dram_tensor("z_rows", [RPC, D], F32, kind="ExternalInput").ap()
    att_d = nc.dram_tensor("att_sl", [D, RPC], F32, kind="ExternalInput").ap()
    wo_d = nc.dram_tensor("wo", [D, D], F32, kind="ExternalInput").ap()
    g2_d = nc.dram_tensor("g2_row", [1, D], F32, kind="ExternalInput").ap()
    b2_d = nc.dram_tensor("b2_row", [1, D], F32, kind="ExternalInput").ap()
    w1_d = nc.dram_tensor("w1", [D, 4 * D], F32, kind="ExternalInput").ap()
    b1_d = nc.dram_tensor("b1", [1, 4 * D], F32, kind="ExternalInput").ap()
    w2_d = nc.dram_tensor("w2", [4 * D, D], F32, kind="ExternalInput").ap()
    id_d = nc.dram_tensor("ident", [128, 128], F32, kind="ExternalInput").ap()
    z2_o = nc.dram_tensor("z2", [RPC, D], F32, kind="ExternalOutput").ap()

    ser = Ser()
    with ExitStack() as ctx:
        sb = lambda name, shape, dt=BF16: ctx.enter_context(nc.sbuf_tensor(name, shape, dt))
        ps = lambda name, shape, dt=F32: ctx.enter_context(nc.psum_tensor(name, shape, dt))
        ident = sb("ident_s", [128, 128])
        wo_sb = sb("wo_sb", [128, 8, D])
        w1_sb = sb("w1_sb", [128, 8, 4 * D])
        w2_sb = sb("w2_sb", [128, 32, D])
        b1_r = sb("b1_r", [1, 4 * D])
        g2b = sb("g2b", [128, D], F32)
        b2b = sb("b2b", [128, D], F32)
        gbounce = sb("gbounce", [1, D], F32)
        onesf = sb("onesf", [1, 128], F32)
        ones = sb("ones_spo", [1, 512])
        attf_t = sb("attf_t", [128, 8, 128])
        z_sb = sb("z_sb", [128, D], F32)
        z1_sb = sb("z1_sb", [128, D], F32)
        zl2a = sb("zl2a", [128, D], F32)
        zl2b = sb("zl2b", [128, D])
        zlf = sb("zlf_t", [128, 8, 128])
        sg = sb("sg", [128, 128])
        act = sb("act", [128, 128])
        z2_sb = sb("z2_sb", [128, D], F32)
        sq = sb("sq", [128, D], F32)
        st = sb("st", [128, 8], F32)
        psO = ps("psO_spo", [128, 512])
        psA = ps("psA_spo", [128, 128])
        psZ0 = ps("psZ0", [128, 512])
        psZ1 = ps("psZ1", [128, 512])
        psT = ps("psT_spo", [128, 128], BF16)

        ser.dmag(lambda e: e.dma_start(out=ident[:], in_=id_d[:]))
        ser.dmag(lambda e: e.dma_start(out=wo_sb[:], in_=wo_d.rearrange("(c p) n -> p c n", p=128)))
        ser.dmag(lambda e: e.dma_start(out=w1_sb[:], in_=w1_d.rearrange("(c p) n -> p c n", p=128)))
        ser.dmag(lambda e: e.dma_start(out=w2_sb[:], in_=w2_d.rearrange("(c p) n -> p c n", p=128)))
        ser.dmag(lambda e: e.dma_start(out=b1_r[:], in_=b1_d[:]))
        ser.vector(lambda e: e.memset(onesf[:], 1.0))
        ser.vector(lambda e: e.memset(ones[:], 1.0))
        for row_d, dst in ((g2_d, g2b), (b2_d, b2b)):
            ser.dma(lambda e, row_d=row_d: e.dma_start(out=gbounce[:], in_=row_d[:]))
            for nb in range(2):
                nsl = slice(512 * nb, 512 * nb + 512)
                ser.tensor(lambda e, nsl=nsl: e.matmul(
                    psO[:], onesf[:, :], gbounce[:, nsl], start=True, stop=True))
                ser.vector(lambda e, dst=dst, nsl=nsl: e.tensor_copy(dst[:, nsl], psO[:]))

        tiles = {"sq": sq, "st": st}
        NT = RPC // 128
        for i in range(NT):
            r = slice(128 * i, 128 * i + 128)
            tsl = slice(128 * i, 128 * i + 128)
            ser.dma(lambda e, r=r: e.dma_start(out=z_sb[:], in_=z_d[r, :]))
            ser.dmag(lambda e, tsl=tsl: e.dma_start(
                out=attf_t[:], in_=att_d.rearrange("(c p) t -> p c t", p=128)[:, :, tsl]))
            for nb in range(2):
                nsl = slice(512 * nb, 512 * nb + 512)
                for c in range(8):
                    ser.tensor(lambda e, c=c, nsl=nsl: e.matmul(
                        psO[:], attf_t[:, c, :], wo_sb[:, c, nsl], start=(c == 0), stop=(c == 7)))
                ser.vector(lambda e, nsl=nsl: e.tensor_tensor(
                    z1_sb[:, nsl], z_sb[:, nsl], psO[:], ALU.add))
            ln_tile(ser, None, z1_sb, zl2a, tiles)
            ser.vector(lambda e: e.tensor_tensor(zl2a[:], zl2a[:], g2b[:], ALU.mult))
            ser.vector(lambda e: e.tensor_tensor(zl2b[:], zl2a[:], b2b[:], ALU.add))
            for c in range(8):
                ser.tensor(lambda e, c=c: e.transpose(psT[:], zl2b[:, 128 * c:128 * c + 128], ident[:]))
                ser.vector(lambda e, c=c: e.tensor_copy(zlf[:, c, :], psT[:]))
            for mh in range(32):
                for c in range(8):
                    ser.tensor(lambda e, c=c, mh=mh: e.matmul(
                        psA[:], w1_sb[:, c, 128 * mh:128 * mh + 128], zlf[:, c, :],
                        start=(c == 0), stop=False))
                ser.tensor(lambda e, mh=mh: e.matmul(
                    psA[:], b1_r[:, 128 * mh:128 * mh + 128], ones[:, 0:128], start=False, stop=True))
                ser.scalar(lambda e: e.activation(sg[:], psA[:], AF.Sigmoid, scale=1.702))
                ser.vector(lambda e: e.tensor_tensor(act[:], sg[:], psA[:], ALU.mult))
                for nb, psZ in ((0, psZ0), (1, psZ1)):
                    nsl = slice(512 * nb, 512 * nb + 512)
                    ser.tensor(lambda e, mh=mh, nsl=nsl, psZ=psZ: e.matmul(
                        psZ[:], act[:], w2_sb[:, mh, nsl], start=(mh == 0), stop=(mh == 31)))
            for nb, psZ in ((0, psZ0), (1, psZ1)):
                nsl = slice(512 * nb, 512 * nb + 512)
                ser.vector(lambda e, nsl=nsl, psZ=psZ: e.tensor_tensor(
                    z2_sb[:, nsl], z1_sb[:, nsl], psZ[:], ALU.add))
            ser.dma(lambda e, r=r: e.dma_start(out=z2_o[r, :], in_=z2_sb[:]))
        ser.emit(nc)
    return nc


def run_subpost(nc, z, att_f, inp, li):
    in_maps = []
    for c in range(NCORE):
        r = slice(RPC * c, RPC * c + RPC)
        in_maps.append({
            "z_rows": np.ascontiguousarray(z[r]),
            "att_sl": np.ascontiguousarray(att_f[:, r]),
            "wo": np.asarray(inp["L1_attn_wo"], np.float32)[li],
            "g2_row": np.asarray(inp["L1_attn_ln2_g"], np.float32)[li].reshape(1, D),
            "b2_row": np.asarray(inp["L1_attn_ln2_b"], np.float32)[li].reshape(1, D),
            "w1": np.asarray(inp["L1_attn_w1"], np.float32)[li],
            "b1": np.asarray(inp["L1_attn_b1"], np.float32)[li].reshape(1, 4 * D),
            "w2": np.asarray(inp["L1_attn_w2"], np.float32)[li],
            "ident": IDENT,
        })
    res = _run(nc, in_maps)
    return np.concatenate([np.asarray(r_["z2"]) for r_ in res], axis=0)



QTH = (0.25, 0.75, 1.25, 1.75, 2.5, 3.5, 5.0)


def emit_quant(ser, src, dst, W, scr):
    scrW = scr["aw"].shape[1]
    if W > scrW:
        for a in range(0, W, scrW):
            w = min(scrW, W - a)
            emit_quant(ser, src[:, a:a + w], dst[:, a:a + w], w, scr)
        return
    nb = W // 16
    aw = scr["aw"][:, 0:W]
    ind = scr["ind"][:, 0:W]
    acc = scr["acc"][:, 0:W]
    sgn = scr["sgn"][:, 0:W]
    sblk = scr["sblk"][:, 0:nb]
    aw3 = aw.rearrange("p (b j) -> p b j", j=16)
    ind3 = ind.rearrange("p (b j) -> p b j", j=16)
    s3 = sblk[:, :, None].broadcast_to([128, nb, 16])
    ser.scalar(lambda e: e.activation(aw, src, AF.Abs))
    ser.scalar(lambda e: e.activation(sgn, src, AF.Sign))
    ser.vector(lambda e: e.tensor_reduce(sblk, aw3, AX.X, ALU.max))
    ser.vector(lambda e: e.tensor_scalar_max(sblk, sblk, 6e-12))
    ser.vector(lambda e: e.tensor_scalar_mul(aw, aw, 24.0))
    U = (1.0, 3.0, 5.0, 7.0, 10.0, 14.0, 20.0)
    ser.vector(lambda e: e.scalar_tensor_tensor(
        out=acc.rearrange("p (b j) -> p b j", j=16), in0=s3, scalar=U[0], in1=aw3,
        op0=ALU.mult, op1=ALU.is_lt))
    for u in U[1:4]:
        ser.vector(lambda e, u=u: e.scalar_tensor_tensor(
            out=ind3, in0=s3, scalar=u, in1=aw3, op0=ALU.mult, op1=ALU.is_lt))
        ser.vector(lambda e: e.tensor_tensor(acc, acc, ind, ALU.add))
    for u, w in ((U[4], 2.0), (U[5], 2.0), (U[6], 4.0)):
        ser.vector(lambda e, u=u: e.scalar_tensor_tensor(
            out=ind3, in0=s3, scalar=u, in1=aw3, op0=ALU.mult, op1=ALU.is_lt))
        ser.vector(lambda e, w=w: e.scalar_tensor_tensor(
            out=acc, in0=ind, scalar=w, in1=acc, op0=ALU.mult, op1=ALU.add))
    ser.vector(lambda e: e.tensor_tensor(ind, acc, sgn, ALU.mult))
    ser.vector(lambda e: e.scalar_tensor_tensor(
        out=dst.rearrange("p (b j) -> p b j", j=16), in0=s3, scalar=1.0 / 12.0, in1=ind3,
        op0=ALU.mult, op1=ALU.mult))


def build_p9(nt=None):
    HI = 2816
    nc = bass.Bass("TRN2", target_bir_lowering=False)
    xn1_d = nc.dram_tensor("xn1_rows", [RPC, D], F32, kind="ExternalInput").ap()
    x1_d = nc.dram_tensor("x1_rows", [RPC, D], F32, kind="ExternalInput").ap()
    z4_d = nc.dram_tensor("z4_rows", [RPC, D], F32, kind="ExternalInput").ap()
    wg_d = nc.dram_tensor("w_gate", [HI, D], F32, kind="ExternalInput").ap()
    wu_d = nc.dram_tensor("w_up", [HI, D], F32, kind="ExternalInput").ap()
    wd_d = nc.dram_tensor("w_down", [D, HI], F32, kind="ExternalInput").ap()
    id_d = nc.dram_tensor("ident", [128, 128], F32, kind="ExternalInput").ap()
    out_o = nc.dram_tensor("out_rows", [RPC, D], F32, kind="ExternalOutput").ap()

    ser = Ser()
    with ExitStack() as ctx:
        sb = lambda name, shape, dt=BF16: ctx.enter_context(nc.sbuf_tensor(name, shape, dt))
        ps = lambda name, shape, dt=F32: ctx.enter_context(nc.psum_tensor(name, shape, dt))
        ident = sb("ident_s", [128, 128])
        wgqT = sb("wgqT", [128, 8, HI])
        wuqT = sb("wuqT", [128, 8, HI])
        wdqT = sb("wdqT", [128, 22, D])
        SW = HI // 2
        stage = sb("stage_c", [128, SW], F32)
        wqc = sb("wqc", [128, HI])
        scr = {"aw": sb("aw", [128, SW], F32), "ind": sb("ind_q", [128, SW]),
               "acc": sb("acc_q", [128, SW]), "sgn": sb("sgn_q", [128, SW]),
               "sblk": sb("sblk", [128, SW // 16], F32), "six": sb("six_q", [128, 1], F32),
               "awb": sb("awb_q", [128, SW]), "sb16": sb("sb16_q", [128, SW // 16])}
        xq = sb("xq", [128, D])
        xqf = sb("xqf", [128, 8, 128])
        h_t = sb("h_t", [128, HI], F32)
        hq = sb("hq", [128, HI])
        hqf = sb("hqf", [128, 22, 128])
        sgh = sb("sgh", [128, 512])
        x1_sb = sb("x1_sb", [128, D], F32)
        z4_sb = sb("z4_sb", [128, D], F32)
        xn1_sb = sb("xn1_sb", [128, D], F32)
        out_sb = sb("out_sb", [128, D], F32)
        psG = ps("psG", [128, 512])
        psU = ps("psU", [128, 512])
        psD = ps("psD", [128, 512])
        psT = ps("psT_p9", [128, 128], BF16)

        ser.dmag(lambda e: e.dma_start(out=ident[:], in_=id_d[:]))
        for (w_d, wT, ncw, W) in ((wg_d, wgqT, 22, D), (wu_d, wuqT, 22, D), (wd_d, wdqT, 8, HI)):
            for co in range(ncw):
                for a in range(0, W, SW):
                    w_ = min(SW, W - a)
                    ser.dma(lambda e, w_d=w_d, co=co, a=a, w_=w_: e.dma_start(
                        out=stage[:, 0:w_], in_=w_d[128 * co:128 * co + 128, a:a + w_]))
                    emit_quant(ser, stage[:, 0:w_], wqc[:, a:a + w_], w_, scr)
                for ci in range(W // 128):
                    ser.tensor(lambda e, ci=ci: e.transpose(
                        psT[:], wqc[:, 128 * ci:128 * ci + 128], ident[:]))
                    ser.vector(lambda e, ci=ci, co=co, wT=wT: e.tensor_copy(
                        wT[:, ci, 128 * co:128 * co + 128], psT[:]))

        gate_blocks = [(0, 512), (512, 512), (1024, 512), (1536, 512), (2048, 512), (2560, 256)]
        NT = nt if nt is not None else RPC // 128
        for i in range(NT):
            r = slice(128 * i, 128 * i + 128)
            ser.dma(lambda e, r=r: e.dma_start(out=xn1_sb[:], in_=xn1_d[r, :]))
            ser.dma(lambda e, r=r: e.dma_start(out=x1_sb[:], in_=x1_d[r, :]))
            ser.dma(lambda e, r=r: e.dma_start(out=z4_sb[:], in_=z4_d[r, :]))
            emit_quant(ser, xn1_sb[:, :], xq[:, :], D, scr)
            for c in range(8):
                ser.tensor(lambda e, c=c: e.transpose(psT[:], xq[:, 128 * c:128 * c + 128], ident[:]))
                ser.vector(lambda e, c=c: e.tensor_copy(xqf[:, c, :], psT[:]))
            for o0, ow in gate_blocks:
                osl = slice(o0, o0 + ow)
                for c in range(8):
                    ser.tensor(lambda e, c=c, osl=osl, ow=ow: e.matmul(
                        psG[:, 0:ow], xqf[:, c, :], wgqT[:, c, osl], start=(c == 0), stop=(c == 7)))
                for c in range(8):
                    ser.tensor(lambda e, c=c, osl=osl, ow=ow: e.matmul(
                        psU[:, 0:ow], xqf[:, c, :], wuqT[:, c, osl], start=(c == 0), stop=(c == 7)))
                ser.scalar(lambda e, ow=ow: e.activation(sgh[:, 0:ow], psG[:, 0:ow], AF.Silu))
                ser.vector(lambda e, osl=osl, ow=ow: e.tensor_tensor(
                    h_t[:, osl], sgh[:, 0:ow], psU[:, 0:ow], ALU.mult))
            emit_quant(ser, h_t[:, :], hq[:, :], HI, scr)
            for c in range(22):
                ser.tensor(lambda e, c=c: e.transpose(psT[:], hq[:, 128 * c:128 * c + 128], ident[:]))
                ser.vector(lambda e, c=c: e.tensor_copy(hqf[:, c, :], psT[:]))
            for nb in range(2):
                nsl = slice(512 * nb, 512 * nb + 512)
                for c in range(22):
                    ser.tensor(lambda e, c=c, nsl=nsl: e.matmul(
                        psD[:], hqf[:, c, :], wdqT[:, c, nsl], start=(c == 0), stop=(c == 21)))
                ser.vector(lambda e, nsl=nsl: e.tensor_tensor(
                    out_sb[:, nsl], x1_sb[:, nsl], z4_sb[:, nsl], ALU.add))
                ser.vector(lambda e, nsl=nsl: e.tensor_tensor(
                    out_sb[:, nsl], out_sb[:, nsl], psD[:], ALU.add))
            ser.dma(lambda e, r=r: e.dma_start(out=out_o[r, :], in_=out_sb[:]))
        ser.emit(nc)
    return nc


def run_p9(xn1, x1, z4, inp):
    nc = build_p9()
    in_maps = []
    for c in range(NCORE):
        r = slice(RPC * c, RPC * c + RPC)
        in_maps.append({
            "xn1_rows": np.ascontiguousarray(xn1[r]),
            "x1_rows": np.ascontiguousarray(x1[r]),
            "z4_rows": np.ascontiguousarray(z4[r]),
            "w_gate": np.asarray(inp["L1_ffn_w_gate"], np.float32),
            "w_up": np.asarray(inp["L1_ffn_w_up"], np.float32),
            "w_down": np.asarray(inp["L1_ffn_w_down"], np.float32),
            "ident": IDENT,
        })
    res = _run(nc, in_maps)
    return np.concatenate([np.asarray(r_["out_rows"]) for r_ in res], axis=0)



def build_fused():
    HI = 2816
    nc = bass.Bass("TRN2", target_bir_lowering=False)
    I = lambda n, s: nc.dram_tensor(n, s, F32, kind="ExternalInput").ap()
    xfull_d = I("x_full", [ROWS, D])
    xb_d = I("x_batch", [T, D])
    Wq_d, bq_d = I("Wq", [D, D]), I("bq", [1, D])
    Wk_d, bk_d = I("Wk", [D, 256]), I("bk", [1, 256])
    Wv_d, bv_d = I("Wv", [D, 256]), I("bv", [1, 256])
    Wo_d, bo_d = I("Wo", [D, D]), I("bo", [1, D])
    wrec_d = {nm: I(nm, [D, D]) for nm in ("wf", "uf", "wi", "ui", "wc", "uc")}
    brec_d = {nm: I(nm, [1, D]) for nm in ("bf", "bi", "bc")}
    inc_d = I("in_c", [128, 8])
    wof_d, uof_d, bof_d = I("wof", [D, D]), I("uof", [D, D]), I("bof", [1, D])
    l1 = []
    for li in range(2):
        l1.append({
            "g1": I(f"g1_{li}", [1, D]), "b1": I(f"b1r_{li}", [1, D]),
            "wq": I(f"wq{li}", [D, D]), "wk": I(f"wk{li}", [D, D]), "wv": I(f"wv{li}", [D, D]),
            "bq": I(f"bqr{li}", [1, D]), "bk": I(f"bkr{li}", [1, D]), "bv": I(f"bvr{li}", [1, D]),
            "wo": I(f"wo{li}", [D, D]), "g2": I(f"g2_{li}", [1, D]), "b2": I(f"b2_{li}", [1, D]),
            "w1": I(f"w1_{li}", [D, 4 * D]), "b1f": I(f"b1f_{li}", [1, 4 * D]),
            "w2": I(f"w2_{li}", [4 * D, D]),
        })
    wg_d, wu_d, wd_d = I("w_gate", [HI, D]), I("w_up", [HI, D]), I("w_down", [D, HI])
    cc_d, ss_d, tb_d = I("cc0", [128, T]), I("ss0", [128, T]), I("tbl1", [128, T])
    m_d = I("masks", [128, 4, 512])
    id_d = I("ident", [128, 128])
    out_o = nc.dram_tensor("out_full", [T, D], F32, kind="ExternalOutput").ap()

    S = lambda n, s, dt=BF16: nc.dram_tensor(n, s, dt).ap()
    s_cf = S("s_cf", [D, T])
    s_q0 = S("s_q0", [D, T])
    s_k0 = S("s_k0", [256, T])
    s_v0 = S("s_v0", [T, 256])
    s_att = S("s_att", [D, T])
    s_x1 = S("s_x1", [T, D], F32)
    s_xn1 = S("s_xn1", [T, D], F32)
    s_z2 = S("s_z2", [T, D], F32)
    s_z4 = S("s_z4", [T, D], F32)
    s_qf = S("s_qf", [D, T])
    s_kf = S("s_kf", [D, T])
    s_vt = S("s_vt", [T, D])

    NTB = T // 128
    NBB = T // 512

    with ExitStack() as sctx:
        sems = {k: sctx.enter_context(nc.semaphore("fs_" + k)) for k in Ser.KEYS}
        acc = {k: 0 for k in Ser.KEYS}

        with ExitStack() as ctx:
            sb = lambda name, shape, dt=BF16: ctx.enter_context(nc.sbuf_tensor(name, shape, dt))
            ps = lambda name, shape, dt=F32: ctx.enter_context(nc.psum_tensor(name, shape, dt))
            ser = Ser()
            ident = sb("f2_id", [128, 128])
            stage = sb("f2_stage", [128, 8, D])
            wT = {nm: sb("f2_" + nm, [128, 8, D]) for nm in wrec_d}
            brow = {nm: sb("f2_" + nm, [1, D]) for nm in brec_d}
            inc_sb = sb("f2_inc", [128, 8], F32)
            ones = sb("f2_ones", [1, 512])
            xf = sb("f2_xf", [128, 8, 512])
            xld2 = [sb("f2_xld0", [128, D], F32), sb("f2_xld1", [128, D], F32)]
            xnb = sb("f2_xnb", [128, D])
            sq = sb("f2_sq", [128, D], F32)
            st = sb("f2_st", [128, 8], F32)
            cf32 = sb("f2_c", [128, 8, 512], F32)
            cb16 = sb("f2_cb", [128, 8, 512])
            cnew = sb("f2_cn", [128, 8, 512], F32)
            gsb = sb("f2_g", [128, 8, 512], F32)
            gch2 = [sb("f2_gch0", [128, 512], F32), sb("f2_gch1", [128, 512], F32)]
            tmp2 = [sb("f2_tmp0", [128, 512], F32), sb("f2_tmp1", [128, 512], F32)]
            psA2 = [ps("f2_psA0", [128, 512]), ps("f2_psA1", [128, 512])]
            psT = [ps("f2_psT0", [128, 128], BF16), ps("f2_psT1", [128, 128], BF16)]

            ser.dmag(lambda e: e.dma_start(out=ident[:], in_=id_d[:]))
            for nm in wrec_d:
                emit_load_wT(ser, nc, wrec_d[nm], stage, wT[nm], ident, psT)
            for nm in brec_d:
                ser.dmag(lambda e, nm=nm: e.dma_start(out=brow[nm][:], in_=brec_d[nm][:]))
            ser.dma(lambda e: e.dma_start(out=inc_sb[:], in_=inc_d[:]))
            ser.vector(lambda e: e.memset(ones[:], 1.0))
            tiles = {"sq": sq, "st": st}
            tile_marks = {}

            def gate_psum(wname, uname, bname, m, psA, w0):
                for ci in range(8):
                    ser.tensor(lambda e, ci=ci, m=m, wname=wname, psA=psA: e.matmul(
                        psA[:], wT[wname][:, ci, 128 * m:128 * m + 128], xf[:, ci, :],
                        start=(ci == 0), stop=False), wait=w0)
                for ci in range(8):
                    ser.tensor(lambda e, ci=ci, m=m, uname=uname, psA=psA: e.matmul(
                        psA[:], wT[uname][:, ci, 128 * m:128 * m + 128], cb16[:, ci, :],
                        start=False, stop=False), wait=w0)
                ser.tensor(lambda e, m=m, bname=bname, psA=psA: e.matmul(
                    psA[:], brow[bname][:, 128 * m:128 * m + 128], ones[:], start=False, stop=True), wait=w0)

            for blk in range(NBB):
                for m in range(8):
                    ser.vector(lambda e, m=m: e.memset(cf32[:, m, :], 0.0))
                    ser.vector(lambda e, m=m: e.tensor_scalar_add(cf32[:, m, :], cf32[:, m, :], inc_sb[:, m:m + 1]))
                    ser.vector(lambda e, m=m: e.tensor_copy(cb16[:, m, :], cf32[:, m, :]))
                for t in range(4):
                    for rt in range(4):
                        row = t * T + 512 * blk + 128 * rt
                        r = slice(row, row + 128)
                        k_ld = 4 * t + rt
                        xld = xld2[k_ld % 2]
                        w_ld = tile_marks.get(("f2", blk, k_ld - 1))
                        ser.dma(lambda e, r=r, xld=xld: e.dma_start(out=xld[:], in_=xfull_d[r, :]), wait=w_ld)
                        tile_marks[("f2", blk, k_ld)] = ser.mark()
                        ln_tile(ser, None, xld, xnb, tiles)
                        w0t = ser.mark()
                        emit_tr(ser, [(xnb[:, 128 * c:128 * c + 128], xf[:, c, 128 * rt:128 * rt + 128])
                                      for c in range(8)], psT, ident, w0=w0t)
                    mdone = {}
                    for m in range(8):
                        pi = m % 2
                        gate_psum("wc", "uc", "bc", m, psA2[pi], mdone.get(m - 2))
                        mk = ser.mark()
                        ser.scalar(lambda e, m=m, pi=pi: e.activation(
                            gsb[:, m, :], psA2[pi][:], AF.Tanh), wait=mk)
                        mdone[m] = ser.mark()
                    mdone = {}
                    for m in range(8):
                        pi = m % 2
                        gate_psum("wf", "uf", "bf", m, psA2[pi], mdone.get(m - 2))
                        mk = ser.mark()
                        ser.scalar(lambda e, pi=pi: e.activation(
                            gch2[pi][:], psA2[pi][:], AF.Sigmoid), wait=mk)
                        mk2 = ser.mark()
                        ser.vector(lambda e, m=m, pi=pi: e.tensor_tensor(
                            cnew[:, m, :], gch2[pi][:], cf32[:, m, :], ALU.mult), wait=mk2)
                        mdone[m] = ser.mark()
                    mdone = {}
                    for m in range(8):
                        pi = m % 2
                        gate_psum("wi", "ui", "bi", m, psA2[pi], mdone.get(m - 2))
                        mk = ser.mark()
                        ser.scalar(lambda e, pi=pi: e.activation(
                            gch2[pi][:], psA2[pi][:], AF.Sigmoid), wait=mk)
                        mk2 = ser.mark()
                        ser.vector(lambda e, m=m, pi=pi: e.tensor_tensor(
                            tmp2[pi][:], gch2[pi][:], gsb[:, m, :], ALU.mult), wait=mk2)
                        ser.vector(lambda e, m=m, pi=pi: e.tensor_tensor(
                            cf32[:, m, :], cnew[:, m, :], tmp2[pi][:], ALU.add))
                        mdone[m] = ser.mark()
                    for m in range(8):
                        ser.vector(lambda e, m=m: e.tensor_copy(cb16[:, m, :], cf32[:, m, :]))
                for m in range(8):
                    ser.dmag(lambda e, m=m, blk=blk: e.dma_start(
                        out=s_cf[128 * m:128 * m + 128, 512 * blk:512 * blk + 512],
                        in_=cf32[:, m, :]))
            ser.emit_block(nc, sems, acc)

        with ExitStack() as ctx:
            sb = lambda name, shape, dt=BF16: ctx.enter_context(nc.sbuf_tensor(name, shape, dt))
            ps = lambda name, shape, dt=F32: ctx.enter_context(nc.psum_tensor(name, shape, dt))
            ser = Ser()
            ident = sb("f3_id", [128, 128])
            wq = sb("f3_wq", [128, 8, D])
            wk = sb("f3_wk", [128, 8, 256])
            wv = sb("f3_wv", [128, 8, 256])
            bqs = sb("f3_bq", [1, D])
            bks = sb("f3_bk", [1, 256])
            bvs = sb("f3_bv", [1, 256])
            ones = sb("f3_ones", [1, 512])
            cc = sb("f3_cc", [128, T])
            ss = sb("f3_ss", [128, T])
            xldt = sb("f3_xld", [128, D], F32)
            xnt = sb("f3_xnt", [128, D])
            sqf3 = sb("f3_sq", [128, D], F32)
            stf3 = sb("f3_st", [128, 8], F32)
            xnf = sb("f3_xnf", [128, 8, T])
            qraw = sb("f3_qraw", [128, 512])
            qshuf = sb("f3_qshuf", [128, 512])
            t1 = sb("f3_t1", [128, 512])
            t2 = sb("f3_t2", [128, 512])
            stage = sb("f3_stage", [128, T])
            v_sb = sb("f3_v", [128, 256])
            psA = ps("f3_psA", [128, 512])
            psT = [ps("f3_psT0", [128, 128], BF16), ps("f3_psT1", [128, 128], BF16)]
            psV = ps("f3_psV", [128, 256])

            ser.dmag(lambda e: e.dma_start(out=ident[:], in_=id_d[:]))
            ser.dmag(lambda e: e.dma_start(out=wq[:], in_=Wq_d.rearrange("(c p) n -> p c n", p=128)))
            ser.dmag(lambda e: e.dma_start(out=wk[:], in_=Wk_d.rearrange("(c p) n -> p c n", p=128)))
            ser.dmag(lambda e: e.dma_start(out=wv[:], in_=Wv_d.rearrange("(c p) n -> p c n", p=128)))
            ser.dmag(lambda e: e.dma_start(out=bqs[:], in_=bq_d[:]))
            ser.dmag(lambda e: e.dma_start(out=bks[:], in_=bk_d[:]))
            ser.dmag(lambda e: e.dma_start(out=bvs[:], in_=bv_d[:]))
            ser.dmag(lambda e: e.dma_start(out=cc[:], in_=cc_d[:]))
            ser.dmag(lambda e: e.dma_start(out=ss[:], in_=ss_d[:]))
            ser.vector(lambda e: e.memset(ones[:], 1.0))
            tilesf3 = {"sq": sqf3, "st": stf3}
            for i in range(NTB):
                ser.dma(lambda e, i=i: e.dma_start(out=xldt[:], in_=xb_d[128 * i:128 * i + 128, :]))
                ln_tile(ser, None, xldt, xnt, tilesf3)
                w0t = ser.mark()
                emit_tr(ser, [(xnt[:, 128 * c:128 * c + 128], xnf[:, c, 128 * i:128 * i + 128])
                              for c in range(8)], psT, ident, w0=w0t)
            for (wsb, bsb, nm_, out_d) in ((wq, bqs, 8, s_q0), (wk, bks, 2, s_k0)):
                for m in range(nm_):
                    msl = slice(128 * m, 128 * m + 128)
                    for n in range(T // 512):
                        nsl = slice(512 * n, 512 * n + 512)
                        w0 = ser.mark()
                        for c in range(8):
                            ser.tensor(lambda e, c=c, msl=msl, nsl=nsl, wsb=wsb: e.matmul(
                                psA[:], wsb[:, c, msl], xnf[:, c, nsl], start=(c == 0), stop=False), wait=w0)
                        ser.tensor(lambda e, msl=msl, bsb=bsb: e.matmul(
                            psA[:], bsb[:, msl], ones[:], start=False, stop=True), wait=w0)
                        ser.vector(lambda e: e.tensor_copy(qraw[:], psA[:]))
                        ser.vector(lambda e: e.stream_shuffle(qshuf[:], qraw[:], SWAPMASK))
                        ser.vector(lambda e, nsl=nsl: e.tensor_tensor(t1[:], qraw[:], cc[:, nsl], ALU.mult))
                        ser.vector(lambda e, nsl=nsl: e.tensor_tensor(t2[:], qshuf[:], ss[:, nsl], ALU.mult))
                        ser.vector(lambda e, nsl=nsl: e.tensor_tensor(stage[:, nsl], t1[:], t2[:], ALU.add))
                    ser.dma(lambda e, msl=msl, out_d=out_d: e.dma_start(out=out_d[msl, :], in_=stage[:]))
            for i in range(NTB):
                w0 = ser.mark()
                for c in range(8):
                    ser.tensor(lambda e, c=c, i=i: e.matmul(
                        psV[:], xnf[:, c, 128 * i:128 * i + 128], wv[:, c, :],
                        start=(c == 0), stop=False), wait=w0)
                ser.tensor(lambda e: e.matmul(psV[:], ones[:, 0:128], bvs[:], start=False, stop=True), wait=w0)
                ser.vector(lambda e: e.tensor_copy(v_sb[:], psV[:]))
                ser.dma(lambda e, i=i: e.dma_start(out=s_v0[128 * i:128 * i + 128, :], in_=v_sb[:]))
            ser.emit_block(nc, sems, acc)

        with ExitStack() as ctx:
            sb = lambda name, shape, dt=BF16: ctx.enter_context(nc.sbuf_tensor(name, shape, dt))
            ps = lambda name, shape, dt=F32: ctx.enter_context(nc.psum_tensor(name, shape, dt))
            ser = Ser()
            q_sb = sb("fb_q", [128, 8, T])
            k2 = sb("fb_k2", [128, 4, T])
            v_sb = sb("fb_v", [128, 32, 256])
            vaug = sb("fb_vaug", [128, 32, 260])
            mask = sb("fb_mask", [128, 4, 512])
            sq = sb("fb_sq", [128, T])
            pa3 = [sb("fb_pa0", [128, 1024]), sb("fb_pa1", [128, 1024])]
            araw = sb("fb_araw", [64, 512], F32)
            asb = sb("fb_asb", [64, 512])
            rec = sb("fb_rec", [1, 512], F32)
            onesb = sb("fb_onesb", [128, 1])
            onesf = sb("fb_onesf", [1, 128], F32)
            hsc = sb("fb_hsc", [1, 16], F32)
            ksc = sb("fb_ksc", [1, 8], F32)
            bias_sb = sb("fb_bias", [128, 8], F32)
            psS3 = [ps("fb_psS0", [128, 1024]), ps("fb_psS1", [128, 1024])]
            psO_A = ps("fb_psOA", [128, 512])
            psO_B = ps("fb_psOB", [128, 512])
            psR = ps("fb_psR", [128, 512])
            psQ = ps("fb_psQ", [128, 512])

            ser.dma(lambda e: e.dma_start(out=q_sb[:], in_=s_q0.rearrange("(c p) t -> p c t", p=128)))
            for g in range(4):
                for half in range(2):
                    ser.dma(lambda e, g=g, half=half: e.dma_start(
                        out=k2[64 * half:64 * half + 64, g, :], in_=s_k0[64 * g:64 * g + 64, :]))
            ser.dma(lambda e: e.dma_start(out=v_sb[:], in_=s_v0.rearrange("(kt p) d -> p kt d", p=128)))
            ser.dmag(lambda e: e.dma_start(out=mask[:], in_=m_d[:]))
            ser.vector(lambda e: e.memset(vaug[:], 1.0))
            for kt in range(32):
                for g in range(4):
                    ser.vector(lambda e, kt=kt, g=g: e.tensor_copy(
                        vaug[:, kt, 65 * g:65 * g + 64], v_sb[:, kt, 64 * g:64 * g + 64]))
            ser.vector(lambda e: e.memset(onesb[:], 1.0))
            ser.vector(lambda e: e.memset(onesf[:], 1.0))
            ser.vector(lambda e: e.memset(hsc[:], 0.0))
            ser.vector(lambda e: e.memset(ksc[:], 0.0))
            for c in range(8):
                ser.scalar(lambda e, c=c: e.activation(sq[:], q_sb[:, c, :], AF.Square))
                for n in range(T // 512):
                    nsl = slice(512 * n, 512 * n + 512)
                    ser.tensor(lambda e, nsl=nsl: e.matmul(
                        psQ[0:1, :], onesb[:, :], sq[:, nsl], start=True, stop=True))
                    ser.vector(lambda e: e.reduce_max(rec[0:1, 0:1], psQ[0:1, :], AX.X))
                    ser.vector(lambda e, c=c: e.tensor_tensor(
                        hsc[0:1, c:c + 1], hsc[0:1, c:c + 1], rec[0:1, 0:1], ALU.max))
            for g in range(4):
                ser.scalar(lambda e, g=g: e.activation(sq[:], k2[:, g, :], AF.Square))
                for n in range(T // 512):
                    nsl = slice(512 * n, 512 * n + 512)
                    ser.tensor(lambda e, nsl=nsl: e.matmul(
                        psQ[0:1, :], onesb[:, :], sq[:, nsl], start=True, stop=True))
                    ser.vector(lambda e: e.reduce_max(rec[0:1, 0:1], psQ[0:1, :], AX.X))
                    ser.vector(lambda e, g=g: e.tensor_tensor(
                        ksc[0:1, g:g + 1], ksc[0:1, g:g + 1], rec[0:1, 0:1], ALU.max))
            ser.scalar(lambda e: e.activation(hsc[0:1, 8:16], hsc[0:1, 0:8], AF.Sqrt))
            ser.scalar(lambda e: e.activation(ksc[0:1, 4:8], ksc[0:1, 0:4], AF.Sqrt))
            for p in range(8):
                g = p // 2
                ser.vector(lambda e, p=p, g=g: e.tensor_tensor(
                    hsc[0:1, p:p + 1], hsc[0:1, 8 + p:9 + p], ksc[0:1, 4 + g:5 + g], ALU.mult))
                ser.vector(lambda e, p=p: e.tensor_scalar_mul(hsc[0:1, p:p + 1], hsc[0:1, p:p + 1], -0.125))
                ser.tensor(lambda e, p=p: e.matmul(
                    psR[:, 0:1], onesf[:, :], hsc[0:1, p:p + 1], start=True, stop=True))
                ser.vector(lambda e, p=p: e.tensor_copy(bias_sb[:, p:p + 1], psR[:, 0:1]))
            prev_tail = None
            for p in range(8):
                g = p // 2
                for j in range(T // 512):
                    jsl = slice(512 * j, 512 * j + 512)
                    nkt = 4 * j + 4
                    m_expd = {}
                    for i in range(nkt + 1):
                        pi = i % 2
                        if i < nkt:
                            isl = slice(128 * i, 128 * i + 128)
                            w_war = m_expd.get(i - 2, prev_tail)
                            ser.tensor(lambda e, g=g, isl=isl, p=p, jsl=jsl, pi=pi: e.matmul(
                                psS3[pi][:, 0:512], k2[0:64, g, isl], q_sb[0:64, p, jsl],
                                start=True, stop=True, tile_position=(0, 0)), wait=w_war)
                            ser.tensor(lambda e, g=g, isl=isl, p=p, jsl=jsl, pi=pi: e.matmul(
                                psS3[pi][:, 512:1024], k2[64:128, g, isl], q_sb[64:128, p, jsl],
                                start=True, stop=True, tile_position=(64, 0)), wait=w_war)
                            mk_mm = ser.mark()
                        if i >= 1:
                            io = i - 1
                            pio = io % 2
                            lasto = io == nkt - 1
                            wav = m_expd[io]
                            ser.tensor(lambda e, io=io, lasto=lasto, g=g, pio=pio: e.matmul(
                                psO_A[0:65, :], vaug[:, io, 65 * g:65 * g + 65], pa3[pio][:, 0:512],
                                start=(io == 0), stop=lasto), wait=wav)
                            ser.tensor(lambda e, io=io, lasto=lasto, g=g, pio=pio: e.matmul(
                                psO_B[0:65, :], vaug[:, io, 65 * g:65 * g + 65], pa3[pio][:, 512:1024],
                                start=(io == 0), stop=lasto), wait=wav)
                        if i < nkt:
                            ser.scalar(lambda e, p=p, pi=pi: e.activation(
                                pa3[pi][:], psS3[pi][:], AF.Exp, bias=bias_sb[:, p:p + 1],
                                scale=0.125), wait=mk_mm)
                            d = i - 4 * j
                            if d >= 0:
                                mkm = ser.mark()
                                ser.vector(lambda e, d=d, pi=pi: e.tensor_tensor(
                                    pa3[pi].rearrange("p (h q) -> p h q", h=2),
                                    pa3[pi].rearrange("p (h q) -> p h q", h=2),
                                    mask[:, d, :][:, None, :].broadcast_to([128, 2, 512]),
                                    ALU.mult), wait=mkm)
                            m_expd[i] = ser.mark()
                    prev_tail = ser.mark()
                    for half, psO in ((0, psO_A), (1, psO_B)):
                        h = 2 * p + half
                        ser.vector(lambda e, psO=psO: e.reciprocal(rec[:], psO[64:65, :]))
                        ser.tensor(lambda e: e.matmul(psR[0:64, :], onesf[:, 0:64], rec[:], start=True, stop=True))
                        ser.scalar(lambda e, psO=psO: e.copy(araw[:], psO[0:64, :]))
                        ser.vector(lambda e: e.tensor_tensor(asb[:], araw[:], psR[0:64, :], ALU.mult))
                        ser.dma(lambda e, h=h, jsl=jsl: e.dma_start(
                            out=s_att[64 * h:64 * h + 64, jsl], in_=asb[:]))
            ser.emit_block(nc, sems, acc)

        with ExitStack() as ctx:
            sb = lambda name, shape, dt=BF16: ctx.enter_context(nc.sbuf_tensor(name, shape, dt))
            ps = lambda name, shape, dt=F32: ctx.enter_context(nc.psum_tensor(name, shape, dt))
            ser = Ser()
            ident = sb("f4_id", [128, 128])
            stage = sb("f4_stage", [128, 8, D])
            wo_sb = sb("f4_wo", [128, 8, D])
            woT = sb("f4_woT", [128, 8, D])
            uoT = sb("f4_uoT", [128, 8, D])
            bo_r = sb("f4_bo", [1, D])
            bof_r = sb("f4_bof", [1, D])
            ones = sb("f4_ones", [1, 512])
            attf_t = sb("f4_attf", [128, 8, 128])
            cf_t = sb("f4_cf", [128, 8, 128])
            x_sb = sb("f4_x", [128, D], F32)
            xn_sb = sb("f4_xn", [128, D])
            xnf = sb("f4_xnf", [128, 8, 128])
            h_sb = sb("f4_h", [128, D], F32)
            x1_sb = sb("f4_x1", [128, D], F32)
            xn1_sb = sb("f4_xn1", [128, D], F32)
            sq = sb("f4_sq", [128, D], F32)
            st = sb("f4_st", [128, 8], F32)
            psO = ps("f4_psO", [128, 512])
            psH = ps("f4_psH", [128, 512])
            psT = [ps("f4_psT0", [128, 128], BF16), ps("f4_psT1", [128, 128], BF16)]

            ser.dmag(lambda e: e.dma_start(out=ident[:], in_=id_d[:]))
            ser.dmag(lambda e: e.dma_start(out=wo_sb[:], in_=Wo_d.rearrange("(c p) n -> p c n", p=128)))
            ser.dmag(lambda e: e.dma_start(out=bo_r[:], in_=bo_d[:]))
            ser.dmag(lambda e: e.dma_start(out=bof_r[:], in_=bof_d[:]))
            emit_load_wT(ser, nc, wof_d, stage, woT, ident, psT)
            emit_load_wT(ser, nc, uof_d, stage, uoT, ident, psT)
            ser.vector(lambda e: e.memset(ones[:], 1.0))
            tiles = {"sq": sq, "st": st}
            for i in range(NTB):
                r = slice(128 * i, 128 * i + 128)
                tsl = slice(128 * i, 128 * i + 128)
                ser.dma(lambda e, r=r: e.dma_start(out=x_sb[:], in_=xb_d[r, :]))
                ln_tile(ser, None, x_sb, xn_sb, tiles)
                ser.dma(lambda e, tsl=tsl: e.dma_start(
                    out=attf_t[:], in_=s_att.rearrange("(c p) t -> p c t", p=128)[:, :, tsl]))
                ser.dma(lambda e, tsl=tsl: e.dma_start(
                    out=cf_t[:], in_=s_cf.rearrange("(c p) t -> p c t", p=128)[:, :, tsl]))
                w0t = ser.mark()
                emit_tr(ser, [(xn_sb[:, 128 * c:128 * c + 128], xnf[:, c, :])
                              for c in range(8)], psT, ident, w0=w0t)
                for nb in range(2):
                    nsl = slice(512 * nb, 512 * nb + 512)
                    w0 = ser.mark()
                    for c in range(8):
                        ser.tensor(lambda e, c=c, nsl=nsl: e.matmul(
                            psO[:], attf_t[:, c, :], wo_sb[:, c, nsl], start=(c == 0), stop=False), wait=w0)
                    ser.tensor(lambda e, nsl=nsl: e.matmul(
                        psO[:], ones[:, 0:128], bo_r[:, nsl], start=False, stop=True), wait=w0)
                    for c in range(8):
                        ser.tensor(lambda e, c=c, nsl=nsl: e.matmul(
                            psH[:], xnf[:, c, :], woT[:, c, nsl], start=(c == 0), stop=False), wait=w0)
                    for c in range(8):
                        ser.tensor(lambda e, c=c, nsl=nsl: e.matmul(
                            psH[:], cf_t[:, c, :], uoT[:, c, nsl], start=False, stop=False), wait=w0)
                    ser.tensor(lambda e, nsl=nsl: e.matmul(
                        psH[:], ones[:, 0:128], bof_r[:, nsl], start=False, stop=True), wait=w0)
                    ser.scalar(lambda e, nsl=nsl: e.activation(h_sb[:, nsl], psH[:], AF.Tanh))
                    ser.vector(lambda e, nsl=nsl: e.tensor_tensor(
                        x1_sb[:, nsl], x_sb[:, nsl], h_sb[:, nsl], ALU.add))
                    ser.vector(lambda e, nsl=nsl: e.tensor_tensor(
                        x1_sb[:, nsl], x1_sb[:, nsl], psO[:], ALU.add))
                ser.dma(lambda e, r=r: e.dma_start(out=s_x1[r, :], in_=x1_sb[:]))
                ln_tile(ser, None, x1_sb, xn1_sb, tiles)
                ser.dma(lambda e, r=r: e.dma_start(out=s_xn1[r, :], in_=xn1_sb[:]))
            ser.emit_block(nc, sems, acc)

        def phase_subpre(w, z_scr, tag):
            with ExitStack() as ctx:
                sb = lambda name, shape, dt=BF16: ctx.enter_context(nc.sbuf_tensor(tag + name, shape, dt))
                ps = lambda name, shape, dt=F32: ctx.enter_context(nc.psum_tensor(tag + name, shape, dt))
                ser = Ser()
                ident = sb("sp_id", [128, 128])
                wqs = sb("sp_wq", [128, 8, D])
                wks = sb("sp_wk", [128, 8, D])
                wvs = sb("sp_wv", [128, 8, D])
                bqs = sb("sp_bq", [1, D])
                bks = sb("sp_bk", [1, D])
                bvs = sb("sp_bv", [1, D])
                tbl = sb("sp_tbl", [128, T])
                gb = sb("sp_gb", [128, D], F32)
                bb = sb("sp_bb", [128, D], F32)
                gbounce = sb("sp_gb1", [1, D], F32)
                onesf = sb("sp_onesf", [1, 128], F32)
                ones = sb("sp_ones", [1, 512])
                z_sb = sb("sp_z", [128, D], F32)
                zl1 = sb("sp_zl1", [128, D], F32)
                zl_sb = sb("sp_zl", [128, D])
                zlf = sb("sp_zlf", [128, 8, T])
                sq = sb("sp_sq", [128, D], F32)
                st = sb("sp_st", [128, 8], F32)
                qraw = sb("sp_qraw", [128, 512])
                stage = sb("sp_stage", [128, T])
                v_sb = sb("sp_v", [128, D])
                psA = ps("sp_psA", [128, 512])
                psT = [ps("sp_psT0", [128, 128], BF16), ps("sp_psT1", [128, 128], BF16)]

                ser.dmag(lambda e: e.dma_start(out=ident[:], in_=id_d[:]))
                for ws, wd_ in ((wqs, w["wq"]), (wks, w["wk"]), (wvs, w["wv"])):
                    ser.dmag(lambda e, ws=ws, wd_=wd_: e.dma_start(
                        out=ws[:], in_=wd_.rearrange("(c p) n -> p c n", p=128)))
                for bs, bd_ in ((bqs, w["bq"]), (bks, w["bk"]), (bvs, w["bv"])):
                    ser.dmag(lambda e, bs=bs, bd_=bd_: e.dma_start(out=bs[:], in_=bd_[:]))
                ser.dmag(lambda e: e.dma_start(out=tbl[:], in_=tb_d[:]))
                ser.vector(lambda e: e.memset(onesf[:], 1.0))
                ser.vector(lambda e: e.memset(ones[:], 1.0))
                for row_d, dst in ((w["g1"], gb), (w["b1"], bb)):
                    ser.dma(lambda e, row_d=row_d: e.dma_start(out=gbounce[:], in_=row_d[:]))
                    for nb in range(2):
                        nsl = slice(512 * nb, 512 * nb + 512)
                        ser.tensor(lambda e, nsl=nsl: e.matmul(
                            psA[:], onesf[:, :], gbounce[:, nsl], start=True, stop=True))
                        ser.vector(lambda e, dst=dst, nsl=nsl: e.tensor_copy(dst[:, nsl], psA[:]))
                tiles = {"sq": sq, "st": st}
                for i in range(NTB):
                    r = slice(128 * i, 128 * i + 128)
                    ser.dma(lambda e, r=r: e.dma_start(out=z_sb[:], in_=z_scr[r, :]))
                    ln_tile(ser, None, z_sb, zl1, tiles)
                    ser.vector(lambda e: e.tensor_tensor(zl1[:], zl1[:], gb[:], ALU.mult))
                    ser.vector(lambda e: e.tensor_tensor(zl_sb[:], zl1[:], bb[:], ALU.add))
                    w0t = ser.mark()
                    emit_tr(ser, [(zl_sb[:, 128 * c:128 * c + 128], zlf[:, c, 128 * i:128 * i + 128])
                                  for c in range(8)], psT, ident, w0=w0t)
                for (wsb, bsb, out_d) in ((wqs, bqs, s_qf), (wks, bks, s_kf)):
                    for m in range(8):
                        msl = slice(128 * m, 128 * m + 128)
                        for n in range(T // 512):
                            nsl = slice(512 * n, 512 * n + 512)
                            w0 = ser.mark()
                            for c in range(8):
                                ser.tensor(lambda e, c=c, msl=msl, nsl=nsl, wsb=wsb: e.matmul(
                                    psA[:], wsb[:, c, msl], zlf[:, c, nsl], start=(c == 0), stop=False), wait=w0)
                            ser.tensor(lambda e, msl=msl, bsb=bsb: e.matmul(
                                psA[:], bsb[:, msl], ones[:], start=False, stop=True), wait=w0)
                            ser.vector(lambda e: e.tensor_copy(qraw[:], psA[:]))
                            ser.vector(lambda e, nsl=nsl: e.tensor_tensor(
                                stage[:, nsl], qraw[:], tbl[:, nsl], ALU.mult))
                        ser.dma(lambda e, msl=msl, out_d=out_d: e.dma_start(out=out_d[msl, :], in_=stage[:]))
                for i in range(NTB):
                    r = slice(128 * i, 128 * i + 128)
                    for nb in range(2):
                        nsl = slice(512 * nb, 512 * nb + 512)
                        w0 = ser.mark()
                        for c in range(8):
                            ser.tensor(lambda e, c=c, i=i, nsl=nsl: e.matmul(
                                psA[:], zlf[:, c, 128 * i:128 * i + 128], wvs[:, c, nsl],
                                start=(c == 0), stop=False), wait=w0)
                        ser.tensor(lambda e, nsl=nsl: e.matmul(
                            psA[:], ones[:, 0:128], bvs[:, nsl], start=False, stop=True), wait=w0)
                        ser.vector(lambda e, nsl=nsl: e.tensor_copy(v_sb[:, nsl], psA[:]))
                    ser.dma(lambda e, r=r: e.dma_start(out=s_vt[r, :], in_=v_sb[:]))
                ser.emit_block(nc, sems, acc)

        def phase_subatt(tag):
            with ExitStack() as ctx:
                sb = lambda name, shape, dt=BF16: ctx.enter_context(nc.sbuf_tensor(tag + name, shape, dt))
                ps = lambda name, shape, dt=F32: ctx.enter_context(nc.psum_tensor(tag + name, shape, dt))
                ser = Ser()
                q_sb = sb("sa_q", [128, 8, T])
                k_sb = sb("sa_k", [128, 8, T])
                v_p = sb("sa_vp", [128, 32, 128])
                vaug = sb("sa_vaug", [128, 32, 130])
                sq = sb("sa_sq", [128, T])
                pa3 = [sb("sa_pa0", [128, 1024]), sb("sa_pa1", [128, 1024])]
                araw = sb("sa_araw", [64, 512], F32)
                asb = sb("sa_asb", [64, 512])
                rec = sb("sa_rec", [1, 512], F32)
                onesb = sb("sa_onesb", [128, 1])
                onesf = sb("sa_onesf", [1, 128], F32)
                hsc = sb("sa_hsc", [1, 16], F32)
                ksc = sb("sa_ksc", [1, 16], F32)
                bias_sb = sb("sa_bias", [128, 8], F32)
                psS3 = [ps("sa_psS0", [128, 1024]), ps("sa_psS1", [128, 1024])]
                psO_A = ps("sa_psOA", [128, 512])
                psO_B = ps("sa_psOB", [128, 512])
                psR = ps("sa_psR", [128, 512])
                psQ = ps("sa_psQ", [128, 512])

                ser.dma(lambda e: e.dma_start(out=q_sb[:], in_=s_qf.rearrange("(c p) t -> p c t", p=128)))
                ser.dma(lambda e: e.dma_start(out=k_sb[:], in_=s_kf.rearrange("(c p) t -> p c t", p=128)))
                ser.vector(lambda e: e.memset(onesb[:], 1.0))
                ser.vector(lambda e: e.memset(onesf[:], 1.0))
                ser.vector(lambda e: e.memset(hsc[:], 0.0))
                ser.vector(lambda e: e.memset(ksc[:], 0.0))
                for src, accv in ((q_sb, hsc), (k_sb, ksc)):
                    for c in range(8):
                        ser.scalar(lambda e, c=c, src=src: e.activation(sq[:], src[:, c, :], AF.Square))
                        for n in range(T // 512):
                            nsl = slice(512 * n, 512 * n + 512)
                            ser.tensor(lambda e, nsl=nsl: e.matmul(
                                psQ[0:1, :], onesb[:, :], sq[:, nsl], start=True, stop=True))
                            ser.vector(lambda e: e.reduce_max(rec[0:1, 0:1], psQ[0:1, :], AX.X))
                            ser.vector(lambda e, c=c, accv=accv: e.tensor_tensor(
                                accv[0:1, c:c + 1], accv[0:1, c:c + 1], rec[0:1, 0:1], ALU.max))
                ser.scalar(lambda e: e.activation(hsc[0:1, 8:16], hsc[0:1, 0:8], AF.Sqrt))
                ser.scalar(lambda e: e.activation(ksc[0:1, 8:16], ksc[0:1, 0:8], AF.Sqrt))
                for p in range(8):
                    ser.vector(lambda e, p=p: e.tensor_tensor(
                        hsc[0:1, p:p + 1], hsc[0:1, 8 + p:9 + p], ksc[0:1, 8 + p:9 + p], ALU.mult))
                    ser.vector(lambda e, p=p: e.tensor_scalar_mul(hsc[0:1, p:p + 1], hsc[0:1, p:p + 1], -0.125))
                    ser.tensor(lambda e, p=p: e.matmul(
                        psR[:, 0:1], onesf[:, :], hsc[0:1, p:p + 1], start=True, stop=True))
                    ser.vector(lambda e, p=p: e.tensor_copy(bias_sb[:, p:p + 1], psR[:, 0:1]))
                prev_tail = None
                for p in range(8):
                    ser.dma(lambda e, p=p: e.dma_start(
                        out=v_p[:], in_=s_vt.rearrange("(kt pp) d -> pp kt d", pp=128)[:, :, 128 * p:128 * p + 128]))
                    ser.vector(lambda e: e.memset(vaug[:], 1.0))
                    for kt in range(32):
                        ser.vector(lambda e, kt=kt: e.tensor_copy(vaug[:, kt, 0:64], v_p[:, kt, 0:64]))
                        ser.vector(lambda e, kt=kt: e.tensor_copy(vaug[:, kt, 65:129], v_p[:, kt, 64:128]))
                    for j in range(T // 512):
                        jsl = slice(512 * j, 512 * j + 512)
                        m_expd = {}
                        for i in range(33):
                            pi = i % 2
                            if i < 32:
                                isl = slice(128 * i, 128 * i + 128)
                                w_war = m_expd.get(i - 2, prev_tail)
                                ser.tensor(lambda e, p=p, isl=isl, jsl=jsl, pi=pi: e.matmul(
                                    psS3[pi][:, 0:512], k_sb[0:64, p, isl], q_sb[0:64, p, jsl],
                                    start=True, stop=True, tile_position=(0, 0)), wait=w_war)
                                ser.tensor(lambda e, p=p, isl=isl, jsl=jsl, pi=pi: e.matmul(
                                    psS3[pi][:, 512:1024], k_sb[64:128, p, isl], q_sb[64:128, p, jsl],
                                    start=True, stop=True, tile_position=(64, 0)), wait=w_war)
                                mk_mm = ser.mark()
                            if i >= 1:
                                io = i - 1
                                pio = io % 2
                                lasto = io == 31
                                wav = m_expd[io]
                                ser.tensor(lambda e, io=io, lasto=lasto, pio=pio: e.matmul(
                                    psO_A[0:65, :], vaug[:, io, 0:65], pa3[pio][:, 0:512],
                                    start=(io == 0), stop=lasto), wait=wav)
                                ser.tensor(lambda e, io=io, lasto=lasto, pio=pio: e.matmul(
                                    psO_B[0:65, :], vaug[:, io, 65:130], pa3[pio][:, 512:1024],
                                    start=(io == 0), stop=lasto), wait=wav)
                            if i < 32:
                                ser.scalar(lambda e, p=p, pi=pi: e.activation(
                                    pa3[pi][:], psS3[pi][:], AF.Exp, bias=bias_sb[:, p:p + 1],
                                    scale=0.125), wait=mk_mm)
                                m_expd[i] = ser.mark()
                        prev_tail = ser.mark()
                        for half, psO in ((0, psO_A), (1, psO_B)):
                            h = 2 * p + half
                            ser.vector(lambda e, psO=psO: e.reciprocal(rec[:], psO[64:65, :]))
                            ser.tensor(lambda e: e.matmul(psR[0:64, :], onesf[:, 0:64], rec[:], start=True, stop=True))
                            ser.scalar(lambda e, psO=psO: e.copy(araw[:], psO[0:64, :]))
                            ser.vector(lambda e: e.tensor_tensor(asb[:], araw[:], psR[0:64, :], ALU.mult))
                            ser.dma(lambda e, h=h, jsl=jsl: e.dma_start(
                                out=s_att[64 * h:64 * h + 64, jsl], in_=asb[:]))
                ser.emit_block(nc, sems, acc)

        def phase_subpost(w, z_scr, zout_scr, tag):
            with ExitStack() as ctx:
                sb = lambda name, shape, dt=BF16: ctx.enter_context(nc.sbuf_tensor(tag + name, shape, dt))
                ps = lambda name, shape, dt=F32: ctx.enter_context(nc.psum_tensor(tag + name, shape, dt))
                ser = Ser()
                ident = sb("po_id", [128, 128])
                wo_sb = sb("po_wo", [128, 8, D])
                w1_sb = sb("po_w1", [128, 8, 4 * D])
                w2buf = sb("po_w2", [128, 32, 512])
                b1_r = sb("po_b1", [1, 4 * D])
                g2b = sb("po_g2", [128, D], F32)
                b2b = sb("po_b2", [128, D], F32)
                gbounce = sb("po_gb", [1, D], F32)
                onesf = sb("po_onesf", [1, 128], F32)
                ones = sb("po_ones", [1, 512])
                attf_t = sb("po_attf", [128, 8, 128])
                z_sb = sb("po_z", [128, D], F32)
                z1blk = sb("po_z1b", [128, 4, D], F32)
                zl2a = sb("po_zl2a", [128, D], F32)
                zl2b = sb("po_zl2b", [128, D])
                zlf = sb("po_zlf", [128, 8, 512])
                act_all = sb("po_act", [128, 32, 512])
                z2_sb = sb("po_z2", [128, 512], F32)
                st = sb("po_st", [128, 8], F32)
                sg2 = [sb("po_sg0", [128, 512]), sb("po_sg1", [128, 512])]
                psO = ps("po_psO", [128, 512])
                psA2 = [ps("po_psA0", [128, 512]), ps("po_psA1", [128, 512])]
                psZ = ps("po_psZ", [128, 512])
                psT = [ps("po_psT0", [128, 128], BF16), ps("po_psT1", [128, 128], BF16)]

                ser.dmag(lambda e: e.dma_start(out=ident[:], in_=id_d[:]))
                ser.dmag(lambda e: e.dma_start(out=wo_sb[:], in_=w["wo"].rearrange("(c p) n -> p c n", p=128)))
                ser.dmag(lambda e: e.dma_start(out=w1_sb[:], in_=w["w1"].rearrange("(c p) n -> p c n", p=128)))
                ser.dmag(lambda e: e.dma_start(out=b1_r[:], in_=w["b1f"][:]))
                ser.vector(lambda e: e.memset(onesf[:], 1.0))
                ser.vector(lambda e: e.memset(ones[:], 1.0))
                for row_d, dst in ((w["g2"], g2b), (w["b2"], b2b)):
                    ser.dma(lambda e, row_d=row_d: e.dma_start(out=gbounce[:], in_=row_d[:]))
                    for nb in range(2):
                        nsl = slice(512 * nb, 512 * nb + 512)
                        ser.tensor(lambda e, nsl=nsl: e.matmul(
                            psO[:], onesf[:, :], gbounce[:, nsl], start=True, stop=True))
                        ser.vector(lambda e, dst=dst, nsl=nsl: e.tensor_copy(dst[:, nsl], psO[:]))
                tiles = {"sq": zl2a, "st": st}
                for blk in range(NBB):
                    for tt in range(4):
                        i = 4 * blk + tt
                        r = slice(128 * i, 128 * i + 128)
                        tsl = slice(128 * i, 128 * i + 128)
                        ser.dma(lambda e, r=r: e.dma_start(out=z_sb[:], in_=z_scr[r, :]))
                        ser.dma(lambda e, tsl=tsl: e.dma_start(
                            out=attf_t[:], in_=s_att.rearrange("(c p) t -> p c t", p=128)[:, :, tsl]))
                        for nb in range(2):
                            nsl = slice(512 * nb, 512 * nb + 512)
                            w0 = ser.mark()
                            for c in range(8):
                                ser.tensor(lambda e, c=c, nsl=nsl: e.matmul(
                                    psO[:], attf_t[:, c, :], wo_sb[:, c, nsl], start=(c == 0), stop=(c == 7)), wait=w0)
                            ser.vector(lambda e, nsl=nsl, tt=tt: e.tensor_tensor(
                                z1blk[:, tt, nsl], z_sb[:, nsl], psO[:], ALU.add))
                        ln_tile(ser, None, z1blk[:, tt, :], zl2a, tiles)
                        ser.vector(lambda e: e.tensor_tensor(zl2a[:], zl2a[:], g2b[:], ALU.mult))
                        ser.vector(lambda e: e.tensor_tensor(zl2b[:], zl2a[:], b2b[:], ALU.add))
                        w0t = ser.mark()
                        emit_tr(ser, [(zl2b[:, 128 * c:128 * c + 128], zlf[:, c, 128 * tt:128 * tt + 128])
                                      for c in range(8)], psT, ident, w0=w0t)
                    mdone = {}
                    for mh in range(32):
                        pi = mh % 2
                        w0 = mdone.get(mh - 2)
                        for c in range(8):
                            ser.tensor(lambda e, c=c, mh=mh, pi=pi: e.matmul(
                                psA2[pi][:], w1_sb[:, c, 128 * mh:128 * mh + 128], zlf[:, c, :],
                                start=(c == 0), stop=False), wait=w0)
                        ser.tensor(lambda e, mh=mh, pi=pi: e.matmul(
                            psA2[pi][:], b1_r[:, 128 * mh:128 * mh + 128], ones[:], start=False, stop=True), wait=w0)
                        mk = ser.mark()
                        ser.scalar(lambda e, pi=pi: e.activation(
                            sg2[pi][:], psA2[pi][:], AF.Sigmoid, scale=1.702), wait=mk)
                        mk2 = ser.mark()
                        ser.vector(lambda e, mh=mh, pi=pi: e.tensor_tensor(
                            act_all[:, mh, :], sg2[pi][:], psA2[pi][:], ALU.mult), wait=mk2)
                        mdone[mh] = ser.mark()
                    for nb in range(2):
                        nsl = slice(512 * nb, 512 * nb + 512)
                        ser.dmag(lambda e, nsl=nsl: e.dma_start(
                            out=w2buf[:], in_=w["w2"].rearrange("(c p) n -> p c n", p=128)[:, :, nsl]))
                        for tt in range(4):
                            i = 4 * blk + tt
                            r = slice(128 * i, 128 * i + 128)
                            w0 = ser.mark()
                            for mh in range(32):
                                ser.tensor(lambda e, mh=mh, tt=tt: e.matmul(
                                    psZ[:], act_all[:, mh, 128 * tt:128 * tt + 128], w2buf[:, mh, :],
                                    start=(mh == 0), stop=(mh == 31)), wait=w0)
                            ser.vector(lambda e, tt=tt, nsl=nsl: e.tensor_tensor(
                                z2_sb[:], z1blk[:, tt, nsl], psZ[:], ALU.add))
                            ser.dma(lambda e, r=r, nsl=nsl: e.dma_start(
                                out=zout_scr[r, nsl], in_=z2_sb[:]))
                ser.emit_block(nc, sems, acc)

        phase_subpre(l1[0], s_xn1, "A")
        phase_subatt("A")
        phase_subpost(l1[0], s_xn1, s_z2, "A")
        phase_subpre(l1[1], s_z2, "B")
        phase_subatt("B")
        phase_subpost(l1[1], s_z2, s_z4, "B")

        with ExitStack() as ctx:
            sb = lambda name, shape, dt=BF16: ctx.enter_context(nc.sbuf_tensor(name, shape, dt))
            ps = lambda name, shape, dt=F32: ctx.enter_context(nc.psum_tensor(name, shape, dt))
            ser = Ser()
            SW = HI // 2
            ident = sb("f7_id", [128, 128])
            wgqT = sb("f7_wg", [128, 8, HI])
            wuqT = sb("f7_wu", [128, 8, HI])
            wdqT = sb("f7_wd", [128, 22, D])
            stage = sb("f7_stage", [128, SW], F32)
            wqc = sb("f7_wqc", [128, HI])
            scr = {"aw": sb("f7_aw", [128, SW], F32), "ind": sb("f7_ind", [128, SW]),
                   "acc": sb("f7_acc", [128, SW]), "sgn": sb("f7_sgn", [128, SW]),
                   "sblk": sb("f7_sblk", [128, SW // 16], F32), "six": sb("f7_six", [128, 1], F32),
                   "awb": sb("f7_awb", [128, SW]), "sb16": sb("f7_sb16", [128, SW // 16])}
            xq = sb("f7_xq", [128, D])
            xqf = sb("f7_xqf", [128, 8, 128])
            h_t = sb("f7_h", [128, HI], F32)
            hq = sb("f7_hq", [128, HI])
            hqf = sb("f7_hqf", [128, 22, 128])
            sgh = sb("f7_sgh", [128, 512])
            x1_sb = sb("f7_x1", [128, D], F32)
            z4_sb = sb("f7_z4", [128, D], F32)
            xn1_sb = sb("f7_xn1", [128, D], F32)
            out_sb = sb("f7_out", [128, D], F32)
            psG = ps("f7_psG", [128, 512])
            psU = ps("f7_psU", [128, 512])
            psD = ps("f7_psD", [128, 512])
            psT = [ps("f7_psT0", [128, 128], BF16), ps("f7_psT1", [128, 128], BF16)]

            ser.dmag(lambda e: e.dma_start(out=ident[:], in_=id_d[:]))
            for (w_d2, wTq, ncw, W) in ((wg_d, wgqT, 22, D), (wu_d, wuqT, 22, D), (wd_d, wdqT, 8, HI)):
                for co in range(ncw):
                    for a in range(0, W, SW):
                        w_ = min(SW, W - a)
                        ser.dma(lambda e, w_d2=w_d2, co=co, a=a, w_=w_: e.dma_start(
                            out=stage[:, 0:w_], in_=w_d2[128 * co:128 * co + 128, a:a + w_]))
                        emit_quant(ser, stage[:, 0:w_], wqc[:, a:a + w_], w_, scr)
                    w0t = ser.mark()
                    emit_tr(ser, [(wqc[:, 128 * ci:128 * ci + 128], wTq[:, ci, 128 * co:128 * co + 128])
                                  for ci in range(W // 128)], psT, ident, w0=w0t)
            gate_blocks = [(0, 512), (512, 512), (1024, 512), (1536, 512), (2048, 512), (2560, 256)]
            for i in range(NTB):
                r = slice(128 * i, 128 * i + 128)
                ser.dma(lambda e, r=r: e.dma_start(out=xn1_sb[:], in_=s_xn1[r, :]))
                ser.dma(lambda e, r=r: e.dma_start(out=x1_sb[:], in_=s_x1[r, :]))
                ser.dma(lambda e, r=r: e.dma_start(out=z4_sb[:], in_=s_z4[r, :]))
                emit_quant(ser, xn1_sb[:, :], xq[:, :], D, scr)
                w0t = ser.mark()
                emit_tr(ser, [(xq[:, 128 * c:128 * c + 128], xqf[:, c, :]) for c in range(8)],
                        psT, ident, w0=w0t)
                for o0, ow in gate_blocks:
                    osl = slice(o0, o0 + ow)
                    w0 = ser.mark()
                    for c in range(8):
                        ser.tensor(lambda e, c=c, osl=osl, ow=ow: e.matmul(
                            psG[:, 0:ow], xqf[:, c, :], wgqT[:, c, osl], start=(c == 0), stop=(c == 7)), wait=w0)
                    for c in range(8):
                        ser.tensor(lambda e, c=c, osl=osl, ow=ow: e.matmul(
                            psU[:, 0:ow], xqf[:, c, :], wuqT[:, c, osl], start=(c == 0), stop=(c == 7)), wait=w0)
                    ser.scalar(lambda e, ow=ow: e.activation(sgh[:, 0:ow], psG[:, 0:ow], AF.Silu))
                    ser.vector(lambda e, osl=osl, ow=ow: e.tensor_tensor(
                        h_t[:, osl], sgh[:, 0:ow], psU[:, 0:ow], ALU.mult))
                emit_quant(ser, h_t[:, :], hq[:, :], HI, scr)
                w0t = ser.mark()
                emit_tr(ser, [(hq[:, 128 * c:128 * c + 128], hqf[:, c, :]) for c in range(22)],
                        psT, ident, w0=w0t)
                for nb in range(2):
                    nsl = slice(512 * nb, 512 * nb + 512)
                    w0 = ser.mark()
                    for c in range(22):
                        ser.tensor(lambda e, c=c, nsl=nsl: e.matmul(
                            psD[:], hqf[:, c, :], wdqT[:, c, nsl], start=(c == 0), stop=(c == 21)), wait=w0)
                    ser.vector(lambda e, nsl=nsl: e.tensor_tensor(
                        out_sb[:, nsl], x1_sb[:, nsl], z4_sb[:, nsl], ALU.add))
                    ser.vector(lambda e, nsl=nsl: e.tensor_tensor(
                        out_sb[:, nsl], out_sb[:, nsl], psD[:], ALU.add))
                ser.dma(lambda e, r=r: e.dma_start(out=out_o[r, :], in_=out_sb[:]))
            ser.emit_block(nc, sems, acc, final=True)
    return nc


def fused_in_maps(inputs):
    x = np.asarray(inputs["L0_attn_x"], np.float32).reshape(ROWS, D)
    cc0, ss0 = _rope0_tables()
    tbl1 = _rope1_table()
    masks = np.ascontiguousarray(_diag_masks().transpose(1, 0, 2))
    g = lambda n: np.asarray(inputs[n], np.float32)
    in_maps = []
    for c in range(NCORE):
        b = c // 2
        m = {
            "x_full": x,
            "x_batch": np.ascontiguousarray(x[b * T:(b + 1) * T]),
            "Wq": g("L0_attn_Wq"), "bq": g("L0_attn_bq").reshape(1, D),
            "Wk": g("L0_attn_Wk"), "bk": g("L0_attn_bk").reshape(1, 256),
            "Wv": g("L0_attn_Wv"), "bv": g("L0_attn_bv").reshape(1, 256),
            "Wo": g("L0_attn_Wo"), "bo": g("L0_attn_bo").reshape(1, D),
            "in_c": np.ascontiguousarray(g("L0_ffn_in_c").reshape(8, 128).T),
            "wof": g("L0_ffn_wo"), "uof": g("L0_ffn_uo"), "bof": g("L0_ffn_bo").reshape(1, D),
            "w_gate": g("L1_ffn_w_gate"), "w_up": g("L1_ffn_w_up"), "w_down": g("L1_ffn_w_down"),
            "cc0": np.ascontiguousarray(cc0[:, 0:T]), "ss0": np.ascontiguousarray(ss0[:, 0:T]),
            "tbl1": np.ascontiguousarray(tbl1[:, 0:T]),
            "masks": masks, "ident": IDENT,
        }
        for nm in ("wf", "uf", "wi", "ui", "wc", "uc"):
            m[nm] = g("L0_ffn_" + nm)
        for nm in ("bf", "bi", "bc"):
            m[nm] = g("L0_ffn_" + nm).reshape(1, D)
        for li in range(2):
            m[f"g1_{li}"] = g("L1_attn_ln1_g")[li].reshape(1, D)
            m[f"b1r_{li}"] = g("L1_attn_ln1_b")[li].reshape(1, D)
            m[f"wq{li}"] = g("L1_attn_wq")[li]
            m[f"wk{li}"] = g("L1_attn_wk")[li]
            m[f"wv{li}"] = g("L1_attn_wv")[li]
            m[f"bqr{li}"] = g("L1_attn_bq")[li].reshape(1, D)
            m[f"bkr{li}"] = g("L1_attn_bk")[li].reshape(1, D)
            m[f"bvr{li}"] = g("L1_attn_bv")[li].reshape(1, D)
            m[f"wo{li}"] = g("L1_attn_wo")[li]
            m[f"g2_{li}"] = g("L1_attn_ln2_g")[li].reshape(1, D)
            m[f"b2_{li}"] = g("L1_attn_ln2_b")[li].reshape(1, D)
            m[f"w1_{li}"] = g("L1_attn_w1")[li]
            m[f"b1f_{li}"] = g("L1_attn_b1")[li].reshape(1, 4 * D)
            m[f"w2_{li}"] = g("L1_attn_w2")[li]
        in_maps.append(m)
    return in_maps


def kernel_fused(**inputs):
    nc = build_fused()
    in_maps = fused_in_maps(inputs)
    res = _run(nc, in_maps)
    out = np.empty((ROWS, D), np.float32)
    for b in range(B):
        out[b * T:(b + 1) * T] = np.asarray(res[2 * b]["out_full"])
    return out.reshape(B, T, D)



def kernel(**inputs):
    return kernel_fused(**inputs)


# version 13
# speedup vs baseline: 1.0276x; 1.0276x over previous
import numpy as np
import ml_dtypes
from contextlib import ExitStack
import concourse.bass as bass
import concourse.mybir as mybir
from concourse.bass_utils import run_bass_kernel_spmd

F32 = mybir.dt.float32
BF16 = mybir.dt.bfloat16
AF = mybir.ActivationFunctionType
ALU = mybir.AluOpType
AX = mybir.AxisListType
BF = ml_dtypes.bfloat16

B, T, D = 4, 4096, 1024
ROWS = B * T
NCORE = 8
RPC = ROWS // NCORE
NH0, NG0, DH = 16, 4, 64
EPS = 1e-5

SWAPMASK = [i ^ 1 for i in range(32)]



class Ser:

    def __init__(self):
        self.steps = []

    def mark(self):
        return len(self.steps)

    def s(self, engine, fn, inc=1, wait=None):
        self.steps.append((engine, fn, inc, wait))
        return len(self.steps)

    def dma(self, fn, wait=None):
        self.steps.append(("sync", fn, 16, wait))
        return len(self.steps)

    def dmag(self, fn, wait=None):
        self.steps.append(("gpsimd", fn, 16, wait))
        return len(self.steps)

    def tensor(self, fn, wait=None):
        return self.s("tensor", fn, wait=wait)

    def vector(self, fn, wait=None):
        return self.s("vector", fn, wait=wait)

    def scalar(self, fn, wait=None):
        return self.s("scalar", fn, wait=wait)

    def gpsimd(self, fn, wait=None):
        return self.s("gpsimd", fn, wait=wait)

    KEYS = ("t", "v", "s", "gc", "gd", "hd")

    @staticmethod
    def _sem_of(e, inc):
        if e == "tensor":
            return "t"
        if e == "vector":
            return "v"
        if e == "scalar":
            return "s"
        if e == "gpsimd":
            return "gd" if inc == 16 else "gc"
        return "hd"

    def emit_block(self, nc, sems, acc0, final=False):
        n = len(self.steps)
        cum = {k: [0] * (n + 1) for k in self.KEYS}
        for i, (e, fn, inc, wait) in enumerate(self.steps):
            for k in self.KEYS:
                cum[k][i + 1] = cum[k][i]
            cum[self._sem_of(e, inc)][i + 1] += inc
        with nc.Block() as block:
            def mk(engine_name):
                def f(eng):
                    waited = {k: -1 for k in self.KEYS}
                    for i, (e, fn, inc, wait) in enumerate(self.steps):
                        if e != engine_name:
                            continue
                        W = i if wait is None else wait
                        own = self._sem_of(e, inc)
                        for k in self.KEYS:
                            if k == "t" and own == "t":
                                continue
                            tgt = acc0[k] + cum[k][W]
                            if tgt > 0 and tgt > waited[k]:
                                eng.wait_ge(sems[k], tgt)
                                waited[k] = tgt
                        fn(eng).then_inc(sems[own], inc)
                    if final and engine_name == "sync":
                        for k in self.KEYS:
                            tgt = acc0[k] + cum[k][n]
                            if tgt > 0 and tgt > waited[k]:
                                eng.wait_ge(sems[k], tgt)
                return f
            block.sync(mk("sync"))
            block.gpsimd(mk("gpsimd"))
            block.vector(mk("vector"))
            block.scalar(mk("scalar"))
            block.tensor(mk("tensor"))
        for k in self.KEYS:
            acc0[k] += cum[k][n]
        self.steps = []

    def emit(self, nc):
        with ExitStack() as sctx:
            sems = {k: sctx.enter_context(nc.semaphore("ser_" + k)) for k in self.KEYS}
            acc = {k: 0 for k in self.KEYS}
            self.emit_block(nc, sems, acc, final=True)


def _run(nc, in_maps):
    res = run_bass_kernel_spmd(nc, in_maps, core_ids=list(range(NCORE)))
    return res.results



def _rope0_tables():
    p = np.arange(128)
    j = (p % 64) // 2
    freq = np.exp(-(2.0 * j / 64.0) * np.log(10000.0))
    t = np.tile(np.arange(T, dtype=np.float64), B)
    ang = freq[:, None] * t[None, :]
    cc = np.cos(ang)
    ss = np.sin(ang)
    ss[p % 2 == 0] *= -1.0
    return cc.astype(np.float32), ss.astype(np.float32)


def _rope1_table():
    import math
    dim_exp = 16
    freqs = 2.0 ** np.linspace(0.0, math.log2(5.0), dim_exp)
    coords = 1.0 + np.arange(T, dtype=np.float64)
    radians = coords[:, None] * freqs[None, :] * math.pi
    c = np.cos(radians)
    s = np.sin(radians)
    tbl = np.ones((128, T), np.float64)
    for pp in range(128):
        dd = pp % 64
        if dd < 32:
            sign = 1.0 if dd % 2 == 1 else -1.0
            tbl[pp] = s[:, dd % 16] + sign * c[:, dd % 16]
    return np.ascontiguousarray(np.tile(tbl, (1, B))).astype(np.float32)


def _diag_masks():
    d = np.arange(4)[:, None, None]
    pk = np.arange(128)[None, :, None]
    fq = np.arange(512)[None, None, :]
    return ((128 * d + pk) <= fq).astype(np.float32)


IDENT = np.eye(128, dtype=np.float32)
B2 = np.zeros((128, 2), np.float32)
B2[:64, 0] = 1.0
B2[64:, 1] = 1.0



def ln_tile(ser, sc, x_sb, out_sb, tiles, gain_col=None, bias_col=None):
    st = tiles["st"]
    mean, var, sd, rstd, m2 = (st[:, i:i + 1] for i in range(5))
    sq = tiles["sq"][:, :]
    x_sb = x_sb[:, :]
    out_sb = out_sb[:, :]
    ser.vector(lambda e, o=mean, i_=x_sb: e.reduce_sum(o, i_, AX.X))
    ser.vector(lambda e, o=mean: e.tensor_scalar_mul(o, o, 1.0 / D))
    ser.scalar(lambda e, o=sq, i_=x_sb, a=var: e.activation(o, i_, AF.Square, accum_out=a))
    ser.vector(lambda e, o=var: e.tensor_scalar_mul(o, o, 1.0 / D))
    ser.vector(lambda e, o=m2, i_=mean: e.tensor_tensor(o, i_, i_, ALU.mult))
    ser.vector(lambda e, o=var, i_=m2: e.tensor_tensor(o, o, i_, ALU.subtract))
    ser.vector(lambda e, o=var: e.tensor_scalar_add(o, o, EPS))
    ser.scalar(lambda e, o=sd, i_=var: e.activation(o, i_, AF.Sqrt))
    ser.vector(lambda e, o=rstd, i_=sd: e.reciprocal(o, i_))
    ser.vector(lambda e, o=out_sb, i_=x_sb, s1=mean, s2=rstd: e.tensor_scalar(
        out=o, in0=i_, scalar1=s1, scalar2=s2, op0=ALU.subtract, op1=ALU.mult))



def build_p1():
    nc = bass.Bass("TRN2", target_bir_lowering=False)
    xd = nc.dram_tensor("x_rows", [RPC, D], F32, kind="ExternalInput").ap()
    wq_d = nc.dram_tensor("Wq", [D, D], F32, kind="ExternalInput").ap()
    bq_d = nc.dram_tensor("bq", [1, D], F32, kind="ExternalInput").ap()
    wk_d = nc.dram_tensor("Wk", [D, 256], F32, kind="ExternalInput").ap()
    bk_d = nc.dram_tensor("bk", [1, 256], F32, kind="ExternalInput").ap()
    wv_d = nc.dram_tensor("Wv", [D, 256], F32, kind="ExternalInput").ap()
    bv_d = nc.dram_tensor("bv", [1, 256], F32, kind="ExternalInput").ap()
    cc_d = nc.dram_tensor("cc0", [128, RPC], F32, kind="ExternalInput").ap()
    ss_d = nc.dram_tensor("ss0", [128, RPC], F32, kind="ExternalInput").ap()
    id_d = nc.dram_tensor("ident", [128, 128], F32, kind="ExternalInput").ap()

    xn_o = nc.dram_tensor("xn", [RPC, D], F32, kind="ExternalOutput").ap()
    qf_o = nc.dram_tensor("q_f", [D, RPC], F32, kind="ExternalOutput").ap()
    kf_o = nc.dram_tensor("k_f", [256, RPC], F32, kind="ExternalOutput").ap()
    vt_o = nc.dram_tensor("v_tok", [RPC, 256], F32, kind="ExternalOutput").ap()

    ser = Ser()
    with ExitStack() as ctx:
        sb = lambda name, shape, dt=BF16: ctx.enter_context(nc.sbuf_tensor(name, shape, dt))
        ps = lambda name, shape, dt=F32: ctx.enter_context(nc.psum_tensor(name, shape, dt))

        wq = sb("wq_s", [128, 8, D])
        wk = sb("wk_s", [128, 8, 256])
        wv = sb("wv_s", [128, 8, 256])
        bq = sb("bq_s", [1, D])
        bk = sb("bk_s", [1, 256])
        bv = sb("bv_s", [1, 256])
        ones = sb("ones", [1, 512])
        cc = sb("cc_s", [128, RPC])
        ss = sb("ss_s", [128, RPC])
        ident = sb("ident_s", [128, 128])
        x_sb = sb("x_sb", [128, D], F32)
        sq = sb("sq", [128, D], F32)
        st = sb("st", [128, 8], F32)
        xn_sb = sb("xn_sb", [128, D])
        xn32 = sb("xn32", [128, D], F32)
        xnf = sb("xnf", [128, 8, RPC])
        qraw = sb("qraw", [128, 512])
        qshuf = sb("qshuf", [128, 512])
        t1 = sb("t1", [128, 512])
        t2 = sb("t2", [128, 512])
        stage = sb("stage", [128, RPC], F32)
        v_sb = sb("v_sb", [128, 256], F32)
        psA = ps("psA", [128, 512])
        psT = ps("psT", [128, 128], BF16)
        psV = ps("psV", [128, 256])

        ser.dmag(lambda e: e.dma_start(out=wq[:], in_=wq_d.rearrange("(c p) n -> p c n", p=128)))
        ser.dmag(lambda e: e.dma_start(out=wk[:], in_=wk_d.rearrange("(c p) n -> p c n", p=128)))
        ser.dmag(lambda e: e.dma_start(out=wv[:], in_=wv_d.rearrange("(c p) n -> p c n", p=128)))
        ser.dmag(lambda e: e.dma_start(out=bq[:], in_=bq_d[:]))
        ser.dmag(lambda e: e.dma_start(out=bk[:], in_=bk_d[:]))
        ser.dmag(lambda e: e.dma_start(out=bv[:], in_=bv_d[:]))
        ser.dmag(lambda e: e.dma_start(out=cc[:], in_=cc_d[:]))
        ser.dmag(lambda e: e.dma_start(out=ss[:], in_=ss_d[:]))
        ser.dmag(lambda e: e.dma_start(out=ident[:], in_=id_d[:]))
        ser.vector(lambda e: e.memset(ones[:], 1.0))

        tiles = {"sq": sq, "st": st}
        NT = RPC // 128
        for i in range(NT):
            r = slice(128 * i, 128 * i + 128)
            ser.dma(lambda e, r=r: e.dma_start(out=x_sb[:], in_=xd[r, :]))
            ln_tile(ser, None, x_sb, xn32, tiles)
            ser.dma(lambda e, r=r: e.dma_start(out=xn_o[r, :], in_=xn32[:]))
            ser.vector(lambda e: e.tensor_copy(xn_sb[:, :], xn32[:, :]))
            for c in range(8):
                ser.tensor(lambda e, c=c: e.transpose(psT[:], xn_sb[:, 128 * c:128 * c + 128], ident[:]))
                ser.vector(lambda e, c=c, i=i: e.tensor_copy(xnf[:, c, 128 * i:128 * i + 128], psT[:]))

        NB = RPC // 512
        for (wsb, bsb, nm, out_d) in ((wq, bq, 8, qf_o), (wk, bk, 2, kf_o)):
            for m in range(nm):
                msl = slice(128 * m, 128 * m + 128)
                for n in range(NB):
                    nsl = slice(512 * n, 512 * n + 512)
                    for c in range(8):
                        ser.tensor(lambda e, c=c, msl=msl, nsl=nsl, wsb=wsb: e.matmul(
                            psA[:], wsb[:, c, msl], xnf[:, c, nsl],
                            start=(c == 0), stop=False))
                    ser.tensor(lambda e, msl=msl, bsb=bsb: e.matmul(
                        psA[:], bsb[:, msl], ones[:], start=False, stop=True))
                    ser.vector(lambda e: e.tensor_copy(qraw[:], psA[:]))
                    ser.vector(lambda e: e.stream_shuffle(qshuf[:], qraw[:], SWAPMASK))
                    ser.vector(lambda e, nsl=nsl: e.tensor_tensor(t1[:], qraw[:], cc[:, nsl], ALU.mult))
                    ser.vector(lambda e, nsl=nsl: e.tensor_tensor(t2[:], qshuf[:], ss[:, nsl], ALU.mult))
                    ser.vector(lambda e, nsl=nsl: e.tensor_tensor(stage[:, nsl], t1[:], t2[:], ALU.add))
                ser.dma(lambda e, msl=msl, out_d=out_d: e.dma_start(out=out_d[msl, :], in_=stage[:]))

        for i in range(NT):
            r = slice(128 * i, 128 * i + 128)
            for c in range(8):
                ser.tensor(lambda e, c=c, i=i: e.matmul(
                    psV[:], xnf[:, c, 128 * i:128 * i + 128], wv[:, c, :],
                    start=(c == 0), stop=False))
            ser.tensor(lambda e: e.matmul(psV[:], ones[:, 0:128], bv[:], start=False, stop=True))
            ser.vector(lambda e: e.tensor_copy(v_sb[:], psV[:]))
            ser.dma(lambda e, r=r: e.dma_start(out=vt_o[r, :], in_=v_sb[:]))

        ser.emit(nc)
    return nc


def run_p1(x_flat, inp):
    cc0, ss0 = _rope0_tables()
    nc = build_p1()
    in_maps = []
    for c in range(NCORE):
        r = slice(RPC * c, RPC * c + RPC)
        in_maps.append({
            "x_rows": np.ascontiguousarray(x_flat[r]),
            "Wq": np.asarray(inp["L0_attn_Wq"], np.float32),
            "bq": np.asarray(inp["L0_attn_bq"], np.float32).reshape(1, D),
            "Wk": np.asarray(inp["L0_attn_Wk"], np.float32),
            "bk": np.asarray(inp["L0_attn_bk"], np.float32).reshape(1, 256),
            "Wv": np.asarray(inp["L0_attn_Wv"], np.float32),
            "bv": np.asarray(inp["L0_attn_bv"], np.float32).reshape(1, 256),
            "cc0": np.ascontiguousarray(cc0[:, r]),
            "ss0": np.ascontiguousarray(ss0[:, r]),
            "ident": IDENT,
        })
    res = _run(nc, in_maps)
    xn = np.concatenate([np.asarray(r_["xn"]) for r_ in res], axis=0)
    q_f = np.concatenate([np.asarray(r_["q_f"]) for r_ in res], axis=1)
    k_f = np.concatenate([np.asarray(r_["k_f"]) for r_ in res], axis=1)
    v_tok = np.concatenate([np.asarray(r_["v_tok"]) for r_ in res], axis=0)
    return xn, q_f, k_f, v_tok



def kernel_multi(**inputs):
    x = np.asarray(inputs["L0_attn_x"], np.float32).reshape(ROWS, D)
    xn, q_f, k_f, v_tok = run_p1(x, inputs)
    att0 = run_p2(q_f, k_f, v_tok)
    c_f = run_p3(xn, inputs)
    x1, xn1 = run_p4(x, xn, att0, c_f, inputs)
    nc_pre = build_subpre()
    nc_att = build_subatt()
    nc_post = build_subpost()
    q0, k0, v0 = run_subpre(nc_pre, xn1, inputs, 0)
    att_s0 = run_subatt(nc_att, q0, k0, v0)
    z2 = run_subpost(nc_post, xn1, att_s0, inputs, 0)
    q1, k1, v1 = run_subpre(nc_pre, z2, inputs, 1)
    att_s1 = run_subatt(nc_att, q1, k1, v1)
    z4 = run_subpost(nc_post, z2, att_s1, inputs, 1)
    out = run_p9(xn1, x1, z4, inputs)
    return out.reshape(B, T, D)



def build_p2():
    nc = bass.Bass("TRN2", target_bir_lowering=False)
    q_d = nc.dram_tensor("q_sl", [512, T], F32, kind="ExternalInput").ap()
    k_d = nc.dram_tensor("k_sl", [128, T], F32, kind="ExternalInput").ap()
    v_d = nc.dram_tensor("v_sl", [T, 128], F32, kind="ExternalInput").ap()
    m_d = nc.dram_tensor("masks", [128, 4, 512], F32, kind="ExternalInput").ap()
    att_o = nc.dram_tensor("att_sl", [512, T], F32, kind="ExternalOutput").ap()

    NKT = T // 128
    NQT = T // 512
    ser = Ser()
    with ExitStack() as ctx:
        sb = lambda name, shape, dt=BF16: ctx.enter_context(nc.sbuf_tensor(name, shape, dt))
        ps = lambda name, shape, dt=F32: ctx.enter_context(nc.psum_tensor(name, shape, dt))

        q_sb = sb("q_sb", [128, 4, T])
        k2 = sb("k2", [128, 2, T])
        v_sb = sb("v_sb", [128, NKT, 128])
        vaug = sb("vaug", [128, NKT, 130])
        mask = sb("mask", [128, 4, 512])
        sq = sb("sq_p2", [128, T])
        pa = sb("pa", [128, 512])
        pb = sb("pb", [128, 512])
        araw = sb("araw", [64, 512], F32)
        asb = sb("asb", [64, 512], F32)
        rec = sb("rec", [1, 512], F32)
        onesb = sb("onesb", [128, 1])
        onesf = sb("onesf", [1, 128], F32)
        hsc = sb("hsc", [1, 16], F32)
        ksc = sb("ksc", [1, 4], F32)
        bias_sb = sb("bias_sb", [128, 8], F32)
        psS_A = ps("psS_A", [128, 512])
        psS_B = ps("psS_B", [128, 512])
        psO_A = ps("psO_A", [128, 512])
        psO_B = ps("psO_B", [128, 512])
        psR = ps("psR", [128, 512])
        psQ = ps("psQ", [128, 512])

        ser.dmag(lambda e: e.dma_start(out=q_sb[:], in_=q_d.rearrange("(c p) t -> p c t", p=128)))
        for g in range(2):
            for half in range(2):
                ser.dmag(lambda e, g=g, half=half: e.dma_start(
                    out=k2[64 * half:64 * half + 64, g, :], in_=k_d[64 * g:64 * g + 64, :]))
        ser.dmag(lambda e: e.dma_start(out=v_sb[:], in_=v_d.rearrange("(kt p) d -> p kt d", p=128)))
        ser.dmag(lambda e: e.dma_start(out=mask[:], in_=m_d[:]))
        ser.vector(lambda e: e.memset(vaug[:], 1.0))
        for kt in range(NKT):
            ser.vector(lambda e, kt=kt: e.tensor_copy(vaug[:, kt, 0:64], v_sb[:, kt, 0:64]))
            ser.vector(lambda e, kt=kt: e.tensor_copy(vaug[:, kt, 65:129], v_sb[:, kt, 64:128]))
        ser.vector(lambda e: e.memset(onesb[:], 1.0))
        ser.vector(lambda e: e.memset(onesf[:], 1.0))
        ser.vector(lambda e: e.memset(hsc[:], 0.0))
        ser.vector(lambda e: e.memset(ksc[:], 0.0))

        for c in range(4):
            ser.scalar(lambda e, c=c: e.activation(sq[:], q_sb[:, c, :], AF.Square))
            for half in range(2):
                h = 2 * c + half
                pr = slice(64 * half, 64 * half + 64)
                for n in range(NQT):
                    nsl = slice(512 * n, 512 * n + 512)
                    ser.tensor(lambda e, pr=pr, nsl=nsl: e.matmul(
                        psQ[0:1, :], onesb[pr, :], sq[pr, nsl], start=True, stop=True,
                        tile_position=(pr.start, 0)))
                    ser.vector(lambda e: e.reduce_max(rec[0:1, 0:1], psQ[0:1, :], AX.X))
                    ser.vector(lambda e, h=h: e.tensor_tensor(
                        hsc[0:1, h:h + 1], hsc[0:1, h:h + 1], rec[0:1, 0:1], ALU.max))
        ser.scalar(lambda e: e.activation(sq[:], k2[:, 0, :], AF.Square))
        for g in range(2):
            pr = slice(64 * g, 64 * g + 64)
            for n in range(NQT):
                nsl = slice(512 * n, 512 * n + 512)
                ser.tensor(lambda e, pr=pr, nsl=nsl: e.matmul(
                    psQ[0:1, :], onesb[pr, :], sq[pr, nsl], start=True, stop=True,
                    tile_position=(pr.start, 0)))
                ser.vector(lambda e: e.reduce_max(rec[0:1, 0:1], psQ[0:1, :], AX.X))
                ser.vector(lambda e, g=g: e.tensor_tensor(
                    ksc[0:1, g:g + 1], ksc[0:1, g:g + 1], rec[0:1, 0:1], ALU.max))
        ser.scalar(lambda e: e.activation(hsc[0:1, 8:16], hsc[0:1, 0:8], AF.Sqrt))
        ser.scalar(lambda e: e.activation(ksc[0:1, 2:4], ksc[0:1, 0:2], AF.Sqrt))
        for h in range(8):
            g = h // 4
            ser.vector(lambda e, h=h, g=g: e.tensor_tensor(
                hsc[0:1, h:h + 1], hsc[0:1, 8 + h:9 + h], ksc[0:1, 2 + g:3 + g], ALU.mult))
            ser.vector(lambda e, h=h: e.tensor_scalar_mul(hsc[0:1, h:h + 1], hsc[0:1, h:h + 1], -0.125))
            ser.tensor(lambda e, h=h: e.matmul(
                psR[:, 0:1], onesf[:, :], hsc[0:1, h:h + 1], start=True, stop=True))
            ser.vector(lambda e, h=h: e.tensor_copy(bias_sb[:, h:h + 1], psR[:, 0:1]))

        for p in range(4):
            g = p // 2
            for j in range(NQT):
                jsl = slice(512 * j, 512 * j + 512)
                nkt = 4 * j + 4
                for i in range(nkt):
                    isl = slice(128 * i, 128 * i + 128)
                    last = i == nkt - 1
                    ser.tensor(lambda e, g=g, isl=isl, p=p, jsl=jsl: e.matmul(
                        psS_A[:], k2[0:64, g, isl], q_sb[0:64, p, jsl],
                        start=True, stop=True, tile_position=(0, 0)))
                    ser.tensor(lambda e, g=g, isl=isl, p=p, jsl=jsl: e.matmul(
                        psS_B[:], k2[64:128, g, isl], q_sb[64:128, p, jsl],
                        start=True, stop=True, tile_position=(64, 0)))
                    ser.scalar(lambda e, p=p: e.activation(
                        pa[:], psS_A[:], AF.Exp, bias=bias_sb[:, 2 * p:2 * p + 1], scale=0.125))
                    ser.scalar(lambda e, p=p: e.activation(
                        pb[:], psS_B[:], AF.Exp, bias=bias_sb[:, 2 * p + 1:2 * p + 2], scale=0.125))
                    d = i - 4 * j
                    if d >= 0:
                        ser.vector(lambda e, d=d: e.tensor_tensor(pa[:], pa[:], mask[:, d, :], ALU.mult))
                        ser.vector(lambda e, d=d: e.tensor_tensor(pb[:], pb[:], mask[:, d, :], ALU.mult))
                    ser.tensor(lambda e, i=i, last=last, g=g: e.matmul(
                        psO_A[0:65, :], vaug[:, i, 65 * g:65 * g + 65], pa[:], start=(i == 0), stop=last))
                    ser.tensor(lambda e, i=i, last=last, g=g: e.matmul(
                        psO_B[0:65, :], vaug[:, i, 65 * g:65 * g + 65], pb[:], start=(i == 0), stop=last))
                for half, psO in ((0, psO_A), (1, psO_B)):
                    h = 2 * p + half
                    ser.vector(lambda e, psO=psO: e.reciprocal(rec[:], psO[64:65, :]))
                    ser.tensor(lambda e: e.matmul(psR[0:64, :], onesf[:, 0:64], rec[:], start=True, stop=True))
                    ser.vector(lambda e, psO=psO: e.tensor_copy(araw[:], psO[0:64, :]))
                    ser.vector(lambda e: e.tensor_tensor(asb[:], araw[:], psR[0:64, :], ALU.mult))
                    ser.dma(lambda e, h=h, jsl=jsl: e.dma_start(
                        out=att_o[64 * h:64 * h + 64, jsl], in_=asb[:]))
        ser.emit(nc)
    return nc


def run_p2(q_f, k_f, v_tok):
    masks = np.ascontiguousarray(_diag_masks().transpose(1, 0, 2))
    nc = build_p2()
    in_maps = []
    for c in range(NCORE):
        b, hg = c // 2, c % 2
        in_maps.append({
            "q_sl": np.ascontiguousarray(q_f[512 * hg:512 * hg + 512, T * b:T * b + T]),
            "k_sl": np.ascontiguousarray(k_f[128 * hg:128 * hg + 128, T * b:T * b + T]),
            "v_sl": np.ascontiguousarray(v_tok[T * b:T * b + T, 128 * hg:128 * hg + 128]),
            "masks": masks,
        })
    res = _run(nc, in_maps)
    att = np.zeros((D, ROWS), np.float32)
    for c in range(NCORE):
        b, hg = c // 2, c % 2
        att[512 * hg:512 * hg + 512, T * b:T * b + T] = np.asarray(res[c]["att_sl"])
    return att



def emit_tr(ser, items, psTs, ident, w0=None):
    lag = 2 if len(psTs) >= 2 else 1
    done = {}
    for idx, (s_ap, d_ap) in enumerate(items):
        pt = psTs[idx % len(psTs)]
        w = done.get(idx - lag)
        if w0 is not None:
            w = w0 if w is None else max(w, w0)
        ser.tensor(lambda e, s_ap=s_ap, pt=pt: e.transpose(pt[:], s_ap, ident[:]), wait=w)
        mk = ser.mark()
        ser.vector(lambda e, d_ap=d_ap, pt=pt: e.tensor_copy(d_ap, pt[:]), wait=mk)
        done[idx] = ser.mark()


def emit_load_wT(ser, nc, w_d, stage, wT, ident, psT, nchunk_out=8, nchunk_in=8):
    ser.dmag(lambda e: e.dma_start(out=stage[:, 0:nchunk_out, :], in_=w_d.rearrange("(c p) n -> p c n", p=128)))
    psTs = psT if isinstance(psT, list) else [psT]
    w0 = ser.mark()
    items = [(stage[:, co, 128 * ci:128 * ci + 128], wT[:, ci, 128 * co:128 * co + 128])
             for co in range(nchunk_out) for ci in range(nchunk_in)]
    emit_tr(ser, items, psTs, ident, w0=w0)


def build_p3():
    nc = bass.Bass("TRN2", target_bir_lowering=False)
    xn_d = nc.dram_tensor("xn_sl", [4 * 512, D], F32, kind="ExternalInput").ap()
    w_ds = {}
    for nm in ("wf", "uf", "wi", "ui", "wc", "uc"):
        w_ds[nm] = nc.dram_tensor(nm, [D, D], F32, kind="ExternalInput").ap()
    b_ds = {}
    for nm in ("bf", "bi", "bc"):
        b_ds[nm] = nc.dram_tensor(nm, [1, D], F32, kind="ExternalInput").ap()
    inc_d = nc.dram_tensor("in_c", [128, 8], F32, kind="ExternalInput").ap()
    id_d = nc.dram_tensor("ident", [128, 128], F32, kind="ExternalInput").ap()
    c_o = nc.dram_tensor("c_final", [D, 512], F32, kind="ExternalOutput").ap()

    ser = Ser()
    with ExitStack() as ctx:
        sb = lambda name, shape, dt=BF16: ctx.enter_context(nc.sbuf_tensor(name, shape, dt))
        ps = lambda name, shape, dt=F32: ctx.enter_context(nc.psum_tensor(name, shape, dt))

        ident = sb("ident_s", [128, 128])
        stage = sb("stage_w", [128, 8, D])
        wT = {nm: sb(nm + "_T", [128, 8, D]) for nm in w_ds}
        brow = {nm: sb(nm + "_r", [1, D]) for nm in b_ds}
        inc_sb = sb("inc_sb", [128, 8], F32)
        ones = sb("ones_p3", [1, 512])
        xf = sb("xf", [128, 8, 512])
        xld = sb("xld", [128, D])
        cf32 = sb("cf32", [128, 8, 512], F32)
        cb16 = sb("cb16", [128, 8, 512])
        cnew = sb("cnew", [128, 8, 512], F32)
        gsb = sb("gsb", [128, 8, 512], F32)
        gch = sb("gch", [128, 512], F32)
        tmp = sb("tmp_p3", [128, 512], F32)
        psA = ps("psA_p3", [128, 512])
        psT = ps("psT_p3", [128, 128], BF16)

        ser.dmag(lambda e: e.dma_start(out=ident[:], in_=id_d[:]))
        for nm in w_ds:
            emit_load_wT(ser, nc, w_ds[nm], stage, wT[nm], ident, psT)
        for nm in b_ds:
            ser.dmag(lambda e, nm=nm: e.dma_start(out=brow[nm][:], in_=b_ds[nm][:]))
        ser.dma(lambda e: e.dma_start(out=inc_sb[:], in_=inc_d[:]))
        ser.vector(lambda e: e.memset(ones[:], 1.0))

        for m in range(8):
            ser.vector(lambda e, m=m: e.memset(cf32[:, m, :], 0.0))
            ser.vector(lambda e, m=m: e.tensor_scalar_add(cf32[:, m, :], cf32[:, m, :], inc_sb[:, m:m + 1]))
            ser.vector(lambda e, m=m: e.tensor_copy(cb16[:, m, :], cf32[:, m, :]))

        def gate_psum(wname, uname, bname, m):
            for ci in range(8):
                ser.tensor(lambda e, ci=ci, m=m, wname=wname: e.matmul(
                    psA[:], wT[wname][:, ci, 128 * m:128 * m + 128], xf[:, ci, :],
                    start=(ci == 0), stop=False))
            for ci in range(8):
                ser.tensor(lambda e, ci=ci, m=m, uname=uname: e.matmul(
                    psA[:], wT[uname][:, ci, 128 * m:128 * m + 128], cb16[:, ci, :],
                    start=False, stop=False))
            ser.tensor(lambda e, m=m, bname=bname: e.matmul(
                psA[:], brow[bname][:, 128 * m:128 * m + 128], ones[:], start=False, stop=True))

        for t in range(4):
            for rt in range(4):
                r = slice(512 * t + 128 * rt, 512 * t + 128 * rt + 128)
                ser.dmag(lambda e, r=r: e.dma_start(out=xld[:], in_=xn_d[r, :]))
                for c in range(8):
                    ser.tensor(lambda e, c=c: e.transpose(
                        psT[:], xld[:, 128 * c:128 * c + 128], ident[:]))
                    ser.vector(lambda e, c=c, rt=rt: e.tensor_copy(
                        xf[:, c, 128 * rt:128 * rt + 128], psT[:]))
            for m in range(8):
                gate_psum("wc", "uc", "bc", m)
                ser.scalar(lambda e, m=m: e.activation(gsb[:, m, :], psA[:], AF.Tanh))
            for m in range(8):
                gate_psum("wf", "uf", "bf", m)
                ser.scalar(lambda e, m=m: e.activation(gch[:], psA[:], AF.Sigmoid))
                ser.vector(lambda e, m=m: e.tensor_tensor(
                    cnew[:, m, :], gch[:], cf32[:, m, :], ALU.mult))
            for m in range(8):
                gate_psum("wi", "ui", "bi", m)
                ser.scalar(lambda e, m=m: e.activation(gch[:], psA[:], AF.Sigmoid))
                ser.vector(lambda e, m=m: e.tensor_tensor(tmp[:], gch[:], gsb[:, m, :], ALU.mult))
                ser.vector(lambda e, m=m: e.tensor_tensor(
                    cf32[:, m, :], cnew[:, m, :], tmp[:], ALU.add))
            for m in range(8):
                ser.vector(lambda e, m=m: e.tensor_copy(cb16[:, m, :], cf32[:, m, :]))

        for m in range(8):
            ser.dma(lambda e, m=m: e.dma_start(out=c_o[128 * m:128 * m + 128, :], in_=cf32[:, m, :]))
        ser.emit(nc)
    return nc


def run_p3(xn, inp):
    nc = build_p3()
    in_maps = []
    for c in range(NCORE):
        s0 = 512 * c
        xn_sl = np.concatenate([xn[t * T + s0:t * T + s0 + 512] for t in range(B)], axis=0)
        m = {"xn_sl": np.ascontiguousarray(xn_sl), "ident": IDENT,
             "in_c": np.ascontiguousarray(np.asarray(inp["L0_ffn_in_c"], np.float32).reshape(8, 128).T)}
        for nm in ("wf", "uf", "wi", "ui", "wc", "uc"):
            m[nm] = np.asarray(inp["L0_ffn_" + nm], np.float32)
        for nm in ("bf", "bi", "bc"):
            m[nm] = np.asarray(inp["L0_ffn_" + nm], np.float32).reshape(1, D)
        in_maps.append(m)
    res = _run(nc, in_maps)
    return np.concatenate([np.asarray(r_["c_final"]) for r_ in res], axis=1)



def build_p4():
    nc = bass.Bass("TRN2", target_bir_lowering=False)
    x_d = nc.dram_tensor("x_rows", [RPC, D], F32, kind="ExternalInput").ap()
    xn_d = nc.dram_tensor("xn_rows", [RPC, D], F32, kind="ExternalInput").ap()
    att_d = nc.dram_tensor("att_sl", [D, RPC], F32, kind="ExternalInput").ap()
    cf_d = nc.dram_tensor("cf_sl", [D, RPC], F32, kind="ExternalInput").ap()
    Wo_d = nc.dram_tensor("Wo", [D, D], F32, kind="ExternalInput").ap()
    bo_d = nc.dram_tensor("bo", [1, D], F32, kind="ExternalInput").ap()
    wof_d = nc.dram_tensor("wof", [D, D], F32, kind="ExternalInput").ap()
    uof_d = nc.dram_tensor("uof", [D, D], F32, kind="ExternalInput").ap()
    bof_d = nc.dram_tensor("bof", [1, D], F32, kind="ExternalInput").ap()
    id_d = nc.dram_tensor("ident", [128, 128], F32, kind="ExternalInput").ap()
    x1_o = nc.dram_tensor("x1", [RPC, D], F32, kind="ExternalOutput").ap()
    xn1_o = nc.dram_tensor("xn1", [RPC, D], F32, kind="ExternalOutput").ap()

    ser = Ser()
    with ExitStack() as ctx:
        sb = lambda name, shape, dt=BF16: ctx.enter_context(nc.sbuf_tensor(name, shape, dt))
        ps = lambda name, shape, dt=F32: ctx.enter_context(nc.psum_tensor(name, shape, dt))
        ident = sb("ident_s", [128, 128])
        stage = sb("stage_w", [128, 8, D])
        attf = sb("attf", [128, 8, RPC])
        cfb = sb("cfb", [128, 8, RPC])
        wo_sb = sb("wo_sb", [128, 8, D])
        woT = sb("woT", [128, 8, D])
        uoT = sb("uoT", [128, 8, D])
        bo_r = sb("bo_r", [1, D])
        bof_r = sb("bof_r", [1, D])
        ones = sb("ones_p4", [1, 512])
        x_sb = sb("x_sb", [128, D], F32)
        xn_sb = sb("xn_sb", [128, D])
        xnf = sb("xnf_t", [128, 8, 128])
        h_sb = sb("h_sb", [128, D], F32)
        x1_sb = sb("x1_sb", [128, D], F32)
        xn1_sb = sb("xn1_sb", [128, D], F32)
        sq = sb("sq", [128, D], F32)
        st = sb("st", [128, 8], F32)
        psO = ps("psO_p4", [128, 512])
        psH = ps("psH_p4", [128, 512])
        psT = ps("psT_p4", [128, 128], BF16)

        ser.dmag(lambda e: e.dma_start(out=ident[:], in_=id_d[:]))
        ser.dmag(lambda e: e.dma_start(out=attf[:], in_=att_d.rearrange("(c p) t -> p c t", p=128)))
        ser.dmag(lambda e: e.dma_start(out=cfb[:], in_=cf_d.rearrange("(c p) t -> p c t", p=128)))
        ser.dmag(lambda e: e.dma_start(out=wo_sb[:], in_=Wo_d.rearrange("(c p) n -> p c n", p=128)))
        ser.dmag(lambda e: e.dma_start(out=bo_r[:], in_=bo_d[:]))
        ser.dmag(lambda e: e.dma_start(out=bof_r[:], in_=bof_d[:]))
        emit_load_wT(ser, nc, wof_d, stage, woT, ident, psT)
        emit_load_wT(ser, nc, uof_d, stage, uoT, ident, psT)
        ser.vector(lambda e: e.memset(ones[:], 1.0))

        tiles = {"sq": sq, "st": st}
        NT = RPC // 128
        for i in range(NT):
            r = slice(128 * i, 128 * i + 128)
            tsl = slice(128 * i, 128 * i + 128)
            ser.dma(lambda e, r=r: e.dma_start(out=x_sb[:], in_=x_d[r, :]))
            ser.dmag(lambda e, r=r: e.dma_start(out=xn_sb[:], in_=xn_d[r, :]))
            for c in range(8):
                ser.tensor(lambda e, c=c: e.transpose(psT[:], xn_sb[:, 128 * c:128 * c + 128], ident[:]))
                ser.vector(lambda e, c=c: e.tensor_copy(xnf[:, c, :], psT[:]))
            for nb in range(2):
                nsl = slice(512 * nb, 512 * nb + 512)
                for c in range(8):
                    ser.tensor(lambda e, c=c, tsl=tsl, nsl=nsl: e.matmul(
                        psO[:], attf[:, c, tsl], wo_sb[:, c, nsl], start=(c == 0), stop=False))
                ser.tensor(lambda e, nsl=nsl: e.matmul(
                    psO[:], ones[:, 0:128], bo_r[:, nsl], start=False, stop=True))
                for c in range(8):
                    ser.tensor(lambda e, c=c, nsl=nsl: e.matmul(
                        psH[:], xnf[:, c, :], woT[:, c, nsl], start=(c == 0), stop=False))
                for c in range(8):
                    ser.tensor(lambda e, c=c, tsl=tsl, nsl=nsl: e.matmul(
                        psH[:], cfb[:, c, tsl], uoT[:, c, nsl], start=False, stop=False))
                ser.tensor(lambda e, nsl=nsl: e.matmul(
                    psH[:], ones[:, 0:128], bof_r[:, nsl], start=False, stop=True))
                ser.scalar(lambda e, nsl=nsl: e.activation(h_sb[:, nsl], psH[:], AF.Tanh))
                ser.vector(lambda e, nsl=nsl: e.tensor_tensor(
                    x1_sb[:, nsl], x_sb[:, nsl], h_sb[:, nsl], ALU.add))
                ser.vector(lambda e, nsl=nsl: e.tensor_tensor(
                    x1_sb[:, nsl], x1_sb[:, nsl], psO[:], ALU.add))
            ser.dma(lambda e, r=r: e.dma_start(out=x1_o[r, :], in_=x1_sb[:]))
            ln_tile(ser, None, x1_sb, xn1_sb, tiles)
            ser.dma(lambda e, r=r: e.dma_start(out=xn1_o[r, :], in_=xn1_sb[:]))
        ser.emit(nc)
    return nc


def run_p4(x_flat, xn, att_f, c_f, inp):
    nc = build_p4()
    in_maps = []
    for c in range(NCORE):
        r = slice(RPC * c, RPC * c + RPC)
        b, half = c // 2, c % 2
        tcols = slice(2048 * half, 2048 * half + 2048)
        in_maps.append({
            "x_rows": np.ascontiguousarray(x_flat[r]),
            "xn_rows": np.ascontiguousarray(xn[r]),
            "att_sl": np.ascontiguousarray(att_f[:, r]),
            "cf_sl": np.ascontiguousarray(c_f[:, tcols]),
            "Wo": np.asarray(inp["L0_attn_Wo"], np.float32),
            "bo": np.asarray(inp["L0_attn_bo"], np.float32).reshape(1, D),
            "wof": np.asarray(inp["L0_ffn_wo"], np.float32),
            "uof": np.asarray(inp["L0_ffn_uo"], np.float32),
            "bof": np.asarray(inp["L0_ffn_bo"], np.float32).reshape(1, D),
            "ident": IDENT,
        })
    res = _run(nc, in_maps)
    x1 = np.concatenate([np.asarray(r_["x1"]) for r_ in res], axis=0)
    xn1 = np.concatenate([np.asarray(r_["xn1"]) for r_ in res], axis=0)
    return x1, xn1



def build_subpre():
    nc = bass.Bass("TRN2", target_bir_lowering=False)
    z_d = nc.dram_tensor("z_rows", [RPC, D], F32, kind="ExternalInput").ap()
    g_d = nc.dram_tensor("g_row", [1, D], F32, kind="ExternalInput").ap()
    b_d = nc.dram_tensor("b_row", [1, D], F32, kind="ExternalInput").ap()
    wq_d = nc.dram_tensor("wq", [D, D], F32, kind="ExternalInput").ap()
    wk_d = nc.dram_tensor("wk", [D, D], F32, kind="ExternalInput").ap()
    wv_d = nc.dram_tensor("wv", [D, D], F32, kind="ExternalInput").ap()
    bq_d = nc.dram_tensor("bq", [1, D], F32, kind="ExternalInput").ap()
    bk_d = nc.dram_tensor("bk", [1, D], F32, kind="ExternalInput").ap()
    bv_d = nc.dram_tensor("bv", [1, D], F32, kind="ExternalInput").ap()
    tb_d = nc.dram_tensor("tbl", [128, RPC], F32, kind="ExternalInput").ap()
    id_d = nc.dram_tensor("ident", [128, 128], F32, kind="ExternalInput").ap()
    qf_o = nc.dram_tensor("q_f", [D, RPC], F32, kind="ExternalOutput").ap()
    kf_o = nc.dram_tensor("k_f", [D, RPC], F32, kind="ExternalOutput").ap()
    vt_o = nc.dram_tensor("v_tok", [RPC, D], F32, kind="ExternalOutput").ap()

    ser = Ser()
    with ExitStack() as ctx:
        sb = lambda name, shape, dt=BF16: ctx.enter_context(nc.sbuf_tensor(name, shape, dt))
        ps = lambda name, shape, dt=F32: ctx.enter_context(nc.psum_tensor(name, shape, dt))
        ident = sb("ident_s", [128, 128])
        wq = sb("wq_s", [128, 8, D])
        wk = sb("wk_s", [128, 8, D])
        wv = sb("wv_s", [128, 8, D])
        bq = sb("bq_s", [1, D])
        bk = sb("bk_s", [1, D])
        bv = sb("bv_s", [1, D])
        tbl = sb("tbl_s", [128, RPC])
        gb = sb("gb", [128, D], F32)
        bb = sb("bb", [128, D], F32)
        onesf = sb("onesf", [1, 128], F32)
        ones = sb("ones_sp", [1, 512])
        z_sb = sb("z_sb", [128, D], F32)
        zl1 = sb("zl1", [128, D], F32)
        zl_sb = sb("zl_sb", [128, D])
        zlf = sb("zlf", [128, 8, RPC])
        sq = sb("sq", [128, D], F32)
        st = sb("st", [128, 8], F32)
        qraw = sb("qraw", [128, 512])
        stage = sb("stage_q", [128, RPC], F32)
        v_sb = sb("v_sb", [128, D], F32)
        psA = ps("psA_sp", [128, 512])
        psT = ps("psT_sp", [128, 128], BF16)

        ser.dmag(lambda e: e.dma_start(out=ident[:], in_=id_d[:]))
        for w, wd in ((wq, wq_d), (wk, wk_d), (wv, wv_d)):
            ser.dmag(lambda e, w=w, wd=wd: e.dma_start(out=w[:], in_=wd.rearrange("(c p) n -> p c n", p=128)))
        for b_, bd in ((bq, bq_d), (bk, bk_d), (bv, bv_d)):
            ser.dmag(lambda e, b_=b_, bd=bd: e.dma_start(out=b_[:], in_=bd[:]))
        ser.dmag(lambda e: e.dma_start(out=tbl[:], in_=tb_d[:]))
        ser.vector(lambda e: e.memset(onesf[:], 1.0))
        ser.vector(lambda e: e.memset(ones[:], 1.0))
        gbounce = sb("gbounce", [1, D], F32)
        for row_d, dst in ((g_d, gb), (b_d, bb)):
            ser.dma(lambda e, row_d=row_d: e.dma_start(out=gbounce[:], in_=row_d[:]))
            for nb in range(2):
                nsl = slice(512 * nb, 512 * nb + 512)
                ser.tensor(lambda e, nsl=nsl: e.matmul(
                    psA[:], onesf[:, :], gbounce[:, nsl], start=True, stop=True))
                ser.vector(lambda e, dst=dst, nsl=nsl: e.tensor_copy(dst[:, nsl], psA[:]))

        tiles = {"sq": sq, "st": st}
        NT = RPC // 128
        for i in range(NT):
            r = slice(128 * i, 128 * i + 128)
            ser.dma(lambda e, r=r: e.dma_start(out=z_sb[:], in_=z_d[r, :]))
            ln_tile(ser, None, z_sb, zl1, tiles)
            ser.vector(lambda e: e.tensor_tensor(zl1[:], zl1[:], gb[:], ALU.mult))
            ser.vector(lambda e: e.tensor_tensor(zl_sb[:], zl1[:], bb[:], ALU.add))
            for c in range(8):
                ser.tensor(lambda e, c=c: e.transpose(psT[:], zl_sb[:, 128 * c:128 * c + 128], ident[:]))
                ser.vector(lambda e, c=c, i=i: e.tensor_copy(zlf[:, c, 128 * i:128 * i + 128], psT[:]))

        NB = RPC // 512
        for (wsb, bsb, out_d, rope) in ((wq, bq, qf_o, True), (wk, bk, kf_o, True)):
            for m in range(8):
                msl = slice(128 * m, 128 * m + 128)
                for n in range(NB):
                    nsl = slice(512 * n, 512 * n + 512)
                    for c in range(8):
                        ser.tensor(lambda e, c=c, msl=msl, nsl=nsl, wsb=wsb: e.matmul(
                            psA[:], wsb[:, c, msl], zlf[:, c, nsl], start=(c == 0), stop=False))
                    ser.tensor(lambda e, msl=msl, bsb=bsb: e.matmul(
                        psA[:], bsb[:, msl], ones[:], start=False, stop=True))
                    ser.vector(lambda e: e.tensor_copy(qraw[:], psA[:]))
                    ser.vector(lambda e, nsl=nsl: e.tensor_tensor(
                        stage[:, nsl], qraw[:], tbl[:, nsl], ALU.mult))
                ser.dma(lambda e, msl=msl, out_d=out_d: e.dma_start(out=out_d[msl, :], in_=stage[:]))
        for i in range(NT):
            r = slice(128 * i, 128 * i + 128)
            for nb in range(2):
                nsl = slice(512 * nb, 512 * nb + 512)
                for c in range(8):
                    ser.tensor(lambda e, c=c, i=i, nsl=nsl: e.matmul(
                        psA[:], zlf[:, c, 128 * i:128 * i + 128], wv[:, c, nsl],
                        start=(c == 0), stop=False))
                ser.tensor(lambda e, nsl=nsl: e.matmul(
                    psA[:], ones[:, 0:128], bv[:, nsl], start=False, stop=True))
                ser.vector(lambda e, nsl=nsl: e.tensor_copy(v_sb[:, nsl], psA[:]))
            ser.dma(lambda e, r=r: e.dma_start(out=vt_o[r, :], in_=v_sb[:]))
        ser.emit(nc)
    return nc


def run_subpre(nc, z, inp, li):
    tbl = _rope1_table()
    in_maps = []
    for c in range(NCORE):
        r = slice(RPC * c, RPC * c + RPC)
        in_maps.append({
            "z_rows": np.ascontiguousarray(z[r]),
            "g_row": np.asarray(inp["L1_attn_ln1_g"], np.float32)[li].reshape(1, D),
            "b_row": np.asarray(inp["L1_attn_ln1_b"], np.float32)[li].reshape(1, D),
            "wq": np.asarray(inp["L1_attn_wq"], np.float32)[li],
            "wk": np.asarray(inp["L1_attn_wk"], np.float32)[li],
            "wv": np.asarray(inp["L1_attn_wv"], np.float32)[li],
            "bq": np.asarray(inp["L1_attn_bq"], np.float32)[li].reshape(1, D),
            "bk": np.asarray(inp["L1_attn_bk"], np.float32)[li].reshape(1, D),
            "bv": np.asarray(inp["L1_attn_bv"], np.float32)[li].reshape(1, D),
            "tbl": np.ascontiguousarray(tbl[:, r]),
            "ident": IDENT,
        })
    res = _run(nc, in_maps)
    q_f = np.concatenate([np.asarray(r_["q_f"]) for r_ in res], axis=1)
    k_f = np.concatenate([np.asarray(r_["k_f"]) for r_ in res], axis=1)
    v_tok = np.concatenate([np.asarray(r_["v_tok"]) for r_ in res], axis=0)
    return q_f, k_f, v_tok



def build_subatt():
    nc = bass.Bass("TRN2", target_bir_lowering=False)
    q_d = nc.dram_tensor("q_sl", [512, T], F32, kind="ExternalInput").ap()
    k_d = nc.dram_tensor("k_sl", [512, T], F32, kind="ExternalInput").ap()
    v_d = nc.dram_tensor("v_sl", [T, 512], F32, kind="ExternalInput").ap()
    att_o = nc.dram_tensor("att_sl", [512, T], F32, kind="ExternalOutput").ap()

    NKT = T // 128
    NQT = T // 512
    ser = Ser()
    with ExitStack() as ctx:
        sb = lambda name, shape, dt=BF16: ctx.enter_context(nc.sbuf_tensor(name, shape, dt))
        ps = lambda name, shape, dt=F32: ctx.enter_context(nc.psum_tensor(name, shape, dt))
        q_sb = sb("q_sb", [128, 4, T])
        k_sb = sb("k_sb", [128, 4, T])
        v_sb = sb("v_sb2", [128, NKT, 512])
        vaug = sb("vaug2", [128, NKT, 8, 65])
        sq = sb("sq_sa", [128, T])
        pa = sb("pa", [128, 512])
        pb = sb("pb", [128, 512])
        araw = sb("araw", [64, 512], F32)
        asb = sb("asb", [64, 512], F32)
        rec = sb("rec", [1, 512], F32)
        onesb = sb("onesb", [128, 1])
        onesf = sb("onesf", [1, 128], F32)
        hsc = sb("hsc", [1, 16], F32)
        ksc = sb("ksc", [1, 16], F32)
        bias_sb = sb("bias_sb", [128, 8], F32)
        psS_A = ps("psS_A", [128, 512])
        psS_B = ps("psS_B", [128, 512])
        psO_A = ps("psO_A", [128, 512])
        psO_B = ps("psO_B", [128, 512])
        psR = ps("psR", [128, 512])
        psQ = ps("psQ", [128, 512])

        ser.dmag(lambda e: e.dma_start(out=q_sb[:], in_=q_d.rearrange("(c p) t -> p c t", p=128)))
        ser.dmag(lambda e: e.dma_start(out=k_sb[:], in_=k_d.rearrange("(c p) t -> p c t", p=128)))
        ser.dmag(lambda e: e.dma_start(out=v_sb[:], in_=v_d.rearrange("(kt p) d -> p kt d", p=128)))
        ser.vector(lambda e: e.memset(vaug[:], 1.0))
        for kt in range(NKT):
            for h in range(8):
                ser.vector(lambda e, kt=kt, h=h: e.tensor_copy(
                    vaug[:, kt, h, 0:64], v_sb[:, kt, 64 * h:64 * h + 64]))
        ser.vector(lambda e: e.memset(onesb[:], 1.0))
        ser.vector(lambda e: e.memset(onesf[:], 1.0))
        ser.vector(lambda e: e.memset(hsc[:], 0.0))
        ser.vector(lambda e: e.memset(ksc[:], 0.0))

        for src, acc in ((q_sb, hsc), (k_sb, ksc)):
            for c in range(4):
                ser.scalar(lambda e, c=c, src=src: e.activation(sq[:], src[:, c, :], AF.Square))
                for half in range(2):
                    h = 2 * c + half
                    pr = slice(64 * half, 64 * half + 64)
                    for n in range(NQT):
                        nsl = slice(512 * n, 512 * n + 512)
                        ser.tensor(lambda e, pr=pr, nsl=nsl: e.matmul(
                            psQ[0:1, :], onesb[pr, :], sq[pr, nsl], start=True, stop=True,
                            tile_position=(pr.start, 0)))
                        ser.vector(lambda e: e.reduce_max(rec[0:1, 0:1], psQ[0:1, :], AX.X))
                        ser.vector(lambda e, h=h, acc=acc: e.tensor_tensor(
                            acc[0:1, h:h + 1], acc[0:1, h:h + 1], rec[0:1, 0:1], ALU.max))
        ser.scalar(lambda e: e.activation(hsc[0:1, 8:16], hsc[0:1, 0:8], AF.Sqrt))
        ser.scalar(lambda e: e.activation(ksc[0:1, 8:16], ksc[0:1, 0:8], AF.Sqrt))
        for h in range(8):
            ser.vector(lambda e, h=h: e.tensor_tensor(
                hsc[0:1, h:h + 1], hsc[0:1, 8 + h:9 + h], ksc[0:1, 8 + h:9 + h], ALU.mult))
            ser.vector(lambda e, h=h: e.tensor_scalar_mul(hsc[0:1, h:h + 1], hsc[0:1, h:h + 1], -0.125))
            ser.tensor(lambda e, h=h: e.matmul(
                psR[:, 0:1], onesf[:, :], hsc[0:1, h:h + 1], start=True, stop=True))
            ser.vector(lambda e, h=h: e.tensor_copy(bias_sb[:, h:h + 1], psR[:, 0:1]))

        for p in range(4):
            for j in range(NQT):
                jsl = slice(512 * j, 512 * j + 512)
                for i in range(NKT):
                    isl = slice(128 * i, 128 * i + 128)
                    last = i == NKT - 1
                    ser.tensor(lambda e, p=p, isl=isl, jsl=jsl: e.matmul(
                        psS_A[:], k_sb[0:64, p, isl], q_sb[0:64, p, jsl],
                        start=True, stop=True, tile_position=(0, 0)))
                    ser.tensor(lambda e, p=p, isl=isl, jsl=jsl: e.matmul(
                        psS_B[:], k_sb[64:128, p, isl], q_sb[64:128, p, jsl],
                        start=True, stop=True, tile_position=(64, 0)))
                    ser.scalar(lambda e, p=p: e.activation(
                        pa[:], psS_A[:], AF.Exp, bias=bias_sb[:, 2 * p:2 * p + 1], scale=0.125))
                    ser.scalar(lambda e, p=p: e.activation(
                        pb[:], psS_B[:], AF.Exp, bias=bias_sb[:, 2 * p + 1:2 * p + 2], scale=0.125))
                    ser.tensor(lambda e, i=i, last=last, p=p: e.matmul(
                        psO_A[0:65, :], vaug[:, i, 2 * p, :], pa[:], start=(i == 0), stop=last))
                    ser.tensor(lambda e, i=i, last=last, p=p: e.matmul(
                        psO_B[0:65, :], vaug[:, i, 2 * p + 1, :], pb[:], start=(i == 0), stop=last))
                for half, psO in ((0, psO_A), (1, psO_B)):
                    h = 2 * p + half
                    ser.vector(lambda e, psO=psO: e.reciprocal(rec[:], psO[64:65, :]))
                    ser.tensor(lambda e: e.matmul(psR[0:64, :], onesf[:, 0:64], rec[:], start=True, stop=True))
                    ser.vector(lambda e, psO=psO: e.tensor_copy(araw[:], psO[0:64, :]))
                    ser.vector(lambda e: e.tensor_tensor(asb[:], araw[:], psR[0:64, :], ALU.mult))
                    ser.dma(lambda e, h=h, jsl=jsl: e.dma_start(
                        out=att_o[64 * h:64 * h + 64, jsl], in_=asb[:]))
        ser.emit(nc)
    return nc


def run_subatt(nc, q_f, k_f, v_tok):
    in_maps = []
    for c in range(NCORE):
        b, hg = c // 2, c % 2
        in_maps.append({
            "q_sl": np.ascontiguousarray(q_f[512 * hg:512 * hg + 512, T * b:T * b + T]),
            "k_sl": np.ascontiguousarray(k_f[512 * hg:512 * hg + 512, T * b:T * b + T]),
            "v_sl": np.ascontiguousarray(v_tok[T * b:T * b + T, 512 * hg:512 * hg + 512]),
        })
    res = _run(nc, in_maps)
    att = np.zeros((D, ROWS), np.float32)
    for c in range(NCORE):
        b, hg = c // 2, c % 2
        att[512 * hg:512 * hg + 512, T * b:T * b + T] = np.asarray(res[c]["att_sl"])
    return att



def build_subpost():
    nc = bass.Bass("TRN2", target_bir_lowering=False)
    z_d = nc.dram_tensor("z_rows", [RPC, D], F32, kind="ExternalInput").ap()
    att_d = nc.dram_tensor("att_sl", [D, RPC], F32, kind="ExternalInput").ap()
    wo_d = nc.dram_tensor("wo", [D, D], F32, kind="ExternalInput").ap()
    g2_d = nc.dram_tensor("g2_row", [1, D], F32, kind="ExternalInput").ap()
    b2_d = nc.dram_tensor("b2_row", [1, D], F32, kind="ExternalInput").ap()
    w1_d = nc.dram_tensor("w1", [D, 4 * D], F32, kind="ExternalInput").ap()
    b1_d = nc.dram_tensor("b1", [1, 4 * D], F32, kind="ExternalInput").ap()
    w2_d = nc.dram_tensor("w2", [4 * D, D], F32, kind="ExternalInput").ap()
    id_d = nc.dram_tensor("ident", [128, 128], F32, kind="ExternalInput").ap()
    z2_o = nc.dram_tensor("z2", [RPC, D], F32, kind="ExternalOutput").ap()

    ser = Ser()
    with ExitStack() as ctx:
        sb = lambda name, shape, dt=BF16: ctx.enter_context(nc.sbuf_tensor(name, shape, dt))
        ps = lambda name, shape, dt=F32: ctx.enter_context(nc.psum_tensor(name, shape, dt))
        ident = sb("ident_s", [128, 128])
        wo_sb = sb("wo_sb", [128, 8, D])
        w1_sb = sb("w1_sb", [128, 8, 4 * D])
        w2_sb = sb("w2_sb", [128, 32, D])
        b1_r = sb("b1_r", [1, 4 * D])
        g2b = sb("g2b", [128, D], F32)
        b2b = sb("b2b", [128, D], F32)
        gbounce = sb("gbounce", [1, D], F32)
        onesf = sb("onesf", [1, 128], F32)
        ones = sb("ones_spo", [1, 512])
        attf_t = sb("attf_t", [128, 8, 128])
        z_sb = sb("z_sb", [128, D], F32)
        z1_sb = sb("z1_sb", [128, D], F32)
        zl2a = sb("zl2a", [128, D], F32)
        zl2b = sb("zl2b", [128, D])
        zlf = sb("zlf_t", [128, 8, 128])
        sg = sb("sg", [128, 128])
        act = sb("act", [128, 128])
        z2_sb = sb("z2_sb", [128, D], F32)
        sq = sb("sq", [128, D], F32)
        st = sb("st", [128, 8], F32)
        psO = ps("psO_spo", [128, 512])
        psA = ps("psA_spo", [128, 128])
        psZ0 = ps("psZ0", [128, 512])
        psZ1 = ps("psZ1", [128, 512])
        psT = ps("psT_spo", [128, 128], BF16)

        ser.dmag(lambda e: e.dma_start(out=ident[:], in_=id_d[:]))
        ser.dmag(lambda e: e.dma_start(out=wo_sb[:], in_=wo_d.rearrange("(c p) n -> p c n", p=128)))
        ser.dmag(lambda e: e.dma_start(out=w1_sb[:], in_=w1_d.rearrange("(c p) n -> p c n", p=128)))
        ser.dmag(lambda e: e.dma_start(out=w2_sb[:], in_=w2_d.rearrange("(c p) n -> p c n", p=128)))
        ser.dmag(lambda e: e.dma_start(out=b1_r[:], in_=b1_d[:]))
        ser.vector(lambda e: e.memset(onesf[:], 1.0))
        ser.vector(lambda e: e.memset(ones[:], 1.0))
        for row_d, dst in ((g2_d, g2b), (b2_d, b2b)):
            ser.dma(lambda e, row_d=row_d: e.dma_start(out=gbounce[:], in_=row_d[:]))
            for nb in range(2):
                nsl = slice(512 * nb, 512 * nb + 512)
                ser.tensor(lambda e, nsl=nsl: e.matmul(
                    psO[:], onesf[:, :], gbounce[:, nsl], start=True, stop=True))
                ser.vector(lambda e, dst=dst, nsl=nsl: e.tensor_copy(dst[:, nsl], psO[:]))

        tiles = {"sq": sq, "st": st}
        NT = RPC // 128
        for i in range(NT):
            r = slice(128 * i, 128 * i + 128)
            tsl = slice(128 * i, 128 * i + 128)
            ser.dma(lambda e, r=r: e.dma_start(out=z_sb[:], in_=z_d[r, :]))
            ser.dmag(lambda e, tsl=tsl: e.dma_start(
                out=attf_t[:], in_=att_d.rearrange("(c p) t -> p c t", p=128)[:, :, tsl]))
            for nb in range(2):
                nsl = slice(512 * nb, 512 * nb + 512)
                for c in range(8):
                    ser.tensor(lambda e, c=c, nsl=nsl: e.matmul(
                        psO[:], attf_t[:, c, :], wo_sb[:, c, nsl], start=(c == 0), stop=(c == 7)))
                ser.vector(lambda e, nsl=nsl: e.tensor_tensor(
                    z1_sb[:, nsl], z_sb[:, nsl], psO[:], ALU.add))
            ln_tile(ser, None, z1_sb, zl2a, tiles)
            ser.vector(lambda e: e.tensor_tensor(zl2a[:], zl2a[:], g2b[:], ALU.mult))
            ser.vector(lambda e: e.tensor_tensor(zl2b[:], zl2a[:], b2b[:], ALU.add))
            for c in range(8):
                ser.tensor(lambda e, c=c: e.transpose(psT[:], zl2b[:, 128 * c:128 * c + 128], ident[:]))
                ser.vector(lambda e, c=c: e.tensor_copy(zlf[:, c, :], psT[:]))
            for mh in range(32):
                for c in range(8):
                    ser.tensor(lambda e, c=c, mh=mh: e.matmul(
                        psA[:], w1_sb[:, c, 128 * mh:128 * mh + 128], zlf[:, c, :],
                        start=(c == 0), stop=False))
                ser.tensor(lambda e, mh=mh: e.matmul(
                    psA[:], b1_r[:, 128 * mh:128 * mh + 128], ones[:, 0:128], start=False, stop=True))
                ser.scalar(lambda e: e.activation(sg[:], psA[:], AF.Sigmoid, scale=1.702))
                ser.vector(lambda e: e.tensor_tensor(act[:], sg[:], psA[:], ALU.mult))
                for nb, psZ in ((0, psZ0), (1, psZ1)):
                    nsl = slice(512 * nb, 512 * nb + 512)
                    ser.tensor(lambda e, mh=mh, nsl=nsl, psZ=psZ: e.matmul(
                        psZ[:], act[:], w2_sb[:, mh, nsl], start=(mh == 0), stop=(mh == 31)))
            for nb, psZ in ((0, psZ0), (1, psZ1)):
                nsl = slice(512 * nb, 512 * nb + 512)
                ser.vector(lambda e, nsl=nsl, psZ=psZ: e.tensor_tensor(
                    z2_sb[:, nsl], z1_sb[:, nsl], psZ[:], ALU.add))
            ser.dma(lambda e, r=r: e.dma_start(out=z2_o[r, :], in_=z2_sb[:]))
        ser.emit(nc)
    return nc


def run_subpost(nc, z, att_f, inp, li):
    in_maps = []
    for c in range(NCORE):
        r = slice(RPC * c, RPC * c + RPC)
        in_maps.append({
            "z_rows": np.ascontiguousarray(z[r]),
            "att_sl": np.ascontiguousarray(att_f[:, r]),
            "wo": np.asarray(inp["L1_attn_wo"], np.float32)[li],
            "g2_row": np.asarray(inp["L1_attn_ln2_g"], np.float32)[li].reshape(1, D),
            "b2_row": np.asarray(inp["L1_attn_ln2_b"], np.float32)[li].reshape(1, D),
            "w1": np.asarray(inp["L1_attn_w1"], np.float32)[li],
            "b1": np.asarray(inp["L1_attn_b1"], np.float32)[li].reshape(1, 4 * D),
            "w2": np.asarray(inp["L1_attn_w2"], np.float32)[li],
            "ident": IDENT,
        })
    res = _run(nc, in_maps)
    return np.concatenate([np.asarray(r_["z2"]) for r_ in res], axis=0)



QTH = (0.25, 0.75, 1.25, 1.75, 2.5, 3.5, 5.0)


def emit_quant(ser, src, dst, W, scr):
    scrW = scr["aw"].shape[1]
    if W > scrW:
        for a in range(0, W, scrW):
            w = min(scrW, W - a)
            emit_quant(ser, src[:, a:a + w], dst[:, a:a + w], w, scr)
        return
    nb = W // 16
    aw = scr["aw"][:, 0:W]
    ind = scr["ind"][:, 0:W]
    acc = scr["acc"][:, 0:W]
    sgn = scr["sgn"][:, 0:W]
    sblk = scr["sblk"][:, 0:nb]
    aw3 = aw.rearrange("p (b j) -> p b j", j=16)
    ind3 = ind.rearrange("p (b j) -> p b j", j=16)
    s3 = sblk[:, :, None].broadcast_to([128, nb, 16])
    ser.scalar(lambda e: e.activation(aw, src, AF.Abs))
    ser.scalar(lambda e: e.activation(sgn, src, AF.Sign))
    ser.vector(lambda e: e.tensor_reduce(sblk, aw3, AX.X, ALU.max))
    ser.vector(lambda e: e.tensor_scalar_max(sblk, sblk, 6e-12))
    ser.vector(lambda e: e.tensor_scalar_mul(aw, aw, 24.0))
    U = (1.0, 3.0, 5.0, 7.0, 10.0, 14.0, 20.0)
    ser.vector(lambda e: e.scalar_tensor_tensor(
        out=acc.rearrange("p (b j) -> p b j", j=16), in0=s3, scalar=U[0], in1=aw3,
        op0=ALU.mult, op1=ALU.is_lt))
    for u in U[1:4]:
        ser.vector(lambda e, u=u: e.scalar_tensor_tensor(
            out=ind3, in0=s3, scalar=u, in1=aw3, op0=ALU.mult, op1=ALU.is_lt))
        ser.vector(lambda e: e.tensor_tensor(acc, acc, ind, ALU.add))
    for u, w in ((U[4], 2.0), (U[5], 2.0), (U[6], 4.0)):
        ser.vector(lambda e, u=u: e.scalar_tensor_tensor(
            out=ind3, in0=s3, scalar=u, in1=aw3, op0=ALU.mult, op1=ALU.is_lt))
        ser.vector(lambda e, w=w: e.scalar_tensor_tensor(
            out=acc, in0=ind, scalar=w, in1=acc, op0=ALU.mult, op1=ALU.add))
    ser.vector(lambda e: e.tensor_tensor(ind, acc, sgn, ALU.mult))
    ser.vector(lambda e: e.scalar_tensor_tensor(
        out=dst.rearrange("p (b j) -> p b j", j=16), in0=s3, scalar=1.0 / 12.0, in1=ind3,
        op0=ALU.mult, op1=ALU.mult))


def build_p9(nt=None):
    HI = 2816
    nc = bass.Bass("TRN2", target_bir_lowering=False)
    xn1_d = nc.dram_tensor("xn1_rows", [RPC, D], F32, kind="ExternalInput").ap()
    x1_d = nc.dram_tensor("x1_rows", [RPC, D], F32, kind="ExternalInput").ap()
    z4_d = nc.dram_tensor("z4_rows", [RPC, D], F32, kind="ExternalInput").ap()
    wg_d = nc.dram_tensor("w_gate", [HI, D], F32, kind="ExternalInput").ap()
    wu_d = nc.dram_tensor("w_up", [HI, D], F32, kind="ExternalInput").ap()
    wd_d = nc.dram_tensor("w_down", [D, HI], F32, kind="ExternalInput").ap()
    id_d = nc.dram_tensor("ident", [128, 128], F32, kind="ExternalInput").ap()
    out_o = nc.dram_tensor("out_rows", [RPC, D], F32, kind="ExternalOutput").ap()

    ser = Ser()
    with ExitStack() as ctx:
        sb = lambda name, shape, dt=BF16: ctx.enter_context(nc.sbuf_tensor(name, shape, dt))
        ps = lambda name, shape, dt=F32: ctx.enter_context(nc.psum_tensor(name, shape, dt))
        ident = sb("ident_s", [128, 128])
        wgqT = sb("wgqT", [128, 8, HI])
        wuqT = sb("wuqT", [128, 8, HI])
        wdqT = sb("wdqT", [128, 22, D])
        SW = HI // 2
        stage = sb("stage_c", [128, SW], F32)
        wqc = sb("wqc", [128, HI])
        scr = {"aw": sb("aw", [128, SW], F32), "ind": sb("ind_q", [128, SW]),
               "acc": sb("acc_q", [128, SW]), "sgn": sb("sgn_q", [128, SW]),
               "sblk": sb("sblk", [128, SW // 16], F32), "six": sb("six_q", [128, 1], F32),
               "awb": sb("awb_q", [128, SW]), "sb16": sb("sb16_q", [128, SW // 16])}
        xq = sb("xq", [128, D])
        xqf = sb("xqf", [128, 8, 128])
        h_t = sb("h_t", [128, HI], F32)
        hq = sb("hq", [128, HI])
        hqf = sb("hqf", [128, 22, 128])
        sgh = sb("sgh", [128, 512])
        x1_sb = sb("x1_sb", [128, D], F32)
        z4_sb = sb("z4_sb", [128, D], F32)
        xn1_sb = sb("xn1_sb", [128, D], F32)
        out_sb = sb("out_sb", [128, D], F32)
        psG = ps("psG", [128, 512])
        psU = ps("psU", [128, 512])
        psD = ps("psD", [128, 512])
        psT = ps("psT_p9", [128, 128], BF16)

        ser.dmag(lambda e: e.dma_start(out=ident[:], in_=id_d[:]))
        for (w_d, wT, ncw, W) in ((wg_d, wgqT, 22, D), (wu_d, wuqT, 22, D), (wd_d, wdqT, 8, HI)):
            for co in range(ncw):
                for a in range(0, W, SW):
                    w_ = min(SW, W - a)
                    ser.dma(lambda e, w_d=w_d, co=co, a=a, w_=w_: e.dma_start(
                        out=stage[:, 0:w_], in_=w_d[128 * co:128 * co + 128, a:a + w_]))
                    emit_quant(ser, stage[:, 0:w_], wqc[:, a:a + w_], w_, scr)
                for ci in range(W // 128):
                    ser.tensor(lambda e, ci=ci: e.transpose(
                        psT[:], wqc[:, 128 * ci:128 * ci + 128], ident[:]))
                    ser.vector(lambda e, ci=ci, co=co, wT=wT: e.tensor_copy(
                        wT[:, ci, 128 * co:128 * co + 128], psT[:]))

        gate_blocks = [(0, 512), (512, 512), (1024, 512), (1536, 512), (2048, 512), (2560, 256)]
        NT = nt if nt is not None else RPC // 128
        for i in range(NT):
            r = slice(128 * i, 128 * i + 128)
            ser.dma(lambda e, r=r: e.dma_start(out=xn1_sb[:], in_=xn1_d[r, :]))
            ser.dma(lambda e, r=r: e.dma_start(out=x1_sb[:], in_=x1_d[r, :]))
            ser.dma(lambda e, r=r: e.dma_start(out=z4_sb[:], in_=z4_d[r, :]))
            emit_quant(ser, xn1_sb[:, :], xq[:, :], D, scr)
            for c in range(8):
                ser.tensor(lambda e, c=c: e.transpose(psT[:], xq[:, 128 * c:128 * c + 128], ident[:]))
                ser.vector(lambda e, c=c: e.tensor_copy(xqf[:, c, :], psT[:]))
            for o0, ow in gate_blocks:
                osl = slice(o0, o0 + ow)
                for c in range(8):
                    ser.tensor(lambda e, c=c, osl=osl, ow=ow: e.matmul(
                        psG[:, 0:ow], xqf[:, c, :], wgqT[:, c, osl], start=(c == 0), stop=(c == 7)))
                for c in range(8):
                    ser.tensor(lambda e, c=c, osl=osl, ow=ow: e.matmul(
                        psU[:, 0:ow], xqf[:, c, :], wuqT[:, c, osl], start=(c == 0), stop=(c == 7)))
                ser.scalar(lambda e, ow=ow: e.activation(sgh[:, 0:ow], psG[:, 0:ow], AF.Silu))
                ser.vector(lambda e, osl=osl, ow=ow: e.tensor_tensor(
                    h_t[:, osl], sgh[:, 0:ow], psU[:, 0:ow], ALU.mult))
            emit_quant(ser, h_t[:, :], hq[:, :], HI, scr)
            for c in range(22):
                ser.tensor(lambda e, c=c: e.transpose(psT[:], hq[:, 128 * c:128 * c + 128], ident[:]))
                ser.vector(lambda e, c=c: e.tensor_copy(hqf[:, c, :], psT[:]))
            for nb in range(2):
                nsl = slice(512 * nb, 512 * nb + 512)
                for c in range(22):
                    ser.tensor(lambda e, c=c, nsl=nsl: e.matmul(
                        psD[:], hqf[:, c, :], wdqT[:, c, nsl], start=(c == 0), stop=(c == 21)))
                ser.vector(lambda e, nsl=nsl: e.tensor_tensor(
                    out_sb[:, nsl], x1_sb[:, nsl], z4_sb[:, nsl], ALU.add))
                ser.vector(lambda e, nsl=nsl: e.tensor_tensor(
                    out_sb[:, nsl], out_sb[:, nsl], psD[:], ALU.add))
            ser.dma(lambda e, r=r: e.dma_start(out=out_o[r, :], in_=out_sb[:]))
        ser.emit(nc)
    return nc


def run_p9(xn1, x1, z4, inp):
    nc = build_p9()
    in_maps = []
    for c in range(NCORE):
        r = slice(RPC * c, RPC * c + RPC)
        in_maps.append({
            "xn1_rows": np.ascontiguousarray(xn1[r]),
            "x1_rows": np.ascontiguousarray(x1[r]),
            "z4_rows": np.ascontiguousarray(z4[r]),
            "w_gate": np.asarray(inp["L1_ffn_w_gate"], np.float32),
            "w_up": np.asarray(inp["L1_ffn_w_up"], np.float32),
            "w_down": np.asarray(inp["L1_ffn_w_down"], np.float32),
            "ident": IDENT,
        })
    res = _run(nc, in_maps)
    return np.concatenate([np.asarray(r_["out_rows"]) for r_ in res], axis=0)



def build_fused():
    HI = 2816
    nc = bass.Bass("TRN2", target_bir_lowering=False)
    I = lambda n, s: nc.dram_tensor(n, s, F32, kind="ExternalInput").ap()
    xfull_d = I("x_full", [ROWS, D])
    xb_d = I("x_batch", [T, D])
    Wq_d, bq_d = I("Wq", [D, D]), I("bq", [1, D])
    Wk_d, bk_d = I("Wk", [D, 256]), I("bk", [1, 256])
    Wv_d, bv_d = I("Wv", [D, 256]), I("bv", [1, 256])
    Wo_d, bo_d = I("Wo", [D, D]), I("bo", [1, D])
    wrec_d = {nm: I(nm, [D, D]) for nm in ("wf", "uf", "wi", "ui", "wc", "uc")}
    brec_d = {nm: I(nm, [1, D]) for nm in ("bf", "bi", "bc")}
    inc_d = I("in_c", [128, 8])
    wof_d, uof_d, bof_d = I("wof", [D, D]), I("uof", [D, D]), I("bof", [1, D])
    l1 = []
    for li in range(2):
        l1.append({
            "g1": I(f"g1_{li}", [1, D]), "b1": I(f"b1r_{li}", [1, D]),
            "wq": I(f"wq{li}", [D, D]), "wk": I(f"wk{li}", [D, D]), "wv": I(f"wv{li}", [D, D]),
            "bq": I(f"bqr{li}", [1, D]), "bk": I(f"bkr{li}", [1, D]), "bv": I(f"bvr{li}", [1, D]),
            "wo": I(f"wo{li}", [D, D]), "g2": I(f"g2_{li}", [1, D]), "b2": I(f"b2_{li}", [1, D]),
            "w1": I(f"w1_{li}", [D, 4 * D]), "b1f": I(f"b1f_{li}", [1, 4 * D]),
            "w2": I(f"w2_{li}", [4 * D, D]),
        })
    wg_d, wu_d, wd_d = I("w_gate", [HI, D]), I("w_up", [HI, D]), I("w_down", [D, HI])
    cc_d, ss_d, tb_d = I("cc0", [128, T]), I("ss0", [128, T]), I("tbl1", [128, T])
    m_d = I("masks", [128, 4, 512])
    id_d = I("ident", [128, 128])
    out_o = nc.dram_tensor("out_full", [T, D], F32, kind="ExternalOutput").ap()

    S = lambda n, s, dt=BF16: nc.dram_tensor(n, s, dt).ap()
    s_cf = S("s_cf", [D, T])
    s_q0 = S("s_q0", [D, T])
    s_k0 = S("s_k0", [256, T])
    s_v0 = S("s_v0", [T, 256])
    s_att = S("s_att", [D, T])
    s_x1 = S("s_x1", [T, D], F32)
    s_xn1 = S("s_xn1", [T, D], F32)
    s_z2 = S("s_z2", [T, D], F32)
    s_z4 = S("s_z4", [T, D], F32)
    s_qf = S("s_qf", [D, T])
    s_kf = S("s_kf", [D, T])
    s_vt = S("s_vt", [T, D])

    NTB = T // 128
    NBB = T // 512

    with ExitStack() as sctx:
        sems = {k: sctx.enter_context(nc.semaphore("fs_" + k)) for k in Ser.KEYS}
        acc = {k: 0 for k in Ser.KEYS}

        with ExitStack() as ctx:
            sb = lambda name, shape, dt=BF16: ctx.enter_context(nc.sbuf_tensor(name, shape, dt))
            ps = lambda name, shape, dt=F32: ctx.enter_context(nc.psum_tensor(name, shape, dt))
            ser = Ser()
            ident = sb("f2_id", [128, 128])
            stage = sb("f2_stage", [128, 8, D])
            wT = {nm: sb("f2_" + nm, [128, 8, D]) for nm in wrec_d}
            brow = {nm: sb("f2_" + nm, [1, D]) for nm in brec_d}
            inc_sb = sb("f2_inc", [128, 8], F32)
            ones = sb("f2_ones", [1, 512])
            xf = sb("f2_xf", [128, 8, 512])
            xld2 = [sb("f2_xld0", [128, D], F32), sb("f2_xld1", [128, D], F32)]
            xnb = sb("f2_xnb", [128, D])
            sq = sb("f2_sq", [128, D], F32)
            st = sb("f2_st", [128, 8], F32)
            cf32 = sb("f2_c", [128, 8, 512], F32)
            cb16 = sb("f2_cb", [128, 8, 512])
            cnew = sb("f2_cn", [128, 8, 512], F32)
            gsb = sb("f2_g", [128, 8, 512], F32)
            gch2 = [sb("f2_gch0", [128, 512], F32), sb("f2_gch1", [128, 512], F32)]
            tmp2 = [sb("f2_tmp0", [128, 512], F32), sb("f2_tmp1", [128, 512], F32)]
            psA2 = [ps("f2_psA0", [128, 512]), ps("f2_psA1", [128, 512])]
            psT = [ps("f2_psT0", [128, 128], BF16), ps("f2_psT1", [128, 128], BF16)]

            ser.dmag(lambda e: e.dma_start(out=ident[:], in_=id_d[:]))
            for nm in wrec_d:
                emit_load_wT(ser, nc, wrec_d[nm], stage, wT[nm], ident, psT)
            for nm in brec_d:
                ser.dmag(lambda e, nm=nm: e.dma_start(out=brow[nm][:], in_=brec_d[nm][:]))
            ser.dma(lambda e: e.dma_start(out=inc_sb[:], in_=inc_d[:]))
            ser.vector(lambda e: e.memset(ones[:], 1.0))
            tiles = {"sq": sq, "st": st}
            tile_marks = {}

            def gate_psum(wname, uname, bname, m, psA, w0):
                for ci in range(8):
                    ser.tensor(lambda e, ci=ci, m=m, wname=wname, psA=psA: e.matmul(
                        psA[:], wT[wname][:, ci, 128 * m:128 * m + 128], xf[:, ci, :],
                        start=(ci == 0), stop=False), wait=w0)
                for ci in range(8):
                    ser.tensor(lambda e, ci=ci, m=m, uname=uname, psA=psA: e.matmul(
                        psA[:], wT[uname][:, ci, 128 * m:128 * m + 128], cb16[:, ci, :],
                        start=False, stop=False), wait=w0)
                ser.tensor(lambda e, m=m, bname=bname, psA=psA: e.matmul(
                    psA[:], brow[bname][:, 128 * m:128 * m + 128], ones[:], start=False, stop=True), wait=w0)

            for blk in range(NBB):
                for m in range(8):
                    ser.vector(lambda e, m=m: e.memset(cf32[:, m, :], 0.0))
                    ser.vector(lambda e, m=m: e.tensor_scalar_add(cf32[:, m, :], cf32[:, m, :], inc_sb[:, m:m + 1]))
                    ser.vector(lambda e, m=m: e.tensor_copy(cb16[:, m, :], cf32[:, m, :]))
                for t in range(4):
                    for rt in range(4):
                        row = t * T + 512 * blk + 128 * rt
                        r = slice(row, row + 128)
                        k_ld = 4 * t + rt
                        xld = xld2[k_ld % 2]
                        w_ld = tile_marks.get(("f2", blk, k_ld - 1))
                        ser.dma(lambda e, r=r, xld=xld: e.dma_start(out=xld[:], in_=xfull_d[r, :]), wait=w_ld)
                        tile_marks[("f2", blk, k_ld)] = ser.mark()
                        ln_tile(ser, None, xld, xnb, tiles)
                        w0t = ser.mark()
                        emit_tr(ser, [(xnb[:, 128 * c:128 * c + 128], xf[:, c, 128 * rt:128 * rt + 128])
                                      for c in range(8)], psT, ident, w0=w0t)
                    mdone = {}
                    for m in range(8):
                        pi = m % 2
                        gate_psum("wc", "uc", "bc", m, psA2[pi], mdone.get(m - 2))
                        mk = ser.mark()
                        ser.scalar(lambda e, m=m, pi=pi: e.activation(
                            gsb[:, m, :], psA2[pi][:], AF.Tanh), wait=mk)
                        mdone[m] = ser.mark()
                    mdone = {}
                    for m in range(8):
                        pi = m % 2
                        gate_psum("wf", "uf", "bf", m, psA2[pi], mdone.get(m - 2))
                        mk = ser.mark()
                        ser.scalar(lambda e, pi=pi: e.activation(
                            gch2[pi][:], psA2[pi][:], AF.Sigmoid), wait=mk)
                        mk2 = ser.mark()
                        ser.vector(lambda e, m=m, pi=pi: e.tensor_tensor(
                            cnew[:, m, :], gch2[pi][:], cf32[:, m, :], ALU.mult), wait=mk2)
                        mdone[m] = ser.mark()
                    mdone = {}
                    for m in range(8):
                        pi = m % 2
                        gate_psum("wi", "ui", "bi", m, psA2[pi], mdone.get(m - 2))
                        mk = ser.mark()
                        ser.scalar(lambda e, pi=pi: e.activation(
                            gch2[pi][:], psA2[pi][:], AF.Sigmoid), wait=mk)
                        mk2 = ser.mark()
                        ser.vector(lambda e, m=m, pi=pi: e.tensor_tensor(
                            tmp2[pi][:], gch2[pi][:], gsb[:, m, :], ALU.mult), wait=mk2)
                        ser.vector(lambda e, m=m, pi=pi: e.tensor_tensor(
                            cf32[:, m, :], cnew[:, m, :], tmp2[pi][:], ALU.add))
                        mdone[m] = ser.mark()
                    for m in range(8):
                        ser.vector(lambda e, m=m: e.tensor_copy(cb16[:, m, :], cf32[:, m, :]))
                for m in range(8):
                    ser.dmag(lambda e, m=m, blk=blk: e.dma_start(
                        out=s_cf[128 * m:128 * m + 128, 512 * blk:512 * blk + 512],
                        in_=cf32[:, m, :]))
            ser.emit_block(nc, sems, acc)

        with ExitStack() as ctx:
            sb = lambda name, shape, dt=BF16: ctx.enter_context(nc.sbuf_tensor(name, shape, dt))
            ps = lambda name, shape, dt=F32: ctx.enter_context(nc.psum_tensor(name, shape, dt))
            ser = Ser()
            ident = sb("f3_id", [128, 128])
            wq = sb("f3_wq", [128, 8, D])
            wk = sb("f3_wk", [128, 8, 256])
            wv = sb("f3_wv", [128, 8, 256])
            bqs = sb("f3_bq", [1, D])
            bks = sb("f3_bk", [1, 256])
            bvs = sb("f3_bv", [1, 256])
            ones = sb("f3_ones", [1, 512])
            cc = sb("f3_cc", [128, T])
            ss = sb("f3_ss", [128, T])
            xldt = sb("f3_xld", [128, D], F32)
            xnt = sb("f3_xnt", [128, D])
            sqf3 = sb("f3_sq", [128, D], F32)
            stf3 = sb("f3_st", [128, 8], F32)
            xnf = sb("f3_xnf", [128, 8, T])
            qraw = sb("f3_qraw", [128, 512])
            qshuf = sb("f3_qshuf", [128, 512])
            t1 = sb("f3_t1", [128, 512])
            t2 = sb("f3_t2", [128, 512])
            stage = sb("f3_stage", [128, T])
            v_sb = sb("f3_v", [128, 256])
            psA = ps("f3_psA", [128, 512])
            psT = [ps("f3_psT0", [128, 128], BF16), ps("f3_psT1", [128, 128], BF16)]
            psV = ps("f3_psV", [128, 256])

            ser.dmag(lambda e: e.dma_start(out=ident[:], in_=id_d[:]))
            ser.dmag(lambda e: e.dma_start(out=wq[:], in_=Wq_d.rearrange("(c p) n -> p c n", p=128)))
            ser.dmag(lambda e: e.dma_start(out=wk[:], in_=Wk_d.rearrange("(c p) n -> p c n", p=128)))
            ser.dmag(lambda e: e.dma_start(out=wv[:], in_=Wv_d.rearrange("(c p) n -> p c n", p=128)))
            ser.dmag(lambda e: e.dma_start(out=bqs[:], in_=bq_d[:]))
            ser.dmag(lambda e: e.dma_start(out=bks[:], in_=bk_d[:]))
            ser.dmag(lambda e: e.dma_start(out=bvs[:], in_=bv_d[:]))
            ser.dmag(lambda e: e.dma_start(out=cc[:], in_=cc_d[:]))
            ser.dmag(lambda e: e.dma_start(out=ss[:], in_=ss_d[:]))
            ser.vector(lambda e: e.memset(ones[:], 1.0))
            tilesf3 = {"sq": sqf3, "st": stf3}
            for i in range(NTB):
                ser.dma(lambda e, i=i: e.dma_start(out=xldt[:], in_=xb_d[128 * i:128 * i + 128, :]))
                ln_tile(ser, None, xldt, xnt, tilesf3)
                w0t = ser.mark()
                emit_tr(ser, [(xnt[:, 128 * c:128 * c + 128], xnf[:, c, 128 * i:128 * i + 128])
                              for c in range(8)], psT, ident, w0=w0t)
            gidx = 0
            gdone = {}
            for (wsb, bsb, nm_, out_d) in ((wq, bqs, 8, s_q0), (wk, bks, 2, s_k0)):
                for m in range(nm_):
                    msl = slice(128 * m, 128 * m + 128)
                    for n in range(T // 512):
                        nsl = slice(512 * n, 512 * n + 512)
                        pg = psA2[gidx % 2]
                        w0 = gdone.get(gidx - 2, 0) or ser.mark()
                        for c in range(8):
                            ser.tensor(lambda e, c=c, msl=msl, nsl=nsl, wsb=wsb, pg=pg: e.matmul(
                                pg[:], wsb[:, c, msl], xnf[:, c, nsl], start=(c == 0), stop=False), wait=w0)
                        ser.tensor(lambda e, msl=msl, bsb=bsb, pg=pg: e.matmul(
                            pg[:], bsb[:, msl], ones[:], start=False, stop=True), wait=w0)
                        ser.vector(lambda e, pg=pg: e.tensor_copy(qraw[:], pg[:]))
                        gdone[gidx] = ser.mark()
                        gidx += 1
                        ser.vector(lambda e: e.stream_shuffle(qshuf[:], qraw[:], SWAPMASK))
                        ser.vector(lambda e, nsl=nsl: e.tensor_tensor(t1[:], qraw[:], cc[:, nsl], ALU.mult))
                        ser.vector(lambda e, nsl=nsl: e.tensor_tensor(t2[:], qshuf[:], ss[:, nsl], ALU.mult))
                        ser.vector(lambda e, nsl=nsl: e.tensor_tensor(stage[:, nsl], t1[:], t2[:], ALU.add))
                    ser.dma(lambda e, msl=msl, out_d=out_d: e.dma_start(out=out_d[msl, :], in_=stage[:]))
            for i in range(NTB):
                w0 = ser.mark()
                for c in range(8):
                    ser.tensor(lambda e, c=c, i=i: e.matmul(
                        psV[:], xnf[:, c, 128 * i:128 * i + 128], wv[:, c, :],
                        start=(c == 0), stop=False), wait=w0)
                ser.tensor(lambda e: e.matmul(psV[:], ones[:, 0:128], bvs[:], start=False, stop=True), wait=w0)
                ser.vector(lambda e: e.tensor_copy(v_sb[:], psV[:]))
                ser.dma(lambda e, i=i: e.dma_start(out=s_v0[128 * i:128 * i + 128, :], in_=v_sb[:]))
            ser.emit_block(nc, sems, acc)

        with ExitStack() as ctx:
            sb = lambda name, shape, dt=BF16: ctx.enter_context(nc.sbuf_tensor(name, shape, dt))
            ps = lambda name, shape, dt=F32: ctx.enter_context(nc.psum_tensor(name, shape, dt))
            ser = Ser()
            q_sb = sb("fb_q", [128, 8, T])
            k2 = sb("fb_k2", [128, 4, T])
            v_sb = sb("fb_v", [128, 32, 256])
            vaug = sb("fb_vaug", [128, 32, 260])
            mask = sb("fb_mask", [128, 4, 512])
            sq = sb("fb_sq", [128, T])
            pa3 = [sb("fb_pa0", [128, 1024]), sb("fb_pa1", [128, 1024])]
            araw = sb("fb_araw", [64, 512], F32)
            asb = sb("fb_asb", [64, 512])
            rec = sb("fb_rec", [1, 512], F32)
            onesb = sb("fb_onesb", [128, 1])
            onesf = sb("fb_onesf", [1, 128], F32)
            hsc = sb("fb_hsc", [1, 16], F32)
            ksc = sb("fb_ksc", [1, 8], F32)
            bias_sb = sb("fb_bias", [128, 8], F32)
            psS3 = [ps("fb_psS0", [128, 1024]), ps("fb_psS1", [128, 1024])]
            psO_A = ps("fb_psOA", [128, 512])
            psO_B = ps("fb_psOB", [128, 512])
            psR = ps("fb_psR", [128, 512])
            psQ = ps("fb_psQ", [128, 512])

            ser.dma(lambda e: e.dma_start(out=q_sb[:], in_=s_q0.rearrange("(c p) t -> p c t", p=128)))
            for g in range(4):
                for half in range(2):
                    ser.dma(lambda e, g=g, half=half: e.dma_start(
                        out=k2[64 * half:64 * half + 64, g, :], in_=s_k0[64 * g:64 * g + 64, :]))
            ser.dma(lambda e: e.dma_start(out=v_sb[:], in_=s_v0.rearrange("(kt p) d -> p kt d", p=128)))
            ser.dmag(lambda e: e.dma_start(out=mask[:], in_=m_d[:]))
            ser.vector(lambda e: e.memset(vaug[:], 1.0))
            for kt in range(32):
                for g in range(4):
                    ser.vector(lambda e, kt=kt, g=g: e.tensor_copy(
                        vaug[:, kt, 65 * g:65 * g + 64], v_sb[:, kt, 64 * g:64 * g + 64]))
            ser.vector(lambda e: e.memset(onesb[:], 1.0))
            ser.vector(lambda e: e.memset(onesf[:], 1.0))
            ser.vector(lambda e: e.memset(hsc[:], 0.0))
            ser.vector(lambda e: e.memset(ksc[:], 0.0))
            for c in range(8):
                ser.scalar(lambda e, c=c: e.activation(sq[:], q_sb[:, c, :], AF.Square))
                for n in range(T // 512):
                    nsl = slice(512 * n, 512 * n + 512)
                    ser.tensor(lambda e, nsl=nsl: e.matmul(
                        psQ[0:1, :], onesb[:, :], sq[:, nsl], start=True, stop=True))
                    ser.vector(lambda e: e.reduce_max(rec[0:1, 0:1], psQ[0:1, :], AX.X))
                    ser.vector(lambda e, c=c: e.tensor_tensor(
                        hsc[0:1, c:c + 1], hsc[0:1, c:c + 1], rec[0:1, 0:1], ALU.max))
            for g in range(4):
                ser.scalar(lambda e, g=g: e.activation(sq[:], k2[:, g, :], AF.Square))
                for n in range(T // 512):
                    nsl = slice(512 * n, 512 * n + 512)
                    ser.tensor(lambda e, nsl=nsl: e.matmul(
                        psQ[0:1, :], onesb[:, :], sq[:, nsl], start=True, stop=True))
                    ser.vector(lambda e: e.reduce_max(rec[0:1, 0:1], psQ[0:1, :], AX.X))
                    ser.vector(lambda e, g=g: e.tensor_tensor(
                        ksc[0:1, g:g + 1], ksc[0:1, g:g + 1], rec[0:1, 0:1], ALU.max))
            ser.scalar(lambda e: e.activation(hsc[0:1, 8:16], hsc[0:1, 0:8], AF.Sqrt))
            ser.scalar(lambda e: e.activation(ksc[0:1, 4:8], ksc[0:1, 0:4], AF.Sqrt))
            for p in range(8):
                g = p // 2
                ser.vector(lambda e, p=p, g=g: e.tensor_tensor(
                    hsc[0:1, p:p + 1], hsc[0:1, 8 + p:9 + p], ksc[0:1, 4 + g:5 + g], ALU.mult))
                ser.vector(lambda e, p=p: e.tensor_scalar_mul(hsc[0:1, p:p + 1], hsc[0:1, p:p + 1], -0.125))
                ser.tensor(lambda e, p=p: e.matmul(
                    psR[:, 0:1], onesf[:, :], hsc[0:1, p:p + 1], start=True, stop=True))
                ser.vector(lambda e, p=p: e.tensor_copy(bias_sb[:, p:p + 1], psR[:, 0:1]))
            prev_tail = None
            for p in range(8):
                g = p // 2
                for j in range(T // 512):
                    jsl = slice(512 * j, 512 * j + 512)
                    nkt = 4 * j + 4
                    m_expd = {}
                    for i in range(nkt + 1):
                        pi = i % 2
                        if i < nkt:
                            isl = slice(128 * i, 128 * i + 128)
                            w_war = m_expd.get(i - 2, prev_tail)
                            ser.tensor(lambda e, g=g, isl=isl, p=p, jsl=jsl, pi=pi: e.matmul(
                                psS3[pi][:, 0:512], k2[0:64, g, isl], q_sb[0:64, p, jsl],
                                start=True, stop=True, tile_position=(0, 0)), wait=w_war)
                            ser.tensor(lambda e, g=g, isl=isl, p=p, jsl=jsl, pi=pi: e.matmul(
                                psS3[pi][:, 512:1024], k2[64:128, g, isl], q_sb[64:128, p, jsl],
                                start=True, stop=True, tile_position=(64, 0)), wait=w_war)
                            mk_mm = ser.mark()
                        if i >= 1:
                            io = i - 1
                            pio = io % 2
                            lasto = io == nkt - 1
                            wav = m_expd[io]
                            ser.tensor(lambda e, io=io, lasto=lasto, g=g, pio=pio: e.matmul(
                                psO_A[0:65, :], vaug[:, io, 65 * g:65 * g + 65], pa3[pio][:, 0:512],
                                start=(io == 0), stop=lasto), wait=wav)
                            ser.tensor(lambda e, io=io, lasto=lasto, g=g, pio=pio: e.matmul(
                                psO_B[0:65, :], vaug[:, io, 65 * g:65 * g + 65], pa3[pio][:, 512:1024],
                                start=(io == 0), stop=lasto), wait=wav)
                        if i < nkt:
                            ser.scalar(lambda e, p=p, pi=pi: e.activation(
                                pa3[pi][:], psS3[pi][:], AF.Exp, bias=bias_sb[:, p:p + 1],
                                scale=0.125), wait=mk_mm)
                            d = i - 4 * j
                            if d >= 0:
                                mkm = ser.mark()
                                ser.vector(lambda e, d=d, pi=pi: e.tensor_tensor(
                                    pa3[pi].rearrange("p (h q) -> p h q", h=2),
                                    pa3[pi].rearrange("p (h q) -> p h q", h=2),
                                    mask[:, d, :][:, None, :].broadcast_to([128, 2, 512]),
                                    ALU.mult), wait=mkm)
                            m_expd[i] = ser.mark()
                    prev_tail = ser.mark()
                    for half, psO in ((0, psO_A), (1, psO_B)):
                        h = 2 * p + half
                        ser.vector(lambda e, psO=psO: e.reciprocal(rec[:], psO[64:65, :]))
                        ser.tensor(lambda e: e.matmul(psR[0:64, :], onesf[:, 0:64], rec[:], start=True, stop=True))
                        ser.scalar(lambda e, psO=psO: e.copy(araw[:], psO[0:64, :]))
                        ser.vector(lambda e: e.tensor_tensor(asb[:], araw[:], psR[0:64, :], ALU.mult))
                        ser.dma(lambda e, h=h, jsl=jsl: e.dma_start(
                            out=s_att[64 * h:64 * h + 64, jsl], in_=asb[:]))
            ser.emit_block(nc, sems, acc)

        with ExitStack() as ctx:
            sb = lambda name, shape, dt=BF16: ctx.enter_context(nc.sbuf_tensor(name, shape, dt))
            ps = lambda name, shape, dt=F32: ctx.enter_context(nc.psum_tensor(name, shape, dt))
            ser = Ser()
            ident = sb("f4_id", [128, 128])
            stage = sb("f4_stage", [128, 8, D])
            wo_sb = sb("f4_wo", [128, 8, D])
            woT = sb("f4_woT", [128, 8, D])
            uoT = sb("f4_uoT", [128, 8, D])
            bo_r = sb("f4_bo", [1, D])
            bof_r = sb("f4_bof", [1, D])
            ones = sb("f4_ones", [1, 512])
            attf_t = sb("f4_attf", [128, 8, 128])
            cf_t = sb("f4_cf", [128, 8, 128])
            x_sb = sb("f4_x", [128, D], F32)
            xn_sb = sb("f4_xn", [128, D])
            xnf = sb("f4_xnf", [128, 8, 128])
            h_sb = sb("f4_h", [128, D], F32)
            x1_sb = sb("f4_x1", [128, D], F32)
            xn1_sb = sb("f4_xn1", [128, D], F32)
            sq = sb("f4_sq", [128, D], F32)
            st = sb("f4_st", [128, 8], F32)
            psO = ps("f4_psO", [128, 512])
            psH = ps("f4_psH", [128, 512])
            psT = [ps("f4_psT0", [128, 128], BF16), ps("f4_psT1", [128, 128], BF16)]

            ser.dmag(lambda e: e.dma_start(out=ident[:], in_=id_d[:]))
            ser.dmag(lambda e: e.dma_start(out=wo_sb[:], in_=Wo_d.rearrange("(c p) n -> p c n", p=128)))
            ser.dmag(lambda e: e.dma_start(out=bo_r[:], in_=bo_d[:]))
            ser.dmag(lambda e: e.dma_start(out=bof_r[:], in_=bof_d[:]))
            emit_load_wT(ser, nc, wof_d, stage, woT, ident, psT)
            emit_load_wT(ser, nc, uof_d, stage, uoT, ident, psT)
            ser.vector(lambda e: e.memset(ones[:], 1.0))
            tiles = {"sq": sq, "st": st}
            for i in range(NTB):
                r = slice(128 * i, 128 * i + 128)
                tsl = slice(128 * i, 128 * i + 128)
                ser.dma(lambda e, r=r: e.dma_start(out=x_sb[:], in_=xb_d[r, :]))
                ln_tile(ser, None, x_sb, xn_sb, tiles)
                ser.dma(lambda e, tsl=tsl: e.dma_start(
                    out=attf_t[:], in_=s_att.rearrange("(c p) t -> p c t", p=128)[:, :, tsl]))
                ser.dma(lambda e, tsl=tsl: e.dma_start(
                    out=cf_t[:], in_=s_cf.rearrange("(c p) t -> p c t", p=128)[:, :, tsl]))
                w0t = ser.mark()
                emit_tr(ser, [(xn_sb[:, 128 * c:128 * c + 128], xnf[:, c, :])
                              for c in range(8)], psT, ident, w0=w0t)
                for nb in range(2):
                    nsl = slice(512 * nb, 512 * nb + 512)
                    w0 = ser.mark()
                    for c in range(8):
                        ser.tensor(lambda e, c=c, nsl=nsl: e.matmul(
                            psO[:], attf_t[:, c, :], wo_sb[:, c, nsl], start=(c == 0), stop=False), wait=w0)
                    ser.tensor(lambda e, nsl=nsl: e.matmul(
                        psO[:], ones[:, 0:128], bo_r[:, nsl], start=False, stop=True), wait=w0)
                    for c in range(8):
                        ser.tensor(lambda e, c=c, nsl=nsl: e.matmul(
                            psH[:], xnf[:, c, :], woT[:, c, nsl], start=(c == 0), stop=False), wait=w0)
                    for c in range(8):
                        ser.tensor(lambda e, c=c, nsl=nsl: e.matmul(
                            psH[:], cf_t[:, c, :], uoT[:, c, nsl], start=False, stop=False), wait=w0)
                    ser.tensor(lambda e, nsl=nsl: e.matmul(
                        psH[:], ones[:, 0:128], bof_r[:, nsl], start=False, stop=True), wait=w0)
                    ser.scalar(lambda e, nsl=nsl: e.activation(h_sb[:, nsl], psH[:], AF.Tanh))
                    ser.vector(lambda e, nsl=nsl: e.tensor_tensor(
                        x1_sb[:, nsl], x_sb[:, nsl], h_sb[:, nsl], ALU.add))
                    ser.vector(lambda e, nsl=nsl: e.tensor_tensor(
                        x1_sb[:, nsl], x1_sb[:, nsl], psO[:], ALU.add))
                ser.dma(lambda e, r=r: e.dma_start(out=s_x1[r, :], in_=x1_sb[:]))
                ln_tile(ser, None, x1_sb, xn1_sb, tiles)
                ser.dma(lambda e, r=r: e.dma_start(out=s_xn1[r, :], in_=xn1_sb[:]))
            ser.emit_block(nc, sems, acc)

        def phase_subpre(w, z_scr, tag):
            with ExitStack() as ctx:
                sb = lambda name, shape, dt=BF16: ctx.enter_context(nc.sbuf_tensor(tag + name, shape, dt))
                ps = lambda name, shape, dt=F32: ctx.enter_context(nc.psum_tensor(tag + name, shape, dt))
                ser = Ser()
                ident = sb("sp_id", [128, 128])
                wqs = sb("sp_wq", [128, 8, D])
                wks = sb("sp_wk", [128, 8, D])
                wvs = sb("sp_wv", [128, 8, D])
                bqs = sb("sp_bq", [1, D])
                bks = sb("sp_bk", [1, D])
                bvs = sb("sp_bv", [1, D])
                tbl = sb("sp_tbl", [128, T])
                gb = sb("sp_gb", [128, D], F32)
                bb = sb("sp_bb", [128, D], F32)
                gbounce = sb("sp_gb1", [1, D], F32)
                onesf = sb("sp_onesf", [1, 128], F32)
                ones = sb("sp_ones", [1, 512])
                z_sb = sb("sp_z", [128, D], F32)
                zl1 = sb("sp_zl1", [128, D], F32)
                zl_sb = sb("sp_zl", [128, D])
                zlf = sb("sp_zlf", [128, 8, T])
                sq = sb("sp_sq", [128, D], F32)
                st = sb("sp_st", [128, 8], F32)
                qraw = sb("sp_qraw", [128, 512])
                stage = sb("sp_stage", [128, T])
                v_sb = sb("sp_v", [128, D])
                psA = ps("sp_psA", [128, 512])
                psT = [ps("sp_psT0", [128, 128], BF16), ps("sp_psT1", [128, 128], BF16)]

                ser.dmag(lambda e: e.dma_start(out=ident[:], in_=id_d[:]))
                for ws, wd_ in ((wqs, w["wq"]), (wks, w["wk"]), (wvs, w["wv"])):
                    ser.dmag(lambda e, ws=ws, wd_=wd_: e.dma_start(
                        out=ws[:], in_=wd_.rearrange("(c p) n -> p c n", p=128)))
                for bs, bd_ in ((bqs, w["bq"]), (bks, w["bk"]), (bvs, w["bv"])):
                    ser.dmag(lambda e, bs=bs, bd_=bd_: e.dma_start(out=bs[:], in_=bd_[:]))
                ser.dmag(lambda e: e.dma_start(out=tbl[:], in_=tb_d[:]))
                ser.vector(lambda e: e.memset(onesf[:], 1.0))
                ser.vector(lambda e: e.memset(ones[:], 1.0))
                for row_d, dst in ((w["g1"], gb), (w["b1"], bb)):
                    ser.dma(lambda e, row_d=row_d: e.dma_start(out=gbounce[:], in_=row_d[:]))
                    for nb in range(2):
                        nsl = slice(512 * nb, 512 * nb + 512)
                        ser.tensor(lambda e, nsl=nsl: e.matmul(
                            psA[:], onesf[:, :], gbounce[:, nsl], start=True, stop=True))
                        ser.vector(lambda e, dst=dst, nsl=nsl: e.tensor_copy(dst[:, nsl], psA[:]))
                tiles = {"sq": sq, "st": st}
                for i in range(NTB):
                    r = slice(128 * i, 128 * i + 128)
                    ser.dma(lambda e, r=r: e.dma_start(out=z_sb[:], in_=z_scr[r, :]))
                    ln_tile(ser, None, z_sb, zl1, tiles)
                    ser.vector(lambda e: e.tensor_tensor(zl1[:], zl1[:], gb[:], ALU.mult))
                    ser.vector(lambda e: e.tensor_tensor(zl_sb[:], zl1[:], bb[:], ALU.add))
                    w0t = ser.mark()
                    emit_tr(ser, [(zl_sb[:, 128 * c:128 * c + 128], zlf[:, c, 128 * i:128 * i + 128])
                                  for c in range(8)], psT, ident, w0=w0t)
                gidx = 0
                gdone = {}
                for (wsb, bsb, out_d) in ((wqs, bqs, s_qf), (wks, bks, s_kf)):
                    for m in range(8):
                        msl = slice(128 * m, 128 * m + 128)
                        for n in range(T // 512):
                            nsl = slice(512 * n, 512 * n + 512)
                            pg = psA2[gidx % 2]
                            w0 = gdone.get(gidx - 2, 0) or ser.mark()
                            for c in range(8):
                                ser.tensor(lambda e, c=c, msl=msl, nsl=nsl, wsb=wsb, pg=pg: e.matmul(
                                    pg[:], wsb[:, c, msl], zlf[:, c, nsl], start=(c == 0), stop=False), wait=w0)
                            ser.tensor(lambda e, msl=msl, bsb=bsb, pg=pg: e.matmul(
                                pg[:], bsb[:, msl], ones[:], start=False, stop=True), wait=w0)
                            ser.vector(lambda e, pg=pg: e.tensor_copy(qraw[:], pg[:]))
                            gdone[gidx] = ser.mark()
                            gidx += 1
                            ser.vector(lambda e, nsl=nsl: e.tensor_tensor(
                                stage[:, nsl], qraw[:], tbl[:, nsl], ALU.mult))
                        ser.dma(lambda e, msl=msl, out_d=out_d: e.dma_start(out=out_d[msl, :], in_=stage[:]))
                for i in range(NTB):
                    r = slice(128 * i, 128 * i + 128)
                    for nb in range(2):
                        nsl = slice(512 * nb, 512 * nb + 512)
                        w0 = ser.mark()
                        for c in range(8):
                            ser.tensor(lambda e, c=c, i=i, nsl=nsl: e.matmul(
                                psA[:], zlf[:, c, 128 * i:128 * i + 128], wvs[:, c, nsl],
                                start=(c == 0), stop=False), wait=w0)
                        ser.tensor(lambda e, nsl=nsl: e.matmul(
                            psA[:], ones[:, 0:128], bvs[:, nsl], start=False, stop=True), wait=w0)
                        ser.vector(lambda e, nsl=nsl: e.tensor_copy(v_sb[:, nsl], psA[:]))
                    ser.dma(lambda e, r=r: e.dma_start(out=s_vt[r, :], in_=v_sb[:]))
                ser.emit_block(nc, sems, acc)

        def phase_subatt(tag):
            with ExitStack() as ctx:
                sb = lambda name, shape, dt=BF16: ctx.enter_context(nc.sbuf_tensor(tag + name, shape, dt))
                ps = lambda name, shape, dt=F32: ctx.enter_context(nc.psum_tensor(tag + name, shape, dt))
                ser = Ser()
                q_sb = sb("sa_q", [128, 8, T])
                k_sb = sb("sa_k", [128, 8, T])
                v_p = sb("sa_vp", [128, 32, 128])
                vaug = sb("sa_vaug", [128, 32, 130])
                sq = sb("sa_sq", [128, T])
                pa3 = [sb("sa_pa0", [128, 1024]), sb("sa_pa1", [128, 1024])]
                araw = sb("sa_araw", [64, 512], F32)
                asb = sb("sa_asb", [64, 512])
                rec = sb("sa_rec", [1, 512], F32)
                onesb = sb("sa_onesb", [128, 1])
                onesf = sb("sa_onesf", [1, 128], F32)
                hsc = sb("sa_hsc", [1, 16], F32)
                ksc = sb("sa_ksc", [1, 16], F32)
                bias_sb = sb("sa_bias", [128, 8], F32)
                psS3 = [ps("sa_psS0", [128, 1024]), ps("sa_psS1", [128, 1024])]
                psO_A = ps("sa_psOA", [128, 512])
                psO_B = ps("sa_psOB", [128, 512])
                psR = ps("sa_psR", [128, 512])
                psQ = ps("sa_psQ", [128, 512])

                ser.dma(lambda e: e.dma_start(out=q_sb[:], in_=s_qf.rearrange("(c p) t -> p c t", p=128)))
                ser.dma(lambda e: e.dma_start(out=k_sb[:], in_=s_kf.rearrange("(c p) t -> p c t", p=128)))
                ser.vector(lambda e: e.memset(onesb[:], 1.0))
                ser.vector(lambda e: e.memset(onesf[:], 1.0))
                ser.vector(lambda e: e.memset(hsc[:], 0.0))
                ser.vector(lambda e: e.memset(ksc[:], 0.0))
                for src, accv in ((q_sb, hsc), (k_sb, ksc)):
                    for c in range(8):
                        ser.scalar(lambda e, c=c, src=src: e.activation(sq[:], src[:, c, :], AF.Square))
                        for n in range(T // 512):
                            nsl = slice(512 * n, 512 * n + 512)
                            ser.tensor(lambda e, nsl=nsl: e.matmul(
                                psQ[0:1, :], onesb[:, :], sq[:, nsl], start=True, stop=True))
                            ser.vector(lambda e: e.reduce_max(rec[0:1, 0:1], psQ[0:1, :], AX.X))
                            ser.vector(lambda e, c=c, accv=accv: e.tensor_tensor(
                                accv[0:1, c:c + 1], accv[0:1, c:c + 1], rec[0:1, 0:1], ALU.max))
                ser.scalar(lambda e: e.activation(hsc[0:1, 8:16], hsc[0:1, 0:8], AF.Sqrt))
                ser.scalar(lambda e: e.activation(ksc[0:1, 8:16], ksc[0:1, 0:8], AF.Sqrt))
                for p in range(8):
                    ser.vector(lambda e, p=p: e.tensor_tensor(
                        hsc[0:1, p:p + 1], hsc[0:1, 8 + p:9 + p], ksc[0:1, 8 + p:9 + p], ALU.mult))
                    ser.vector(lambda e, p=p: e.tensor_scalar_mul(hsc[0:1, p:p + 1], hsc[0:1, p:p + 1], -0.125))
                    ser.tensor(lambda e, p=p: e.matmul(
                        psR[:, 0:1], onesf[:, :], hsc[0:1, p:p + 1], start=True, stop=True))
                    ser.vector(lambda e, p=p: e.tensor_copy(bias_sb[:, p:p + 1], psR[:, 0:1]))
                prev_tail = None
                for p in range(8):
                    ser.dma(lambda e, p=p: e.dma_start(
                        out=v_p[:], in_=s_vt.rearrange("(kt pp) d -> pp kt d", pp=128)[:, :, 128 * p:128 * p + 128]))
                    ser.vector(lambda e: e.memset(vaug[:], 1.0))
                    for kt in range(32):
                        ser.vector(lambda e, kt=kt: e.tensor_copy(vaug[:, kt, 0:64], v_p[:, kt, 0:64]))
                        ser.vector(lambda e, kt=kt: e.tensor_copy(vaug[:, kt, 65:129], v_p[:, kt, 64:128]))
                    for j in range(T // 512):
                        jsl = slice(512 * j, 512 * j + 512)
                        m_expd = {}
                        for i in range(33):
                            pi = i % 2
                            if i < 32:
                                isl = slice(128 * i, 128 * i + 128)
                                w_war = m_expd.get(i - 2, prev_tail)
                                ser.tensor(lambda e, p=p, isl=isl, jsl=jsl, pi=pi: e.matmul(
                                    psS3[pi][:, 0:512], k_sb[0:64, p, isl], q_sb[0:64, p, jsl],
                                    start=True, stop=True, tile_position=(0, 0)), wait=w_war)
                                ser.tensor(lambda e, p=p, isl=isl, jsl=jsl, pi=pi: e.matmul(
                                    psS3[pi][:, 512:1024], k_sb[64:128, p, isl], q_sb[64:128, p, jsl],
                                    start=True, stop=True, tile_position=(64, 0)), wait=w_war)
                                mk_mm = ser.mark()
                            if i >= 1:
                                io = i - 1
                                pio = io % 2
                                lasto = io == 31
                                wav = m_expd[io]
                                ser.tensor(lambda e, io=io, lasto=lasto, pio=pio: e.matmul(
                                    psO_A[0:65, :], vaug[:, io, 0:65], pa3[pio][:, 0:512],
                                    start=(io == 0), stop=lasto), wait=wav)
                                ser.tensor(lambda e, io=io, lasto=lasto, pio=pio: e.matmul(
                                    psO_B[0:65, :], vaug[:, io, 65:130], pa3[pio][:, 512:1024],
                                    start=(io == 0), stop=lasto), wait=wav)
                            if i < 32:
                                ser.scalar(lambda e, p=p, pi=pi: e.activation(
                                    pa3[pi][:], psS3[pi][:], AF.Exp, bias=bias_sb[:, p:p + 1],
                                    scale=0.125), wait=mk_mm)
                                m_expd[i] = ser.mark()
                        prev_tail = ser.mark()
                        for half, psO in ((0, psO_A), (1, psO_B)):
                            h = 2 * p + half
                            ser.vector(lambda e, psO=psO: e.reciprocal(rec[:], psO[64:65, :]))
                            ser.tensor(lambda e: e.matmul(psR[0:64, :], onesf[:, 0:64], rec[:], start=True, stop=True))
                            ser.scalar(lambda e, psO=psO: e.copy(araw[:], psO[0:64, :]))
                            ser.vector(lambda e: e.tensor_tensor(asb[:], araw[:], psR[0:64, :], ALU.mult))
                            ser.dma(lambda e, h=h, jsl=jsl: e.dma_start(
                                out=s_att[64 * h:64 * h + 64, jsl], in_=asb[:]))
                ser.emit_block(nc, sems, acc)

        def phase_subpost(w, z_scr, zout_scr, tag):
            with ExitStack() as ctx:
                sb = lambda name, shape, dt=BF16: ctx.enter_context(nc.sbuf_tensor(tag + name, shape, dt))
                ps = lambda name, shape, dt=F32: ctx.enter_context(nc.psum_tensor(tag + name, shape, dt))
                ser = Ser()
                ident = sb("po_id", [128, 128])
                wo_sb = sb("po_wo", [128, 8, D])
                w1_sb = sb("po_w1", [128, 8, 4 * D])
                w2buf = sb("po_w2", [128, 32, 512])
                b1_r = sb("po_b1", [1, 4 * D])
                g2b = sb("po_g2", [128, D], F32)
                b2b = sb("po_b2", [128, D], F32)
                gbounce = sb("po_gb", [1, D], F32)
                onesf = sb("po_onesf", [1, 128], F32)
                ones = sb("po_ones", [1, 512])
                attf_t = sb("po_attf", [128, 8, 128])
                z_sb = sb("po_z", [128, D], F32)
                z1blk = sb("po_z1b", [128, 4, D], F32)
                zl2a = sb("po_zl2a", [128, D], F32)
                zl2b = sb("po_zl2b", [128, D])
                zlf = sb("po_zlf", [128, 8, 512])
                act_all = sb("po_act", [128, 32, 512])
                z2_sb = sb("po_z2", [128, 512], F32)
                st = sb("po_st", [128, 8], F32)
                sg2 = [sb("po_sg0", [128, 512]), sb("po_sg1", [128, 512])]
                psO = ps("po_psO", [128, 512])
                psA2 = [ps("po_psA0", [128, 512]), ps("po_psA1", [128, 512])]
                psZ = ps("po_psZ", [128, 512])
                psT = [ps("po_psT0", [128, 128], BF16), ps("po_psT1", [128, 128], BF16)]

                ser.dmag(lambda e: e.dma_start(out=ident[:], in_=id_d[:]))
                ser.dmag(lambda e: e.dma_start(out=wo_sb[:], in_=w["wo"].rearrange("(c p) n -> p c n", p=128)))
                ser.dmag(lambda e: e.dma_start(out=w1_sb[:], in_=w["w1"].rearrange("(c p) n -> p c n", p=128)))
                ser.dmag(lambda e: e.dma_start(out=b1_r[:], in_=w["b1f"][:]))
                ser.vector(lambda e: e.memset(onesf[:], 1.0))
                ser.vector(lambda e: e.memset(ones[:], 1.0))
                for row_d, dst in ((w["g2"], g2b), (w["b2"], b2b)):
                    ser.dma(lambda e, row_d=row_d: e.dma_start(out=gbounce[:], in_=row_d[:]))
                    for nb in range(2):
                        nsl = slice(512 * nb, 512 * nb + 512)
                        ser.tensor(lambda e, nsl=nsl: e.matmul(
                            psO[:], onesf[:, :], gbounce[:, nsl], start=True, stop=True))
                        ser.vector(lambda e, dst=dst, nsl=nsl: e.tensor_copy(dst[:, nsl], psO[:]))
                tiles = {"sq": zl2a, "st": st}
                for blk in range(NBB):
                    for tt in range(4):
                        i = 4 * blk + tt
                        r = slice(128 * i, 128 * i + 128)
                        tsl = slice(128 * i, 128 * i + 128)
                        ser.dma(lambda e, r=r: e.dma_start(out=z_sb[:], in_=z_scr[r, :]))
                        ser.dma(lambda e, tsl=tsl: e.dma_start(
                            out=attf_t[:], in_=s_att.rearrange("(c p) t -> p c t", p=128)[:, :, tsl]))
                        for nb in range(2):
                            nsl = slice(512 * nb, 512 * nb + 512)
                            w0 = ser.mark()
                            for c in range(8):
                                ser.tensor(lambda e, c=c, nsl=nsl: e.matmul(
                                    psO[:], attf_t[:, c, :], wo_sb[:, c, nsl], start=(c == 0), stop=(c == 7)), wait=w0)
                            ser.vector(lambda e, nsl=nsl, tt=tt: e.tensor_tensor(
                                z1blk[:, tt, nsl], z_sb[:, nsl], psO[:], ALU.add))
                        ln_tile(ser, None, z1blk[:, tt, :], zl2a, tiles)
                        ser.vector(lambda e: e.tensor_tensor(zl2a[:], zl2a[:], g2b[:], ALU.mult))
                        ser.vector(lambda e: e.tensor_tensor(zl2b[:], zl2a[:], b2b[:], ALU.add))
                        w0t = ser.mark()
                        emit_tr(ser, [(zl2b[:, 128 * c:128 * c + 128], zlf[:, c, 128 * tt:128 * tt + 128])
                                      for c in range(8)], psT, ident, w0=w0t)
                    mdone = {}
                    for mh in range(32):
                        pi = mh % 2
                        w0 = mdone.get(mh - 2)
                        for c in range(8):
                            ser.tensor(lambda e, c=c, mh=mh, pi=pi: e.matmul(
                                psA2[pi][:], w1_sb[:, c, 128 * mh:128 * mh + 128], zlf[:, c, :],
                                start=(c == 0), stop=False), wait=w0)
                        ser.tensor(lambda e, mh=mh, pi=pi: e.matmul(
                            psA2[pi][:], b1_r[:, 128 * mh:128 * mh + 128], ones[:], start=False, stop=True), wait=w0)
                        mk = ser.mark()
                        ser.scalar(lambda e, pi=pi: e.activation(
                            sg2[pi][:], psA2[pi][:], AF.Sigmoid, scale=1.702), wait=mk)
                        mk2 = ser.mark()
                        ser.vector(lambda e, mh=mh, pi=pi: e.tensor_tensor(
                            act_all[:, mh, :], sg2[pi][:], psA2[pi][:], ALU.mult), wait=mk2)
                        mdone[mh] = ser.mark()
                    for nb in range(2):
                        nsl = slice(512 * nb, 512 * nb + 512)
                        ser.dmag(lambda e, nsl=nsl: e.dma_start(
                            out=w2buf[:], in_=w["w2"].rearrange("(c p) n -> p c n", p=128)[:, :, nsl]))
                        for tt in range(4):
                            i = 4 * blk + tt
                            r = slice(128 * i, 128 * i + 128)
                            w0 = ser.mark()
                            for mh in range(32):
                                ser.tensor(lambda e, mh=mh, tt=tt: e.matmul(
                                    psZ[:], act_all[:, mh, 128 * tt:128 * tt + 128], w2buf[:, mh, :],
                                    start=(mh == 0), stop=(mh == 31)), wait=w0)
                            ser.vector(lambda e, tt=tt, nsl=nsl: e.tensor_tensor(
                                z2_sb[:], z1blk[:, tt, nsl], psZ[:], ALU.add))
                            ser.dma(lambda e, r=r, nsl=nsl: e.dma_start(
                                out=zout_scr[r, nsl], in_=z2_sb[:]))
                ser.emit_block(nc, sems, acc)

        phase_subpre(l1[0], s_xn1, "A")
        phase_subatt("A")
        phase_subpost(l1[0], s_xn1, s_z2, "A")
        phase_subpre(l1[1], s_z2, "B")
        phase_subatt("B")
        phase_subpost(l1[1], s_z2, s_z4, "B")

        with ExitStack() as ctx:
            sb = lambda name, shape, dt=BF16: ctx.enter_context(nc.sbuf_tensor(name, shape, dt))
            ps = lambda name, shape, dt=F32: ctx.enter_context(nc.psum_tensor(name, shape, dt))
            ser = Ser()
            SW = HI // 2
            ident = sb("f7_id", [128, 128])
            wgqT = sb("f7_wg", [128, 8, HI])
            wuqT = sb("f7_wu", [128, 8, HI])
            wdqT = sb("f7_wd", [128, 22, D])
            stage = sb("f7_stage", [128, SW], F32)
            wqc = sb("f7_wqc", [128, HI])
            scr = {"aw": sb("f7_aw", [128, SW], F32), "ind": sb("f7_ind", [128, SW]),
                   "acc": sb("f7_acc", [128, SW]), "sgn": sb("f7_sgn", [128, SW]),
                   "sblk": sb("f7_sblk", [128, SW // 16], F32), "six": sb("f7_six", [128, 1], F32),
                   "awb": sb("f7_awb", [128, SW]), "sb16": sb("f7_sb16", [128, SW // 16])}
            xq = sb("f7_xq", [128, D])
            xqf = sb("f7_xqf", [128, 8, 128])
            h_t = sb("f7_h", [128, HI], F32)
            hq = sb("f7_hq", [128, HI])
            hqf = sb("f7_hqf", [128, 22, 128])
            sgh = sb("f7_sgh", [128, 512])
            x1_sb = sb("f7_x1", [128, D], F32)
            z4_sb = sb("f7_z4", [128, D], F32)
            xn1_sb = sb("f7_xn1", [128, D], F32)
            out_sb = sb("f7_out", [128, D], F32)
            psG = ps("f7_psG", [128, 512])
            psU = ps("f7_psU", [128, 512])
            psD = ps("f7_psD", [128, 512])
            psT = [ps("f7_psT0", [128, 128], BF16), ps("f7_psT1", [128, 128], BF16)]

            ser.dmag(lambda e: e.dma_start(out=ident[:], in_=id_d[:]))
            for (w_d2, wTq, ncw, W) in ((wg_d, wgqT, 22, D), (wu_d, wuqT, 22, D), (wd_d, wdqT, 8, HI)):
                for co in range(ncw):
                    for a in range(0, W, SW):
                        w_ = min(SW, W - a)
                        ser.dma(lambda e, w_d2=w_d2, co=co, a=a, w_=w_: e.dma_start(
                            out=stage[:, 0:w_], in_=w_d2[128 * co:128 * co + 128, a:a + w_]))
                        emit_quant(ser, stage[:, 0:w_], wqc[:, a:a + w_], w_, scr)
                    w0t = ser.mark()
                    emit_tr(ser, [(wqc[:, 128 * ci:128 * ci + 128], wTq[:, ci, 128 * co:128 * co + 128])
                                  for ci in range(W // 128)], psT, ident, w0=w0t)
            gate_blocks = [(0, 512), (512, 512), (1024, 512), (1536, 512), (2048, 512), (2560, 256)]
            for i in range(NTB):
                r = slice(128 * i, 128 * i + 128)
                ser.dma(lambda e, r=r: e.dma_start(out=xn1_sb[:], in_=s_xn1[r, :]))
                ser.dma(lambda e, r=r: e.dma_start(out=x1_sb[:], in_=s_x1[r, :]))
                ser.dma(lambda e, r=r: e.dma_start(out=z4_sb[:], in_=s_z4[r, :]))
                emit_quant(ser, xn1_sb[:, :], xq[:, :], D, scr)
                w0t = ser.mark()
                emit_tr(ser, [(xq[:, 128 * c:128 * c + 128], xqf[:, c, :]) for c in range(8)],
                        psT, ident, w0=w0t)
                for o0, ow in gate_blocks:
                    osl = slice(o0, o0 + ow)
                    w0 = ser.mark()
                    for c in range(8):
                        ser.tensor(lambda e, c=c, osl=osl, ow=ow: e.matmul(
                            psG[:, 0:ow], xqf[:, c, :], wgqT[:, c, osl], start=(c == 0), stop=(c == 7)), wait=w0)
                    for c in range(8):
                        ser.tensor(lambda e, c=c, osl=osl, ow=ow: e.matmul(
                            psU[:, 0:ow], xqf[:, c, :], wuqT[:, c, osl], start=(c == 0), stop=(c == 7)), wait=w0)
                    ser.scalar(lambda e, ow=ow: e.activation(sgh[:, 0:ow], psG[:, 0:ow], AF.Silu))
                    ser.vector(lambda e, osl=osl, ow=ow: e.tensor_tensor(
                        h_t[:, osl], sgh[:, 0:ow], psU[:, 0:ow], ALU.mult))
                emit_quant(ser, h_t[:, :], hq[:, :], HI, scr)
                w0t = ser.mark()
                emit_tr(ser, [(hq[:, 128 * c:128 * c + 128], hqf[:, c, :]) for c in range(22)],
                        psT, ident, w0=w0t)
                for nb in range(2):
                    nsl = slice(512 * nb, 512 * nb + 512)
                    w0 = ser.mark()
                    for c in range(22):
                        ser.tensor(lambda e, c=c, nsl=nsl: e.matmul(
                            psD[:], hqf[:, c, :], wdqT[:, c, nsl], start=(c == 0), stop=(c == 21)), wait=w0)
                    ser.vector(lambda e, nsl=nsl: e.tensor_tensor(
                        out_sb[:, nsl], x1_sb[:, nsl], z4_sb[:, nsl], ALU.add))
                    ser.vector(lambda e, nsl=nsl: e.tensor_tensor(
                        out_sb[:, nsl], out_sb[:, nsl], psD[:], ALU.add))
                ser.dma(lambda e, r=r: e.dma_start(out=out_o[r, :], in_=out_sb[:]))
            ser.emit_block(nc, sems, acc, final=True)
    return nc


def fused_in_maps(inputs):
    x = np.asarray(inputs["L0_attn_x"], np.float32).reshape(ROWS, D)
    cc0, ss0 = _rope0_tables()
    tbl1 = _rope1_table()
    masks = np.ascontiguousarray(_diag_masks().transpose(1, 0, 2))
    g = lambda n: np.asarray(inputs[n], np.float32)
    in_maps = []
    for c in range(NCORE):
        b = c // 2
        m = {
            "x_full": x,
            "x_batch": np.ascontiguousarray(x[b * T:(b + 1) * T]),
            "Wq": g("L0_attn_Wq"), "bq": g("L0_attn_bq").reshape(1, D),
            "Wk": g("L0_attn_Wk"), "bk": g("L0_attn_bk").reshape(1, 256),
            "Wv": g("L0_attn_Wv"), "bv": g("L0_attn_bv").reshape(1, 256),
            "Wo": g("L0_attn_Wo"), "bo": g("L0_attn_bo").reshape(1, D),
            "in_c": np.ascontiguousarray(g("L0_ffn_in_c").reshape(8, 128).T),
            "wof": g("L0_ffn_wo"), "uof": g("L0_ffn_uo"), "bof": g("L0_ffn_bo").reshape(1, D),
            "w_gate": g("L1_ffn_w_gate"), "w_up": g("L1_ffn_w_up"), "w_down": g("L1_ffn_w_down"),
            "cc0": np.ascontiguousarray(cc0[:, 0:T]), "ss0": np.ascontiguousarray(ss0[:, 0:T]),
            "tbl1": np.ascontiguousarray(tbl1[:, 0:T]),
            "masks": masks, "ident": IDENT,
        }
        for nm in ("wf", "uf", "wi", "ui", "wc", "uc"):
            m[nm] = g("L0_ffn_" + nm)
        for nm in ("bf", "bi", "bc"):
            m[nm] = g("L0_ffn_" + nm).reshape(1, D)
        for li in range(2):
            m[f"g1_{li}"] = g("L1_attn_ln1_g")[li].reshape(1, D)
            m[f"b1r_{li}"] = g("L1_attn_ln1_b")[li].reshape(1, D)
            m[f"wq{li}"] = g("L1_attn_wq")[li]
            m[f"wk{li}"] = g("L1_attn_wk")[li]
            m[f"wv{li}"] = g("L1_attn_wv")[li]
            m[f"bqr{li}"] = g("L1_attn_bq")[li].reshape(1, D)
            m[f"bkr{li}"] = g("L1_attn_bk")[li].reshape(1, D)
            m[f"bvr{li}"] = g("L1_attn_bv")[li].reshape(1, D)
            m[f"wo{li}"] = g("L1_attn_wo")[li]
            m[f"g2_{li}"] = g("L1_attn_ln2_g")[li].reshape(1, D)
            m[f"b2_{li}"] = g("L1_attn_ln2_b")[li].reshape(1, D)
            m[f"w1_{li}"] = g("L1_attn_w1")[li]
            m[f"b1f_{li}"] = g("L1_attn_b1")[li].reshape(1, 4 * D)
            m[f"w2_{li}"] = g("L1_attn_w2")[li]
        in_maps.append(m)
    return in_maps


def kernel_fused(**inputs):
    nc = build_fused()
    in_maps = fused_in_maps(inputs)
    res = _run(nc, in_maps)
    out = np.empty((ROWS, D), np.float32)
    for b in range(B):
        out[b * T:(b + 1) * T] = np.asarray(res[2 * b]["out_full"])
    return out.reshape(B, T, D)



def kernel(**inputs):
    return kernel_fused(**inputs)
